# version 02
# baseline (speedup 1.0000x reference)
import contextlib
import numpy as np
import ml_dtypes
import concourse.bass as bass
import concourse.mybir as mybir
from concourse.bass_utils import run_bass_kernel_spmd

F32 = mybir.dt.float32
BF16 = mybir.dt.bfloat16
I32 = mybir.dt.int32
AF = mybir.ActivationFunctionType
ALU = mybir.AluOpType
AX = mybir.AxisListType

NCORES = 8
B = 2
L = 4096
D = 2048
DIN = 4096
DST = 16
DTR = 128
HEADS = 16
HD = 128
HBIT = 5632
HGHOST = 5504
TC = 512
TOK = B * TC
EPS_LN = 1e-5
EPS_BIT = 1e-8

BF = ml_dtypes.bfloat16


def build_L1():
    nc = bass.Bass("TRN2", target_bir_lowering=False, debug=False)
    x_in = nc.dram_tensor("x_in", [TOK, D], F32, kind="ExternalInput").ap()
    nsc_in = nc.dram_tensor("nsc_in", [1, D], F32, kind="ExternalInput").ap()
    winT_in = nc.dram_tensor("winT_in", [D, 2 * DIN], F32, kind="ExternalInput").ap()
    cst_in = nc.dram_tensor("cst_in", [1, 2], F32, kind="ExternalInput").ap()

    ln0_out = nc.dram_tensor("ln0_out", [TOK, D], F32, kind="ExternalOutput").ap()
    xz_out = nc.dram_tensor("xz_out", [2 * DIN, TOK], F32, kind="ExternalOutput").ap()

    xn_scr = nc.dram_tensor("xn_scr", [TOK, D], BF16).ap()

    NT = TOK // 128
    KT = D // 128
    MT = (2 * DIN) // 128

    with (
        nc.sbuf_tensor("xt", [128, NT, D], F32) as xt,
        nc.sbuf_tensor("nsc", [128, D], F32) as nsc,
        nc.sbuf_tensor("cst", [128, 2], F32) as cst,
        nc.sbuf_tensor("sq", [128, D], F32) as sq,
        nc.sbuf_tensor("stat", [128, NT, 6], F32) as stat,
        nc.sbuf_tensor("lnt", [128, NT, D], F32) as lnt,
        nc.sbuf_tensor("xnt", [128, D], BF16) as xnt,
        nc.sbuf_tensor("xnf", [128, KT, TOK], BF16) as xnf,
        nc.sbuf_tensor("wb", [128, 2, KT, 128], BF16) as wb,
        nc.sbuf_tensor("ev", [128, 2, 512], F32) as ev,
        nc.psum_tensor("ps", [128, 2, 512], F32) as ps,
        nc.semaphore("s_ld") as s_ld,
        nc.semaphore("s_v") as s_v,
        nc.semaphore("s_s") as s_s,
        nc.semaphore("s_t") as s_t,
        nc.semaphore("s_wl") as s_wl,
        nc.semaphore("s_xf") as s_xf,
        nc.semaphore("s_st") as s_st,
        nc.Block() as block,
    ):
        @block.sync
        def _(sync):
            sync.dma_start(out=xt[:], in_=x_in.rearrange("(n p) d -> p n d", p=128)).then_inc(s_ld, 16)
            sync.dma_start(out=nsc[:], in_=nsc_in.partition_broadcast(128)).then_inc(s_ld, 16)
            sync.dma_start(out=cst[:], in_=cst_in.partition_broadcast(128)).then_inc(s_ld, 16)
            sync.wait_ge(s_st, 16 * NT)
            for k in range(KT):
                sync.dma_start(out=xnf[:, k, :], in_=xn_scr[:, k * 128:(k + 1) * 128],
                               transpose=True).then_inc(s_xf, 16)

        @block.vector
        def _(vector):
            vector.wait_ge(s_ld, 48)
            for n in range(NT):
                vector.reduce_sum(out=stat[:, n, 0:1], in_=xt[:, n, :], axis=AX.X)
                vector.drain()
                vector.tensor_scalar(out=stat[:, n, 1:2], in0=stat[:, n, 0:1],
                                     scalar1=-1.0 / D, scalar2=None, op0=ALU.mult, op1=ALU.bypass)
                vector.drain()
                vector.tensor_scalar(out=lnt[:, n, :], in0=xt[:, n, :],
                                     scalar1=stat[:, n, 1:2], scalar2=None,
                                     op0=ALU.add, op1=ALU.bypass).then_inc(s_v, 1)
                vector.wait_ge(s_s, 2 * n + 1)
                vector.reciprocal(out=stat[:, n, 3:4], in_=stat[:, n, 2:3])
                vector.drain()
                vector.tensor_scalar(out=lnt[:, n, :], in0=lnt[:, n, :],
                                     scalar1=stat[:, n, 3:4], scalar2=None,
                                     op0=ALU.mult, op1=ALU.bypass).then_inc(s_v, 1)
                vector.wait_ge(s_s, 2 * n + 2)
                vector.reciprocal(out=stat[:, n, 5:6], in_=stat[:, n, 4:5])
                vector.drain()
                vector.tensor_scalar(out=sq[:], in0=lnt[:, n, :],
                                     scalar1=stat[:, n, 5:6], scalar2=None,
                                     op0=ALU.mult, op1=ALU.bypass)
                vector.drain()
                vector.tensor_tensor(out=xnt[:], in0=sq[:], in1=nsc[:],
                                     op=ALU.mult).then_inc(s_v, 1)
                if n < NT - 1:
                    vector.wait_ge(s_st, 16 * (n + 1))
            for m in range(MT):
                for j in range(2):
                    i = m * 2 + j
                    vector.wait_ge(s_t, i + 1)
                    if i >= 2:
                        vector.wait_ge(s_st, 16 * (NT + i))
                    vector.tensor_copy(out=ev[:, i % 2, :], in_=ps[:, j, :]).then_inc(s_v, 1)

        @block.scalar
        def _(scalar):
            scalar.wait_ge(s_ld, 48)
            for n in range(NT):
                scalar.wait_ge(s_v, 3 * n + 1)
                scalar.activation(out=sq[:], in_=lnt[:, n, :], func=AF.Square,
                                  accum_out=stat[:, n, 0:1])
                scalar.drain()
                scalar.activation(out=stat[:, n, 2:3], in_=stat[:, n, 0:1], func=AF.Sqrt,
                                  scale=1.0 / D, bias=cst[:, 0:1]).then_inc(s_s, 1)
                scalar.wait_ge(s_v, 3 * n + 2)
                scalar.activation(out=sq[:], in_=lnt[:, n, :], func=AF.Square,
                                  accum_out=stat[:, n, 0:1])
                scalar.drain()
                scalar.activation(out=stat[:, n, 4:5], in_=stat[:, n, 0:1], func=AF.Sqrt,
                                  scale=1.0 / D, bias=cst[:, 0:1]).then_inc(s_s, 1)

        @block.gpsimd
        def _(gpsimd):
            for n in range(NT):
                gpsimd.wait_ge(s_v, 3 * n + 3)
                gpsimd.dma_start(out=xn_scr[n * 128:(n + 1) * 128, :], in_=xnt[:]).then_inc(s_st, 16)
            for m in range(MT):
                pb = m % 2
                if m >= 2:
                    gpsimd.wait_ge(s_t, (m - 1) * 2)
                for k in range(KT):
                    gpsimd.dma_start(out=wb[:, pb, k, :],
                                     in_=winT_in[k * 128:(k + 1) * 128, m * 128:(m + 1) * 128]
                                     ).then_inc(s_wl, 16)

        @block.tensor
        def _(tensor):
            tensor.wait_ge(s_xf, 16 * KT)
            for m in range(MT):
                pb = m % 2
                tensor.wait_ge(s_wl, 16 * KT * (m + 1))
                for j in range(2):
                    if m >= 1:
                        tensor.wait_ge(s_v, 3 * NT + (m - 1) * 2 + j + 1)
                    for k in range(KT):
                        mm = tensor.matmul(ps[:, j, :], wb[:, pb, k, :],
                                           xnf[:, k, j * 512:(j + 1) * 512],
                                           start=(k == 0), stop=(k == KT - 1))
                    mm.then_inc(s_t, 1)

        @block.sync
        def _(sync):
            sync.wait_ge(s_v, 3 * NT)
            sync.dma_start(out=ln0_out.rearrange("(n p) d -> p n d", p=128), in_=lnt[:]).then_inc(s_st, 16)
            for m in range(MT):
                for j in range(2):
                    sync.wait_ge(s_v, 3 * NT + m * 2 + j + 1)
                    sync.dma_start(out=xz_out[m * 128:(m + 1) * 128, j * 512:(j + 1) * 512],
                                   in_=ev[:, (m * 2 + j) % 2, :]).then_inc(s_st, 16)
            sync.wait_ge(s_st, 16 * (NT + 1 + MT * 2))
    return nc


def run_L1(inputs):
    x = inputs["L0_attn_hidden_states"]
    nsc = np.ascontiguousarray(inputs["L0_attn_norm_scale"].reshape(1, D))
    winT = np.ascontiguousarray(inputs["L0_attn_in_proj_w"].T)
    nc = build_L1()
    in_maps = []
    for c in range(NCORES):
        xc = np.ascontiguousarray(
            x[:, c * TC:(c + 1) * TC, :].reshape(TOK, D))
        in_maps.append(dict(x_in=xc, nsc_in=nsc, winT_in=winT,
                            cst_in=np.array([[EPS_LN, 1.0]], np.float32)))
    res = run_bass_kernel_spmd(nc, in_maps, core_ids=list(range(NCORES)))
    ln0 = np.empty((B, L, D), np.float32)
    xz = np.empty((2 * DIN, B, L), np.float32)
    for c, r in enumerate(res.results):
        ln0[:, c * TC:(c + 1) * TC, :] = r["ln0_out"].reshape(B, TC, D)
        xz[:, :, c * TC:(c + 1) * TC] = r["xz_out"].reshape(2 * DIN, B, TC)
    return ln0, xz


LANES = 1024
NLG = LANES // 128


def build_L2():
    nc = bass.Bass("TRN2", target_bir_lowering=False, debug=False)
    xin_in = nc.dram_tensor("xin_in", [LANES, L], F32, kind="ExternalInput").ap()
    cw_in = nc.dram_tensor("cw_in", [128, NLG, 4], F32, kind="ExternalInput").ap()
    cb_in = nc.dram_tensor("cb_in", [128, NLG], F32, kind="ExternalInput").ap()
    xpwT_in = nc.dram_tensor("xpwT_in", [512, 160], F32, kind="ExternalInput").ap()
    w1_in = nc.dram_tensor("w1_in", [768, D], F32, kind="ExternalInput").ap()
    w2_in = nc.dram_tensor("w2_in", [768, D], F32, kind="ExternalInput").ap()
    w3s_in = nc.dram_tensor("w3s_in", [256, HBIT], F32, kind="ExternalInput").ap()

    xc_out = nc.dram_tensor("xc_out", [LANES, L], F32, kind="ExternalOutput").ap()
    xdbl_out = nc.dram_tensor("xdbl_out", [B, 160, L], F32, kind="ExternalOutput").ap()
    gp_out = nc.dram_tensor("gp_out", [1, 3], F32, kind="ExternalOutput").ap()

    NB = L // 512

    with contextlib.ExitStack() as ctx:
        sb = lambda *a: ctx.enter_context(nc.sbuf_tensor(*a))
        pt = lambda *a: ctx.enter_context(nc.psum_tensor(*a))
        sem = lambda n: ctx.enter_context(nc.semaphore(n))
        xpad = sb("xpad", [128, 2, 3 + L], F32)
        acc = sb("acc", [128, 2, L], F32)
        xcf = sb("xcf", [128, 2, L], F32)
        xcb = sb("xcb", [128, NLG, L], BF16)
        cw = sb("cw", [128, NLG, 4], F32)
        cb = sb("cb", [128, NLG], F32)
        xpw = sb("xpw", [128, 4, 160], BF16)
        evA = sb("evA", [128, 2, 512], F32)
        evB = sb("evB", [32, 2, 512], F32)
        gw = sb("gw", [128, 2, HBIT], BF16)
        gtmp = sb("gtmp", [128, 16], F32)
        gcol = sb("gcol", [128, 3], F32)
        ones = sb("ones", [128, 1], F32)
        gp = sb("gp", [1, 3], F32)
        psA = pt("psA", [128, 2, 512], F32)
        psB = pt("psB", [32, 2, 512], F32)
        psg = pt("psg", [1, 3], F32)
        s_ld = sem("s_ld"); s_c = sem("s_c"); s_v = sem("s_v")
        s_sl = sem("s_sl"); s_xs = sem("s_xs"); s_mm = sem("s_mm")
        s_gl = sem("s_gl"); s_gv = sem("s_gv"); s_st = sem("s_st")
        block = ctx.enter_context(nc.Block())
        @block.sync
        def _(sync):
            sync.dma_start(out=cw[:], in_=cw_in[:]).then_inc(s_c, 16)
            sync.dma_start(out=cb[:], in_=cb_in[:]).then_inc(s_c, 16)
            for g in range(NLG):
                pb = g % 2
                if g >= 2:
                    sync.wait_ge(s_v, 2 * (g - 2) + 1)
                sync.dma_start(out=xpad[:, pb, 3:3 + L],
                               in_=xin_in[g * 128:(g + 1) * 128, :]).then_inc(s_ld, 16)

        @block.gpsimd
        def _(gpsimd):
            for k in range(4):
                gpsimd.dma_start(out=xpw[:, k, :],
                                 in_=xpwT_in[k * 128:(k + 1) * 128, :]).then_inc(s_c, 16)
            for g in range(NLG):
                gpsimd.wait_ge(s_sl, g + 1)
                gpsimd.dma_start(out=xc_out[g * 128:(g + 1) * 128, :],
                                 in_=xcf[:, g % 2, :]).then_inc(s_xs, 16)
            gsrcs = ([w1_in[i * 128:(i + 1) * 128, :] for i in range(6)]
                     + [w2_in[i * 128:(i + 1) * 128, :] for i in range(6)]
                     + [w3s_in[i * 128:(i + 1) * 128, :] for i in range(2)])
            for i, srcap in enumerate(gsrcs):
                if i >= 2:
                    gpsimd.wait_ge(s_gv, i - 1)
                wid = srcap.shape[1]
                gpsimd.dma_start(out=gw[:, i % 2, 0:wid], in_=srcap).then_inc(s_gl, 16)

        @block.vector
        def _(vector):
            vector.wait_ge(s_c, 32)
            vector.memset(ones[:], 1.0)
            for g in range(NLG):
                pb = g % 2
                vector.memset(xpad[:, pb, 0:3], 0.0)
                vector.wait_ge(s_ld, 16 * (g + 1))
                vector.tensor_scalar(out=acc[:, 0, :], in0=xpad[:, pb, 0:L],
                                     scalar1=cw[:, g, 0:1], scalar2=None,
                                     op0=ALU.mult, op1=ALU.bypass)
                vector.drain()
                vector.scalar_tensor_tensor(out=acc[:, 1, :], in0=xpad[:, pb, 1:1 + L],
                                            scalar=cw[:, g, 1:2], in1=acc[:, 0, :],
                                            op0=ALU.mult, op1=ALU.add)
                vector.drain()
                vector.scalar_tensor_tensor(out=acc[:, 0, :], in0=xpad[:, pb, 2:2 + L],
                                            scalar=cw[:, g, 2:3], in1=acc[:, 1, :],
                                            op0=ALU.mult, op1=ALU.add)
                vector.drain()
                vector.scalar_tensor_tensor(out=acc[:, 1, :], in0=xpad[:, pb, 3:3 + L],
                                            scalar=cw[:, g, 3:4], in1=acc[:, 0, :],
                                            op0=ALU.mult, op1=ALU.add).then_inc(s_v, 1)
                vector.wait_ge(s_sl, g + 1)
                vector.tensor_copy(out=xcb[:, g, :], in_=xcf[:, pb, :]).then_inc(s_v, 1)
            nev = 0
            for i in range(B * NB):
                vector.wait_ge(s_mm, 2 * i + 2)
                pb = i % 2
                if i >= 2:
                    vector.wait_ge(s_st, 16 * (2 * (i - 2) + 2))
                vector.tensor_copy(out=evA[:, pb, :], in_=psA[:, pb, :]).then_inc(s_v, 1)
                vector.tensor_copy(out=evB[:, pb, :], in_=psB[:, pb, :]).then_inc(s_v, 1)
                nev += 2
            for i in range(14):
                vector.wait_ge(s_gl, 16 * (i + 1))
                wid = HBIT if i >= 12 else D
                col = i if i < 12 else 12 + (i - 12)
                vector.tensor_reduce(out=gtmp[:, col:col + 1], in_=gw[:, i % 2, 0:wid],
                                     axis=AX.X, op=ALU.add,
                                     apply_absolute_value=True).then_inc(s_gv, 1)
            vector.drain()
            vector.tensor_reduce(out=gcol[:, 0:1], in_=gtmp[:, 0:6], axis=AX.X, op=ALU.add)
            vector.tensor_reduce(out=gcol[:, 1:2], in_=gtmp[:, 6:12], axis=AX.X, op=ALU.add)
            vector.tensor_reduce(out=gcol[:, 2:3], in_=gtmp[:, 12:14], axis=AX.X,
                                 op=ALU.add).then_inc(s_gv, 1)
            vector.wait_ge(s_mm, 2 * B * NB + 1)
            vector.tensor_copy(out=gp[:], in_=psg[:]).then_inc(s_gv, 1)

        @block.scalar
        def _(scalar):
            scalar.wait_ge(s_c, 32)
            for g in range(NLG):
                pb = g % 2
                scalar.wait_ge(s_v, 2 * g + 1)
                if g >= 2:
                    scalar.wait_ge(s_xs, 16 * (g - 1))
                scalar.activation(out=xcf[:, pb, :], in_=acc[:, 1, :], func=AF.Silu,
                                  bias=cb[:, g:g + 1]).then_inc(s_sl, 1)

        @block.tensor
        def _(tensor):
            tensor.wait_ge(s_c, 16 * 6)
            tensor.wait_ge(s_v, 2 * NLG)
            for i in range(B * NB):
                b, n = divmod(i, NB)
                pb = i % 2
                if i >= 2:
                    tensor.wait_ge(s_v, 2 * NLG + 2 * (i - 2) + 2)
                for k in range(4):
                    mmA = tensor.matmul(psA[:, pb, :], xpw[:, k, 0:128],
                                        xcb[:, b * 4 + k, n * 512:(n + 1) * 512],
                                        start=(k == 0), stop=(k == 3))
                mmA.then_inc(s_mm, 1)
                for k in range(4):
                    mmB = tensor.matmul(psB[:, pb, :], xpw[:, k, 128:160],
                                        xcb[:, b * 4 + k, n * 512:(n + 1) * 512],
                                        start=(k == 0), stop=(k == 3))
                mmB.then_inc(s_mm, 1)
            tensor.wait_ge(s_gv, 15)
            for j in range(3):
                mg = tensor.matmul(psg[0:1, j:j + 1], ones[:], gcol[:, j:j + 1],
                                   start=True, stop=True)
            mg.then_inc(s_mm, 1)

        @block.sync
        def _(sync):
            for i in range(B * NB):
                b, n = divmod(i, NB)
                pb = i % 2
                sync.wait_ge(s_v, 2 * NLG + 2 * i + 1)
                sync.dma_start(out=xdbl_out[b, 0:128, n * 512:(n + 1) * 512],
                               in_=evA[:, pb, :]).then_inc(s_st, 16)
                sync.wait_ge(s_v, 2 * NLG + 2 * i + 2)
                sync.dma_start(out=xdbl_out[b, 128:160, n * 512:(n + 1) * 512],
                               in_=evB[:, pb, :]).then_inc(s_st, 16)
            sync.wait_ge(s_gv, 16)
            sync.dma_start(out=gp_out[:], in_=gp[:]).then_inc(s_st, 16)
            sync.wait_ge(s_st, 16 * (2 * B * NB + 1))
            sync.wait_ge(s_xs, 16 * NLG)

    return nc


def run_L2(inputs, xz):
    cwf = inputs["L0_attn_conv_w"][:, 0, :]
    nc = build_L2()
    in_maps = []
    xpwT = np.ascontiguousarray(inputs["L0_attn_x_proj_w"].T)
    w1 = inputs["L0_ffn_w1"]
    w2 = inputs["L0_ffn_w2"]
    w3 = inputs["L0_ffn_w3"]
    pad = np.zeros((64, D), np.float32)
    for c in range(NCORES):
        ch = slice(c * 512, (c + 1) * 512)
        xin = np.ascontiguousarray(xz[ch, :, :].transpose(1, 0, 2).reshape(LANES, L))
        cw_c = np.ascontiguousarray(
            np.concatenate([cwf[ch]] * B, 0).reshape(NLG, 128, 4).transpose(1, 0, 2))
        cb_c = np.ascontiguousarray(
            np.concatenate([inputs["L0_attn_conv_b"][ch]] * B).reshape(NLG, 128).T)
        w1_c = np.concatenate([w1[c * 704:(c + 1) * 704], pad], 0)
        w2_c = np.concatenate([w2[c * 704:(c + 1) * 704], pad], 0)
        w3_c = np.ascontiguousarray(w3[c * 256:(c + 1) * 256])
        in_maps.append(dict(xin_in=xin, cw_in=cw_c, cb_in=cb_c,
                            xpwT_in=np.ascontiguousarray(xpwT[ch]),
                            w1_in=np.ascontiguousarray(w1_c),
                            w2_in=np.ascontiguousarray(w2_c), w3s_in=w3_c))
    res = run_bass_kernel_spmd(nc, in_maps, core_ids=list(range(NCORES)))
    xc = np.empty((B, DIN, L), np.float32)
    xdbl_parts = np.empty((NCORES, B, 160, L), np.float32)
    gparts = np.empty((NCORES, 1, 3), np.float32)
    for c, r in enumerate(res.results):
        xc[:, c * 512:(c + 1) * 512, :] = r["xc_out"].reshape(B, 512, L)
        xdbl_parts[c] = r["xdbl_out"]
        gparts[c] = r["gp_out"]
    return xc, xdbl_parts, gparts


TCH = 128
NTC = L // TCH


def build_L3():
    nc = bass.Bass("TRN2", target_bir_lowering=False, debug=False)
    xc_in = nc.dram_tensor("xc_in", [LANES, L], F32, kind="ExternalInput").ap()
    z_in = nc.dram_tensor("z_in", [LANES, L], F32, kind="ExternalInput").ap()
    xdblp_in = nc.dram_tensor("xdblp_in", [NCORES, B, 160, L], F32, kind="ExternalInput").ap()
    dtwT_in = nc.dram_tensor("dtwT_in", [128, 512], F32, kind="ExternalInput").ap()
    dtb_in = nc.dram_tensor("dtb_in", [128, 4], F32, kind="ExternalInput").ap()
    A_in = nc.dram_tensor("A_in", [128, 4, DST], F32, kind="ExternalInput").ap()
    D_in = nc.dram_tensor("D_in", [128, 4], F32, kind="ExternalInput").ap()
    cst_in = nc.dram_tensor("cst_in", [1, 2], F32, kind="ExternalInput").ap()

    y_out = nc.dram_tensor("y_out", [LANES, L], F32, kind="ExternalOutput").ap()
    bc_scr = nc.dram_tensor("bc_scr", [32, B, L], BF16).ap()

    NG = 4
    NB = L // 512
    HL = L // 2

    with contextlib.ExitStack() as ctx:
        sb = lambda *a: ctx.enter_context(nc.sbuf_tensor(*a))
        pt = lambda *a: ctx.enter_context(nc.psum_tensor(*a))
        sem = lambda n: ctx.enter_context(nc.semaphore(n))
        pload = sb("pload", [128, 2, HL], F32)
        bcload = sb("bcload", [32, 2, HL], F32)
        dtl = sb("dtl", [128, L], F32)
        bca = sb("bca", [32, L], F32)
        dtlb = sb("dtlb", [128, L], BF16)
        dtw = sb("dtw", [128, 512], BF16)
        dtb = sb("dtb", [128, 4], F32)
        negA = sb("negA", [128, NG, DST], F32)
        Dt = sb("Dt", [128, NG], F32)
        cst = sb("cst", [128, 2], F32)
        delta = sb("delta", [128, NG, L], BF16)
        xcb = sb("xcb", [128, NG, L], BF16)
        etile = sb("etile", [128, 1, 512], F32)
        Bbc = sb("Bbc", [128, DST, TCH], F32)
        Cbc = sb("Cbc", [128, DST, TCH], F32)
        dA = sb("dA", [128, DST, TCH], F32)
        bArg = sb("bArg", [128, DST, TCH], BF16)
        du = sb("du", [128, TCH], BF16)
        hsb = sb("hsb", [128, NG, DST, TCH], F32)
        hc = sb("hc", [128, DST, TCH], BF16)
        yt = sb("yt", [128, TCH], F32)
        y2 = sb("y2", [128, TCH], F32)
        y3 = sb("y3", [128, 2, TCH], F32)
        zt = sb("zt", [128, 2, TCH], F32)
        sz = sb("sz", [128, TCH], F32)
        sbar = sb("sbar", [1, 2], F32)
        ps = pt("ps", [128, 2, 512], F32)
        s_c = sem("s_c")
        s_p = sem("s_p")
        s_q = sem("s_q")
        s_va = sem("s_va")
        s_vb = sem("s_vb")
        s_dl = sem("s_dl")
        s_bs = sem("s_bs")
        s_mm = sem("s_mm")
        s_sp = sem("s_sp")
        s_xc = sem("s_xc")
        s_bc = sem("s_bc")
        s_z = sem("s_z")
        s_da = sem("s_da")
        s_sl = sem("s_sl")
        s_vy = sem("s_vy")
        s_ys = sem("s_ys")
        s_neg = sem("s_neg")
        block = ctx.enter_context(nc.Block())

        NIT = NTC * NG

        @block.sync
        def _(sync):
            for b in range(B):
                for j in range(NCORES):
                    for h in range(2):
                        k = j * 2 + h
                        pb = k % 2
                        if k >= 2:
                            sync.wait_ge(s_va, b * 16 + k - 1)
                        sync.dma_start(out=pload[:, pb, :],
                                       in_=xdblp_in[j, b, 0:128, h * HL:(h + 1) * HL]
                                       ).then_inc(s_p, 16)
                        if k >= 2:
                            sync.wait_ge(s_vb, b * 17 + k - 1)
                        sync.dma_start(out=bcload[:, pb, :],
                                       in_=xdblp_in[j, b, 128:160, h * HL:(h + 1) * HL]
                                       ).then_inc(s_q, 16)
                sync.wait_ge(s_bs, 16 * (b + 1))
                for tc in range(NTC):
                    jj = b * NTC + tc
                    for g in range(NG):
                        it = jj * NG + g
                        if it >= 2:
                            sync.wait_ge(s_sl, it - 1)
                        sync.dma_start(out=zt[:, it % 2, :],
                                       in_=z_in[(b * NG + g) * 128:(b * NG + g + 1) * 128,
                                                tc * TCH:(tc + 1) * TCH]).then_inc(s_z, 16)

        @block.gpsimd
        def _(gpsimd):
            gpsimd.dma_start(out=dtw[:], in_=dtwT_in[:]).then_inc(s_c, 16)
            gpsimd.dma_start(out=dtb[:], in_=dtb_in[:]).then_inc(s_c, 16)
            gpsimd.dma_start(out=negA[:], in_=A_in[:]).then_inc(s_c, 16)
            gpsimd.dma_start(out=Dt[:], in_=D_in[:]).then_inc(s_c, 16)
            gpsimd.dma_start(out=cst[:], in_=cst_in.partition_broadcast(128)).then_inc(s_c, 16)
            for b in range(B):
                gpsimd.wait_ge(s_vb, b * 17 + 17)
                gpsimd.dma_start(out=bc_scr[:, b, :], in_=bca[:]).then_inc(s_bs, 16)
                if b > 0:
                    gpsimd.wait_ge(s_vy, b * NIT)
                for g in range(NG):
                    gpsimd.dma_start(out=xcb[:, g, :],
                                     in_=xc_in[(b * NG + g) * 128:(b * NG + g + 1) * 128, :]
                                     ).then_inc(s_xc, 16)
                for it0 in range(NIT):
                    it = b * NIT + it0
                    tc, g = divmod(it0, NG)
                    if g == 0:
                        jj = b * NTC + tc
                        if jj > 0:
                            gpsimd.wait_ge(s_vy, jj * NG)
                        for s in range(DST):
                            gpsimd.dma_start(
                                out=Bbc[:, s, :],
                                in_=bc_scr[s:s + 1, b, tc * TCH:(tc + 1) * TCH].partition_broadcast(128)
                            ).then_inc(s_bc, 16)
                            gpsimd.dma_start(
                                out=Cbc[:, s, :],
                                in_=bc_scr[DST + s:DST + s + 1, b, tc * TCH:(tc + 1) * TCH].partition_broadcast(128)
                            ).then_inc(s_bc, 16)
                    gpsimd.wait_ge(s_vy, it + 1)
                    gpsimd.dma_start(out=y_out[(b * NG + g) * 128:(b * NG + g + 1) * 128,
                                               tc * TCH:(tc + 1) * TCH],
                                     in_=y3[:, it % 2, :]).then_inc(s_ys, 16)

        @block.vector
        def _(vector):
            vector.wait_ge(s_c, 80)
            vector.wait_ge(s_da, 1)
            vector.tensor_scalar(out=negA[:], in0=negA[:], scalar1=-1.0, scalar2=None,
                                 op0=ALU.mult, op1=ALU.bypass)
            vector.drain()
            vector.memset(yt[:, 0:1], 0.0).then_inc(s_neg, 1)
            for b in range(B):
                if b > 0:
                    vector.wait_ge(s_bs, 16 * b)
                for j in range(NCORES):
                    for h in range(2):
                        k = j * 2 + h
                        pb = k % 2
                        hsl = slice(h * HL, (h + 1) * HL)
                        vector.wait_ge(s_p, 16 * (b * 16 + k) + 16)
                        if j == 0:
                            vector.tensor_copy(out=dtl[:, hsl], in_=pload[:, pb, :]).then_inc(s_va, 1)
                        else:
                            vector.tensor_tensor(out=dtl[:, hsl], in0=dtl[:, hsl],
                                                 in1=pload[:, pb, :], op=ALU.add).then_inc(s_va, 1)
                        vector.wait_ge(s_q, 16 * (b * 16 + k) + 16)
                        if j == 0:
                            vector.tensor_copy(out=bca[:, hsl], in_=bcload[:, pb, :]).then_inc(s_vb, 1)
                        else:
                            vector.tensor_tensor(out=bca[:, hsl], in0=bca[:, hsl],
                                                 in1=bcload[:, pb, :], op=ALU.add).then_inc(s_vb, 1)
                        if j == NCORES - 1 and h == 1:
                            vector.drain()
                            vector.memset(yt[:, 0:1], 0.0).then_inc(s_vb, 1)
                vector.drain()
                vector.tensor_copy(out=dtlb[:], in_=dtl[:])
                vector.drain()
                vector.memset(yt[:, 0:1], 0.0).then_inc(s_dl, 1)
                for it0 in range(NIT):
                    it = b * NIT + it0
                    tc, g = divmod(it0, NG)
                    tsl = slice(tc * TCH, (tc + 1) * TCH)
                    if it0 == 0:
                        vector.wait_ge(s_xc, 16 * (b + 1) * NG)
                        vector.memset(hsb[:], 0.0)
                        vector.drain()
                    vector.wait_ge(s_sp, b * 32 + 32)
                    vector.tensor_tensor(out=du[:], in0=delta[:, g, tsl], in1=xcb[:, g, tsl],
                                         op=ALU.mult)
                    vector.drain()
                    vector.wait_ge(s_bc, 32 * 16 * (b * NTC + tc) + 32 * 16)
                    for s in range(DST):
                        vector.tensor_tensor(out=bArg[:, s, :], in0=du[:], in1=Bbc[:, s, :],
                                             op=ALU.mult)
                    vector.drain()
                    vector.wait_ge(s_da, it + 2)
                    for s in range(DST):
                        vector.tensor_tensor_scan(out=hsb[:, g, s, :], data0=dA[:, s, :],
                                                  data1=bArg[:, s, :],
                                                  initial=hsb[:, g, s, TCH - 1:TCH],
                                                  op0=ALU.mult, op1=ALU.add)
                    vector.drain()
                    vector.tensor_tensor(out=hc[:], in0=hsb[:, g, :, :], in1=Cbc[:],
                                         op=ALU.mult)
                    vector.drain()
                    vector.tensor_tensor(out=dA[:, 8:16, :], in0=hc[:, 0:8, :],
                                         in1=hc[:, 8:16, :], op=ALU.add)
                    vector.drain()
                    vector.tensor_tensor(out=dA[:, 0:4, :], in0=dA[:, 8:12, :],
                                         in1=dA[:, 12:16, :], op=ALU.add)
                    vector.drain()
                    vector.tensor_tensor(out=dA[:, 4:6, :], in0=dA[:, 0:2, :],
                                         in1=dA[:, 2:4, :], op=ALU.add)
                    vector.drain()
                    vector.tensor_tensor(out=yt[:], in0=dA[:, 4, :], in1=dA[:, 5, :],
                                         op=ALU.add)
                    vector.drain()
                    vector.scalar_tensor_tensor(out=y2[:], in0=xcb[:, g, tsl],
                                                scalar=Dt[:, g:g + 1], in1=yt[:],
                                                op0=ALU.mult, op1=ALU.add)
                    vector.drain()
                    vector.wait_ge(s_sl, it + 1)
                    if it >= 2:
                        vector.wait_ge(s_ys, 16 * (it - 1))
                    vector.tensor_tensor(out=y3[:, it % 2, :], in0=y2[:], in1=sz[:],
                                         op=ALU.mult)
                    vector.drain()
                    vector.memset(yt[:, 0:1], 0.0).then_inc(s_vy, 1)

        @block.scalar
        def _(scalar):
            scalar.wait_ge(s_c, 80)
            scalar.activation(out=negA[:], in_=negA[:], func=AF.Exp).then_inc(s_da, 1)
            for b in range(B):
                for g in range(NG):
                    for nb2 in range(NB):
                        idx = b * 32 + g * NB + nb2
                        pb = idx % 2
                        scalar.wait_ge(s_mm, idx + 1)
                        scalar.activation(out=etile[:, 0, :], in_=ps[:, pb, :], func=AF.Exp,
                                          bias=dtb[:, g:g + 1])
                        scalar.drain()
                        scalar.activation(out=delta[:, g, nb2 * 512:(nb2 + 1) * 512],
                                          in_=etile[:, 0, :], func=AF.Ln,
                                          bias=cst[:, 1:2])
                        scalar.drain()
                        scalar.copy(out=sbar[0:1, 0:1], in_=sbar[0:1, 1:2]).then_inc(s_sp, 1)
                for it0 in range(NIT):
                    it = b * NIT + it0
                    tc, g = divmod(it0, NG)
                    tsl = slice(tc * TCH, (tc + 1) * TCH)
                    if it >= 1:
                        scalar.wait_ge(s_vy, it)
                    if it == 0:
                        scalar.wait_ge(s_neg, 1)
                    for s in range(DST):
                        scalar.activation(out=dA[:, s, :], in_=delta[:, g, tsl],
                                          func=AF.Exp, scale=negA[:, g, s:s + 1])
                    scalar.drain()
                    scalar.copy(out=sbar[0:1, 0:1], in_=sbar[0:1, 1:2]).then_inc(s_da, 1)
                    scalar.wait_ge(s_z, 16 * (it + 1))
                    scalar.activation(out=sz[:], in_=zt[:, it % 2, :], func=AF.Silu)
                    scalar.drain()
                    scalar.copy(out=sbar[0:1, 0:1], in_=sbar[0:1, 1:2]).then_inc(s_sl, 1)

        @block.tensor
        def _(tensor):
            tensor.wait_ge(s_c, 80)
            for b in range(B):
                tensor.wait_ge(s_dl, b + 1)
                if b > 0:
                    tensor.wait_ge(s_vy, b * NIT)
                for g in range(NG):
                    for nb2 in range(NB):
                        idx = b * 32 + g * NB + nb2
                        pb = idx % 2
                        if idx >= 2:
                            tensor.wait_ge(s_sp, idx - 1)
                        tensor.matmul(ps[:, pb, :], dtw[:, g * 128:(g + 1) * 128],
                                      dtlb[:, nb2 * 512:(nb2 + 1) * 512],
                                      start=True, stop=True).then_inc(s_mm, 1)

        @block.sync
        def _(sync):
            sync.wait_ge(s_ys, 16 * B * NIT)

    return nc


def run_L3(inputs, xz, xc, xdbl_parts):
    nc = build_L3()
    in_maps = []
    dtw = inputs["L0_attn_dt_proj_w"]
    dtbv = inputs["L0_attn_dt_proj_b"]
    Alog = inputs["L0_attn_A_log"]
    Dv = inputs["L0_attn_D"]
    cstv = np.array([[EPS_LN, 1.0]], np.float32)
    for c in range(NCORES):
        ch = slice(c * 512, (c + 1) * 512)
        xc_c = np.ascontiguousarray(xc[:, ch, :].reshape(B * 512, L))
        z_c = np.ascontiguousarray(xz[DIN + c * 512:DIN + (c + 1) * 512, :, :]
                                   .transpose(1, 0, 2).reshape(LANES, L))
        dtwT_c = np.ascontiguousarray(dtw[ch].T)
        dtb_c = np.ascontiguousarray(dtbv[ch].reshape(4, 128).T)
        A_c = np.ascontiguousarray(Alog[ch].reshape(4, 128, DST).transpose(1, 0, 2))
        D_c = np.ascontiguousarray(Dv[ch].reshape(4, 128).T)
        in_maps.append(dict(xc_in=xc_c, z_in=z_c, xdblp_in=xdbl_parts,
                            dtwT_in=dtwT_c, dtb_in=dtb_c, A_in=A_c, D_in=D_c,
                            cst_in=cstv))
    res = run_bass_kernel_spmd(nc, in_maps, core_ids=list(range(NCORES)))
    y = np.empty((B, DIN, L), np.float32)
    for c, r in enumerate(res.results):
        y[:, c * 512:(c + 1) * 512, :] = r["y_out"].reshape(B, 512, L)
    return y


def build_L4():
    nc = bass.Bass("TRN2", target_bir_lowering=False, debug=False)
    yfm_in = nc.dram_tensor("yfm_in", [DIN, TOK], F32, kind="ExternalInput").ap()
    woT_in = nc.dram_tensor("woT_in", [DIN, D], F32, kind="ExternalInput").ap()
    x_in = nc.dram_tensor("x_in", [TOK, D], F32, kind="ExternalInput").ap()
    ln0_in = nc.dram_tensor("ln0_in", [TOK, D], F32, kind="ExternalInput").ap()
    gparts_in = nc.dram_tensor("gparts_in", [1, 3 * NCORES], F32, kind="ExternalInput").ap()
    w1s_in = nc.dram_tensor("w1s_in", [768, D], F32, kind="ExternalInput").ap()
    w2s_in = nc.dram_tensor("w2s_in", [768, D], F32, kind="ExternalInput").ap()
    w3s_in = nc.dram_tensor("w3s_in", [256, HBIT], F32, kind="ExternalInput").ap()
    cst_in = nc.dram_tensor("cst_in", [1, 2], F32, kind="ExternalInput").ap()

    x1_out = nc.dram_tensor("x1_out", [TOK, D], F32, kind="ExternalOutput").ap()
    qw1_out = nc.dram_tensor("qw1_out", [768, D], BF16, kind="ExternalOutput").ap()
    qw2_out = nc.dram_tensor("qw2_out", [768, D], BF16, kind="ExternalOutput").ap()
    qw3_out = nc.dram_tensor("qw3_out", [256, HBIT], BF16, kind="ExternalOutput").ap()

    out_scr = nc.dram_tensor("out_scr", [D, TOK], BF16).ap()
    ginv_scr = nc.dram_tensor("ginv_scr", [1, 3], F32).ap()

    KT = DIN // 128
    NMP = D // 256
    NT = TOK // 128

    with contextlib.ExitStack() as ctx:
        sb = lambda *a: ctx.enter_context(nc.sbuf_tensor(*a))
        pt = lambda *a: ctx.enter_context(nc.psum_tensor(*a))
        sem = lambda n: ctx.enter_context(nc.semaphore(n))
        yfm = sb("yfm", [128, KT, TOK], BF16)
        wb = sb("wb", [128, 1, KT, 256], BF16)
        ofm = sb("ofm", [128, 2, 4, 512], BF16)
        obf = sb("obf", [128, D], BF16)
        of32 = sb("of32", [128, D], F32)
        xt = sb("xt", [128, D], F32)
        lnt = sb("lnt", [128, D], F32)
        x1t = sb("x1t", [128, 2, D], F32)
        gq = sb("gq", [1, 3 * NCORES], F32)
        gsum = sb("gsum", [1, 8, 3], F32)
        ginv = sb("ginv", [1, 3], F32)
        ginvb = sb("ginvb", [128, 3], F32)
        HH = HBIT // 2
        gw = sb("gw", [128, 2, HH], F32)
        sct = sb("sct", [128, HH], F32)
        itt = sb("itt", [128, HH], I32)
        qt = sb("qt", [128, 2, HH], BF16)
        cst = sb("cst", [128, 2], F32)
        bar = sb("bar", [128, 1], F32)
        ps = pt("ps", [128, 2, 4, 512], F32)
        s_y = sem("s_y")
        s_w = sem("s_w")
        s_mm = sem("s_mm")
        s_ev = sem("s_ev")
        s_os = sem("s_os")
        s_tl = sem("s_tl")
        s_v2 = sem("s_v2")
        s_x1 = sem("s_x1")
        s_g = sem("s_g")
        s_gv = sem("s_gv")
        s_gs = sem("s_gs")
        s_gr = sem("s_gr")
        s_sc = sem("s_sc")
        s_og = sem("s_og")
        s_x1g = sem("s_x1g")
        s_g1 = sem("s_g1")
        s_gs1 = sem("s_gs1")
        s_c = sem("s_c")
        block = ctx.enter_context(nc.Block())

        QTILES = ([("w1", i, 0, D) for i in range(6)] + [("w2", i, 0, D) for i in range(6)]
                  + [("w3", i, c * 2816, 2816) for i in range(2) for c in range(2)])

        @block.gpsimd
        def _(gpsimd):
            for k in range(KT):
                gpsimd.dma_start(out=yfm[:, k, :],
                                 in_=yfm_in[k * 128:(k + 1) * 128, :]).then_inc(s_y, 16)
            for mp in range(NMP):
                if mp >= 1:
                    gpsimd.wait_ge(s_mm, 4 * mp)
                for k in range(KT):
                    gpsimd.dma_start(out=wb[:, 0, k, :],
                                     in_=woT_in[k * 128:(k + 1) * 128, mp * 256:(mp + 1) * 256]
                                     ).then_inc(s_w, 16)
            def _qstore(i):
                wn, ti, c0, wid = QTILES[i]
                dst = {"w1": qw1_out, "w2": qw2_out, "w3": qw3_out}[wn]
                gpsimd.wait_ge(s_gv, i + 1)
                gpsimd.dma_start(out=dst[ti * 128:(ti + 1) * 128, c0:c0 + wid],
                                 in_=qt[:, i % 2, 0:wid]).then_inc([s_gs, s_gs1][i % 2], 16)
            for i, (wn, ti, c0, wid) in enumerate(QTILES):
                if i >= 2:
                    gpsimd.wait_ge(s_sc, i - 1)
                srcw = {"w1": w1s_in, "w2": w2s_in, "w3": w3s_in}[wn]
                gpsimd.dma_start(out=gw[:, i % 2, 0:wid],
                                 in_=srcw[ti * 128:(ti + 1) * 128, c0:c0 + wid]
                                 ).then_inc([s_g, s_g1][i % 2], 16)
                if i >= 1:
                    _qstore(i - 1)
            _qstore(len(QTILES) - 1)

        @block.sync
        def _(sync):
            sync.dma_start(out=cst[:], in_=cst_in.partition_broadcast(128)).then_inc(s_c, 16)
            sync.dma_start(out=gq[:], in_=gparts_in[:]).then_inc(s_c, 16)
            for mp in range(NMP):
                pb = mp % 2
                for i in range(4):
                    mm, j = divmod(i, 2)
                    m = mp * 2 + mm
                    sync.wait_ge(s_ev, 4 * mp + i + 1)
                    sync.dma_start(out=out_scr[m * 128:(m + 1) * 128, j * 512:(j + 1) * 512],
                                   in_=ofm[:, pb, i, :]).then_inc(s_os, 16)
                sync.wait_ge(s_os, 64 * (mp + 1))
                sync.sem_inc(s_og, 1)
            sync.wait_ge(s_os, 16 * 4 * NMP)
            for nt in range(NT):
                if nt >= 1:
                    sync.wait_ge(s_v2, 3 * nt)
                sync.dma_start(out=obf[:], in_=out_scr[:, nt * 128:(nt + 1) * 128],
                               transpose=True).then_inc(s_tl, 16)
                sync.dma_start(out=xt[:], in_=x_in[nt * 128:(nt + 1) * 128, :]).then_inc(s_tl, 16)
                sync.dma_start(out=lnt[:], in_=ln0_in[nt * 128:(nt + 1) * 128, :]).then_inc(s_tl, 16)
                sync.wait_ge(s_v2, 3 * nt + 3)
                sync.dma_start(out=x1_out[nt * 128:(nt + 1) * 128, :],
                               in_=x1t[:, nt % 2, :]).then_inc(s_x1, 16)
                sync.wait_ge(s_x1, 16 * (nt + 1))
                sync.sem_inc(s_x1g, 1)
            sync.wait_ge(s_gr, 3)
            sync.dma_start(out=ginv_scr[:], in_=ginv[:]).then_inc(s_c, 16)
            sync.wait_ge(s_c, 48)
            sync.dma_start(out=ginvb[:], in_=ginv_scr.partition_broadcast(128)).then_inc(s_c, 16)
            sync.wait_ge(s_gs, 16 * (len(QTILES) // 2))
            sync.wait_ge(s_gs1, 16 * (len(QTILES) // 2))

        @block.tensor
        def _(tensor):
            tensor.wait_ge(s_y, 16 * KT)
            for mp in range(NMP):
                pb = mp % 2
                tensor.wait_ge(s_w, 16 * KT * (mp + 1))
                for i in range(4):
                    mm, j = divmod(i, 2)
                    if mp >= 2:
                        tensor.wait_ge(s_ev, 4 * (mp - 1))
                    for k in range(KT):
                        mo = tensor.matmul(ps[:, pb, i, :],
                                           wb[:, 0, k, mm * 128:(mm + 1) * 128],
                                           yfm[:, k, j * 512:(j + 1) * 512],
                                           start=(k == 0), stop=(k == KT - 1))
                    mo.then_inc(s_mm, 1)

        @block.vector
        def _(vector):
            for mp in range(NMP):
                pb = mp % 2
                for i in range(4):
                    vector.wait_ge(s_mm, 4 * mp + i + 1)
                    if mp >= 2:
                        vector.wait_ge(s_og, mp - 1)
                    vector.tensor_copy(out=ofm[:, pb, i, :], in_=ps[:, pb, i, :])
                    vector.drain()
                    vector.memset(bar[:, 0:1], 0.0).then_inc(s_ev, 1)
            for nt in range(NT):
                vector.wait_ge(s_tl, 48 * nt + 48)
                if nt >= 2:
                    vector.wait_ge(s_x1g, nt - 1)
                vector.tensor_copy(out=of32[:], in_=obf[:]).then_inc(s_v2, 1)
                vector.tensor_tensor(out=x1t[:, nt % 2, :], in0=xt[:], in1=lnt[:],
                                     op=ALU.add).then_inc(s_v2, 1)
                vector.drain()
                vector.tensor_tensor(out=x1t[:, nt % 2, :], in0=x1t[:, nt % 2, :],
                                     in1=of32[:], op=ALU.add)
                vector.drain()
                vector.memset(bar[:, 0:1], 0.0).then_inc(s_v2, 1)
            vector.wait_ge(s_c, 32)
            vector.tensor_reduce(out=gsum[0:1, 0, :],
                                 in_=gq.ap().rearrange("p (c j) -> p j c", c=NCORES),
                                 axis=AX.X, op=ALU.add)
            vector.drain()
            vector.memset(bar[:, 0:1], 0.0).then_inc(s_gr, 1)
            vector.wait_ge(s_gr, 2)
            vector.reciprocal(out=ginv[:], in_=gsum[0:1, 1, :])
            vector.drain()
            vector.memset(bar[:, 0:1], 0.0).then_inc(s_gr, 1)
            for i, (wn, ti, c0, wid) in enumerate(QTILES):
                vector.wait_ge(s_sc, i + 1)
                if i >= 2:
                    vector.wait_ge([s_gs, s_gs1][i % 2], 16 * (i // 2))
                vector.tensor_copy(out=itt[:, 0:wid], in_=sct[:, 0:wid])
                vector.drain()
                vector.tensor_scalar(out=qt[:, i % 2, 0:wid], in0=itt[:, 0:wid],
                                     scalar1=-1.0, scalar2=1.0, op0=ALU.max, op1=ALU.min)
                vector.drain()
                vector.memset(bar[:, 0:1], 0.0).then_inc(s_gv, 1)

        @block.scalar
        def _(scalar):
            NW = float(HBIT * D)
            scalar.wait_ge(s_gr, 1)
            scalar.activation(out=gsum[0:1, 1, :], in_=gsum[0:1, 0, :], func=AF.Identity,
                              scale=1.0 / NW, bias=cst[0:1, 0:1])
            scalar.drain()
            scalar.copy(out=bar[0:1, 0:1], in_=cst[0:1, 0:1]).then_inc(s_gr, 1)
            scalar.wait_ge(s_c, 64)
            for i, (wn, ti, c0, wid) in enumerate(QTILES):
                j = {"w1": 0, "w2": 1, "w3": 2}[wn]
                scalar.wait_ge([s_g, s_g1][i % 2], 16 * (i // 2 + 1))
                if i >= 1:
                    scalar.wait_ge(s_gv, i)
                scalar.activation(out=sct[:, 0:wid], in_=gw[:, i % 2, 0:wid],
                                  func=AF.Identity, scale=ginvb[:, j:j + 1])
                scalar.drain()
                scalar.copy(out=bar[0:1, 0:1], in_=cst[0:1, 0:1]).then_inc(s_sc, 1)

    return nc


def run_L4(inputs, y, ln0, gparts):
    nc = build_L4()
    x = inputs["L0_attn_hidden_states"]
    woT = np.ascontiguousarray(inputs["L0_attn_out_proj_w"].T)
    gp = np.ascontiguousarray(gparts.reshape(1, 3 * NCORES))
    w1 = inputs["L0_ffn_w1"]
    w2 = inputs["L0_ffn_w2"]
    w3 = inputs["L0_ffn_w3"]
    pad = np.zeros((64, D), np.float32)
    cstv = np.array([[EPS_BIT, 1.0]], np.float32)
    in_maps = []
    for c in range(NCORES):
        tsl = slice(c * TC, (c + 1) * TC)
        yfm = np.ascontiguousarray(
            y[:, :, tsl].transpose(1, 0, 2).reshape(DIN, TOK))
        in_maps.append(dict(
            yfm_in=yfm, woT_in=woT,
            x_in=np.ascontiguousarray(x[:, tsl, :].reshape(TOK, D)),
            ln0_in=np.ascontiguousarray(ln0[:, tsl, :].reshape(TOK, D)),
            gparts_in=gp,
            w1s_in=np.ascontiguousarray(np.concatenate([w1[c * 704:(c + 1) * 704], pad], 0)),
            w2s_in=np.ascontiguousarray(np.concatenate([w2[c * 704:(c + 1) * 704], pad], 0)),
            w3s_in=np.ascontiguousarray(w3[c * 256:(c + 1) * 256]),
            cst_in=cstv))
    res = run_bass_kernel_spmd(nc, in_maps, core_ids=list(range(NCORES)))
    x1 = np.empty((B, L, D), np.float32)
    qw1 = np.empty((HBIT, D), BF)
    qw2 = np.empty((HBIT, D), BF)
    qw3 = np.empty((D, HBIT), BF)
    for c, r in enumerate(res.results):
        x1[:, c * TC:(c + 1) * TC, :] = r["x1_out"].reshape(B, TC, D)
        qw1[c * 704:(c + 1) * 704] = r["qw1_out"][:704]
        qw2[c * 704:(c + 1) * 704] = r["qw2_out"][:704]
        qw3[c * 256:(c + 1) * 256] = r["qw3_out"]
    return x1, qw1, qw2, qw3




def build_L5():
    nc = bass.Bass("TRN2", target_bir_lowering=False, debug=False)
    x1_in = nc.dram_tensor("x1_in", [TOK, D], F32, kind="ExternalInput").ap()
    qw1T_in = nc.dram_tensor("qw1T_in", [D, HBIT], BF16, kind="ExternalInput").ap()
    qw2T_in = nc.dram_tensor("qw2T_in", [D, HBIT], BF16, kind="ExternalInput").ap()
    qw3T_in = nc.dram_tensor("qw3T_in", [HBIT, D], BF16, kind="ExternalInput").ap()
    wkT_in = nc.dram_tensor("wkT_in", [D, D], F32, kind="ExternalInput").ap()
    wvT_in = nc.dram_tensor("wvT_in", [D, D], F32, kind="ExternalInput").ap()
    cst_in = nc.dram_tensor("cst_in", [1, 2], F32, kind="ExternalInput").ap()

    x2_out = nc.dram_tensor("x2_out", [TOK, D], F32, kind="ExternalOutput").ap()
    xn2_out = nc.dram_tensor("xn2_out", [TOK, D], BF16, kind="ExternalOutput").ap()
    K_out = nc.dram_tensor("K_out", [D, TOK], BF16, kind="ExternalOutput").ap()
    V_out = nc.dram_tensor("V_out", [D, TOK], BF16, kind="ExternalOutput").ap()

    xn1_scr = nc.dram_tensor("xn1_scr", [TOK, D], BF16).ap()
    h_scr = nc.dram_tensor("h_scr", [HBIT, TOK], BF16).ap()
    dn_scr = nc.dram_tensor("dn_scr", [D, TOK], BF16).ap()

    KT = D // 128
    MPH = HBIT // 256
    MPD = D // 256
    KH = HBIT // 128
    NT = TOK // 128

    with contextlib.ExitStack() as ctx:
        sb = lambda *a: ctx.enter_context(nc.sbuf_tensor(*a))
        pt = lambda *a: ctx.enter_context(nc.psum_tensor(*a))
        sem = lambda n: ctx.enter_context(nc.semaphore(n))
        xt = sb("xt", [128, D], F32)
        dnb = sb("dnb", [128, D], BF16)
        dn32 = sb("dn32", [128, D], F32)
        x2t = sb("x2t", [128, 2, D], F32)
        sq = sb("sq", [128, D], F32)
        sq2 = sb("sq2", [128, D], F32)
        stat = sb("stat", [128, 4], F32)
        xnt = sb("xnt", [128, 2, D], BF16)
        cst = sb("cst", [128, 2], F32)
        bar = sb("bar", [128, 1], F32)
        xfm = sb("xfm", [128, KT, TOK], BF16)
        wA = sb("wA", [128, KT, 256], BF16)
        wB = sb("wB", [128, KT, 256], BF16)
        wC = sb("wC", [128, KH, 256], BF16)
        hst = sb("hst", [128, 2, 4, 512], BF16)
        hbuf = sb("hbuf", [128, 2, TOK], BF16)
        evg = sb("evg", [128, 2, 4, 512], BF16)
        evu = sb("evu", [128, 2, 4, 512], BF16)
        sgt = sb("sgt", [128, 512], F32)
        ps = pt("ps", [128, 8, 512], F32)
        s_c = sem("s_c")
        s_lx = sem("s_lx")
        s_xs1 = sem("s_xs1")
        s_xq1 = sem("s_xq1")
        s_ln = sem("s_ln")
        s_lns = sem("s_lns")
        s_tf1 = sem("s_tf1")
        s_tf2 = sem("s_tf2")
        s_w = sem("s_w")
        s_mm = sem("s_mm")
        s_hv = sem("s_hv")
        s_md = sem("s_md")
        s_dv = sem("s_dv")
        s_ds = sem("s_ds")
        s_x2 = sem("s_x2")
        s_xv = sem("s_xv")
        s_x2q = sem("s_x2q")
        s_x2s = sem("s_x2s")
        s_hs = sem("s_hs")
        s_hq = sem("s_hq")
        s_hl0 = sem("s_hl0")
        s_hl1 = sem("s_hl1")
        s_mdk = sem("s_mdk")
        s_mk = sem("s_mk")
        s_sg = sem("s_sg")
        s_kv = sem("s_kv")
        s_kvs = sem("s_kvs")
        block = ctx.enter_context(nc.Block())

        def ln_vector(vector, in_ap, nn):
            vector.reduce_sum(out=stat[:, 0:1], in_=in_ap, axis=AX.X)
            vector.drain()
            vector.tensor_scalar(out=stat[:, 1:2], in0=stat[:, 0:1], scalar1=-1.0 / D,
                                 scalar2=None, op0=ALU.mult, op1=ALU.bypass)
            vector.drain()
            vector.tensor_scalar(out=sq[:], in0=in_ap, scalar1=stat[:, 1:2],
                                 scalar2=None, op0=ALU.add, op1=ALU.bypass)
            vector.drain()
            vector.memset(bar[:, 0:1], 0.0).then_inc(s_ln, 1)
            vector.wait_ge(s_lns, nn + 1)
            vector.reciprocal(out=stat[:, 3:4], in_=stat[:, 2:3])
            vector.drain()
            vector.tensor_scalar(out=xnt[:, nn % 2, :], in0=sq[:], scalar1=stat[:, 3:4],
                                 scalar2=None, op0=ALU.mult, op1=ALU.bypass)
            vector.drain()
            vector.memset(bar[:, 0:1], 0.0).then_inc(s_ln, 1)

        def ln_scalar(scalar, nn):
            scalar.wait_ge(s_ln, 2 * nn + 1)
            scalar.activation(out=sq2[:], in_=sq[:], func=AF.Square, accum_out=stat[:, 0:1])
            scalar.drain()
            scalar.activation(out=stat[:, 2:3], in_=stat[:, 0:1], func=AF.Sqrt,
                              scale=1.0 / D, bias=cst[:, 0:1])
            scalar.drain()
            scalar.copy(out=bar[0:1, 0:1], in_=cst[0:1, 0:1]).then_inc(s_lns, 1)

        @block.sync
        def _(sync):
            sync.dma_start(out=cst[:], in_=cst_in.partition_broadcast(128)).then_inc(s_c, 16)
            for n in range(NT):
                if n >= 1:
                    sync.wait_ge(s_ln, 2 * n)
                sync.dma_start(out=xt[:], in_=x1_in[n * 128:(n + 1) * 128, :]).then_inc(s_lx, 16)
                sync.wait_ge(s_ln, 2 * n + 2)
                sync.dma_start(out=xn1_scr[n * 128:(n + 1) * 128, :],
                               in_=xnt[:, n % 2, :]).then_inc(s_xs1, 16)
                sync.wait_ge(s_xs1, 16 * (n + 1))
                sync.sem_inc(s_xq1, 1)
            for k in range(KT):
                sync.dma_start(out=xfm[:, k, :], in_=xn1_scr[:, k * 128:(k + 1) * 128],
                               transpose=True).then_inc(s_tf1, 16)
            for mp in range(MPH):
                pb = mp % 2
                for i in range(4):
                    mm, j = divmod(i, 2)
                    m = mp * 2 + mm
                    sync.wait_ge(s_hv, 4 * mp + i + 1)
                    sync.dma_start(out=h_scr[m * 128:(m + 1) * 128, j * 512:(j + 1) * 512],
                                   in_=hst[:, pb, i, :]).then_inc(s_hs, 16)
                sync.wait_ge(s_hs, 64 * (mp + 1))
                sync.sem_inc(s_hq, 1)
            for mp in range(MPD):
                pb = mp % 2
                for i in range(4):
                    mm, j = divmod(i, 2)
                    m = mp * 2 + mm
                    sync.wait_ge(s_dv, 4 * mp + i + 1)
                    sync.dma_start(out=dn_scr[m * 128:(m + 1) * 128, j * 512:(j + 1) * 512],
                                   in_=evg[:, pb, i, :]).then_inc(s_ds, 16)
                sync.wait_ge(s_ds, 64 * (mp + 1) + mp)
                sync.sem_inc(s_ds, 1)
            sync.wait_ge(s_ds, 64 * MPD + MPD)
            for n in range(NT):
                if n >= 1:
                    sync.wait_ge(s_ln, 2 * (NT + n))
                sync.dma_start(out=xt[:], in_=x1_in[n * 128:(n + 1) * 128, :]).then_inc(s_x2, 16)
                sync.dma_start(out=dnb[:], in_=dn_scr[:, n * 128:(n + 1) * 128],
                               transpose=True).then_inc(s_x2, 16)
                sync.wait_ge(s_xv, n + 1)
                sync.dma_start(out=x2_out[n * 128:(n + 1) * 128, :],
                               in_=x2t[:, n % 2, :]).then_inc(s_x2s, 16)
                sync.wait_ge(s_ln, 2 * (NT + n) + 2)
                sync.dma_start(out=xn2_out[n * 128:(n + 1) * 128, :],
                               in_=xnt[:, n % 2, :]).then_inc(s_x2s, 16)
                sync.wait_ge(s_x2s, 32 * (n + 1))
                sync.sem_inc(s_x2q, 1)
            for k in range(KT):
                sync.dma_start(out=xfm[:, k, :], in_=xn2_out[:, k * 128:(k + 1) * 128],
                               transpose=True).then_inc(s_tf2, 16)
            for mp in range(MPD):
                pb = mp % 2
                for i in range(4):
                    sync.wait_ge(s_kv, 8 * mp + 2 * i + 2)
                    mm, j = divmod(i, 2)
                    m = mp * 2 + mm
                    sync.dma_start(out=K_out[m * 128:(m + 1) * 128, j * 512:(j + 1) * 512],
                                   in_=evg[:, pb, i, :]).then_inc(s_kvs, 16)
                    sync.dma_start(out=V_out[m * 128:(m + 1) * 128, j * 512:(j + 1) * 512],
                                   in_=evu[:, pb, i, :]).then_inc(s_kvs, 16)
                sync.wait_ge(s_kvs, 128 * (mp + 1) + mp)
                sync.sem_inc(s_kvs, 1)
            sync.wait_ge(s_kvs, 128 * MPD + MPD)

        @block.gpsimd
        def _(gpsimd):
            for mp in range(MPH):
                if mp >= 1:
                    gpsimd.wait_ge(s_mm, 8 * mp)
                for k in range(KT):
                    gpsimd.dma_start(out=wA[:, k, :],
                                     in_=qw1T_in[k * 128:(k + 1) * 128, mp * 256:(mp + 1) * 256]
                                     ).then_inc(s_w, 16)
                    gpsimd.dma_start(out=wB[:, k, :],
                                     in_=qw2T_in[k * 128:(k + 1) * 128, mp * 256:(mp + 1) * 256]
                                     ).then_inc(s_w, 16)
            gpsimd.wait_ge(s_hq, MPH)
            for mp in range(MPD):
                if mp >= 1:
                    gpsimd.wait_ge(s_dv, 4 * mp)
                for k in range(KH):
                    gpsimd.dma_start(out=wC[:, k, :],
                                     in_=qw3T_in[k * 128:(k + 1) * 128, mp * 256:(mp + 1) * 256]
                                     ).then_inc(s_w, 16)
                for k in range(KH):
                    if mp * KH + k >= 2:
                        gpsimd.wait_ge(s_mdk, mp * KH + k - 1)
                    gpsimd.dma_start(out=hbuf[:, k % 2, :],
                                     in_=h_scr[k * 128:(k + 1) * 128, :]
                                     ).then_inc([s_hl0, s_hl1][k % 2], 16)
            for mp in range(MPD):
                if mp >= 1:
                    gpsimd.wait_ge(s_mk, 8 * mp)
                for k in range(KT):
                    gpsimd.dma_start(out=wA[:, k, :],
                                     in_=wkT_in[k * 128:(k + 1) * 128, mp * 256:(mp + 1) * 256]
                                     ).then_inc(s_w, 16)
                    gpsimd.dma_start(out=wB[:, k, :],
                                     in_=wvT_in[k * 128:(k + 1) * 128, mp * 256:(mp + 1) * 256]
                                     ).then_inc(s_w, 16)

        @block.tensor
        def _(tensor):
            tensor.wait_ge(s_tf1, 16 * KT)
            for mp in range(MPH):
                tensor.wait_ge(s_w, 16 * 32 * (mp + 1))
                if mp >= 1:
                    tensor.wait_ge(s_hv, 4 * mp)
                for i in range(4):
                    mm, j = divmod(i, 2)
                    for k in range(KT):
                        mg = tensor.matmul(ps[:, i, :], wA[:, k, mm * 128:(mm + 1) * 128],
                                           xfm[:, k, j * 512:(j + 1) * 512],
                                           start=(k == 0), stop=(k == KT - 1))
                    mg.then_inc(s_mm, 1)
                    for k in range(KT):
                        mu = tensor.matmul(ps[:, 4 + i, :], wB[:, k, mm * 128:(mm + 1) * 128],
                                           xfm[:, k, j * 512:(j + 1) * 512],
                                           start=(k == 0), stop=(k == KT - 1))
                    mu.then_inc(s_mm, 1)
            tensor.wait_ge(s_sg, 4 * MPH)
            tensor.wait_ge(s_hv, 4 * MPH)
            W0 = 16 * 32 * MPH
            for mp in range(MPD):
                tensor.wait_ge(s_w, W0 + 16 * KH * (mp + 1))
                if mp >= 1:
                    tensor.wait_ge(s_dv, 4 * mp)
                for k in range(KH):
                    tensor.wait_ge([s_hl0, s_hl1][k % 2], 16 * (mp * 22 + k // 2 + 1))
                    for i in range(4):
                        mm, j = divmod(i, 2)
                        md = tensor.matmul(ps[:, i, :], wC[:, k, mm * 128:(mm + 1) * 128],
                                           hbuf[:, k % 2, j * 512:(j + 1) * 512],
                                           start=(k == 0), stop=(k == KH - 1))
                    md.then_inc(s_mdk, 1)
            tensor.wait_ge(s_dv, 4 * MPD)
            tensor.wait_ge(s_tf2, 16 * KT)
            W1 = W0 + 16 * KH * MPD
            for mp in range(MPD):
                tensor.wait_ge(s_w, W1 + 16 * 32 * (mp + 1))
                if mp >= 1:
                    tensor.wait_ge(s_kv, 8 * mp)
                for i in range(4):
                    mm, j = divmod(i, 2)
                    for k in range(KT):
                        mk1 = tensor.matmul(ps[:, i, :], wA[:, k, mm * 128:(mm + 1) * 128],
                                            xfm[:, k, j * 512:(j + 1) * 512],
                                            start=(k == 0), stop=(k == KT - 1))
                    mk1.then_inc(s_mk, 1)
                    for k in range(KT):
                        mv1 = tensor.matmul(ps[:, 4 + i, :], wB[:, k, mm * 128:(mm + 1) * 128],
                                            xfm[:, k, j * 512:(j + 1) * 512],
                                            start=(k == 0), stop=(k == KT - 1))
                    mv1.then_inc(s_mk, 1)

        @block.vector
        def _(vector):
            vector.wait_ge(s_c, 16)
            for n in range(NT):
                vector.wait_ge(s_lx, 16 * (n + 1))
                if n >= 2:
                    vector.wait_ge(s_xq1, n - 1)
                ln_vector(vector, xt[:], n)
            for mp in range(MPH):
                pb = mp % 2
                for i in range(4):
                    vector.wait_ge(s_sg, 4 * mp + i + 1)
                    if mp >= 2:
                        vector.wait_ge(s_hq, mp - 1)
                    vector.tensor_tensor(out=hst[:, pb, i, :],
                                         in0=sgt[:], in1=ps[:, 4 + i, :], op=ALU.mult)
                    vector.drain()
                    vector.memset(bar[:, 0:1], 0.0).then_inc(s_hv, 1)
            for mp in range(MPD):
                pb = mp % 2
                for i in range(4):
                    vector.wait_ge(s_mdk, (mp + 1) * KH)
                    if mp >= 2:
                        vector.wait_ge(s_ds, 64 * (mp - 1) + (mp - 1))
                    vector.tensor_copy(out=evg[:, pb, i, :], in_=ps[:, i, :])
                    vector.drain()
                    vector.memset(bar[:, 0:1], 0.0).then_inc(s_dv, 1)
            for n in range(NT):
                vector.wait_ge(s_x2, 32 * n + 32)
                if n >= 2:
                    vector.wait_ge(s_x2q, n - 1)
                vector.tensor_copy(out=dn32[:], in_=dnb[:])
                vector.drain()
                vector.tensor_tensor(out=x2t[:, n % 2, :], in0=xt[:], in1=dn32[:],
                                     op=ALU.add)
                vector.drain()
                vector.memset(bar[:, 0:1], 0.0).then_inc(s_xv, 1)
                ln_vector(vector, x2t[:, n % 2, :], NT + n)
            vector.wait_ge(s_ds, 64 * MPD + MPD)
            for mp in range(MPD):
                pb = mp % 2
                for i in range(4):
                    vector.wait_ge(s_mk, 8 * mp + 2 * i + 2)
                    if mp >= 2:
                        vector.wait_ge(s_kvs, 128 * (mp - 1) + (mp - 1))
                    vector.tensor_copy(out=evg[:, pb, i, :], in_=ps[:, i, :])
                    vector.drain()
                    vector.tensor_copy(out=evu[:, pb, i, :], in_=ps[:, 4 + i, :])
                    vector.drain()
                    vector.memset(bar[:, 0:1], 0.0).then_inc(s_kv, 2)

        @block.scalar
        def _(scalar):
            scalar.wait_ge(s_c, 16)
            for n in range(NT):
                ln_scalar(scalar, n)
            for mp in range(MPH):
                for i in range(4):
                    scalar.wait_ge(s_mm, 8 * mp + 2 * i + 2)
                    if 4 * mp + i >= 1:
                        scalar.wait_ge(s_hv, 4 * mp + i)
                    scalar.activation(out=sgt[:], in_=ps[:, i, :], func=AF.Sigmoid)
                    scalar.drain()
                    scalar.copy(out=bar[0:1, 0:1], in_=cst[0:1, 0:1]).then_inc(s_sg, 1)
            for n in range(NT):
                ln_scalar(scalar, NT + n)

    return nc


def run_L5(inputs, x1, qw1, qw2, qw3):
    nc = build_L5()
    qw1T = np.ascontiguousarray(qw1.T)
    qw2T = np.ascontiguousarray(qw2.T)
    qw3T = np.ascontiguousarray(qw3.T)
    wkT = np.ascontiguousarray(inputs["L1_attn_w_k"].T)
    wvT = np.ascontiguousarray(inputs["L1_attn_w_v"].T)
    cstv = np.array([[EPS_LN, 1.0]], np.float32)
    in_maps = []
    for c in range(NCORES):
        tsl = slice(c * TC, (c + 1) * TC)
        in_maps.append(dict(
            x1_in=np.ascontiguousarray(x1[:, tsl, :].reshape(TOK, D)),
            qw1T_in=qw1T, qw2T_in=qw2T, qw3T_in=qw3T,
            wkT_in=wkT, wvT_in=wvT, cst_in=cstv))
    res = run_bass_kernel_spmd(nc, in_maps, core_ids=list(range(NCORES)))
    x2 = np.empty((B, L, D), np.float32)
    xn2 = np.empty((B, L, D), BF)
    Kf = np.empty((D, B, L), BF)
    Vf = np.empty((D, B, L), BF)
    for c, r in enumerate(res.results):
        x2[:, c * TC:(c + 1) * TC, :] = r["x2_out"].reshape(B, TC, D)
        xn2[:, c * TC:(c + 1) * TC, :] = r["xn2_out"].reshape(B, TC, D)
        Kf[:, :, c * TC:(c + 1) * TC] = r["K_out"].reshape(D, B, TC)
        Vf[:, :, c * TC:(c + 1) * TC] = r["V_out"].reshape(D, B, TC)
    return x2, xn2, Kf, Vf


HB2 = 5632


def build_L6():
    nc = bass.Bass("TRN2", target_bir_lowering=False, debug=False)
    gidx_in = nc.dram_tensor("gidx_in", [768, D], I32, kind="ExternalInput").ap()
    uidx_in = nc.dram_tensor("uidx_in", [768, D], I32, kind="ExternalInput").ap()
    didx_in = nc.dram_tensor("didx_in", [256, HGHOST], I32, kind="ExternalInput").ap()
    lut_in = nc.dram_tensor("lut_in", [1, 16], F32, kind="ExternalInput").ap()

    gw_out = nc.dram_tensor("gw_out", [768, D], BF16, kind="ExternalOutput").ap()
    uw_out = nc.dram_tensor("uw_out", [768, D], BF16, kind="ExternalOutput").ap()
    dw_out = nc.dram_tensor("dw_out", [256, HGHOST], BF16, kind="ExternalOutput").ap()

    TILES = ([("g", i, D) for i in range(6)] + [("u", i, D) for i in range(6)]
             + [("d", i, HGHOST) for i in range(2)])

    with contextlib.ExitStack() as ctx:
        sb = lambda *a: ctx.enter_context(nc.sbuf_tensor(*a))
        sem = lambda n: ctx.enter_context(nc.semaphore(n))
        lutb = sb("lutb", [128, 16], F32)
        idxt = sb("idxt", [128, 2, HGHOST], I32)
        acc = sb("acc", [128, 2, HGHOST], BF16)
        tmp = sb("tmp", [128, HGHOST], BF16)
        bar = sb("bar", [128, 1], F32)
        s_c = sem("s_c")
        s_l0 = sem("s_l0")
        s_l1 = sem("s_l1")
        s_v = sem("s_v")
        s_st0 = sem("s_st0")
        s_st1 = sem("s_st1")
        block = ctx.enter_context(nc.Block())

        @block.sync
        def _(sync):
            sync.dma_start(out=lutb[:], in_=lut_in.partition_broadcast(128)).then_inc(s_c, 16)
            for i, (wn, ti, wid) in enumerate(TILES):
                if i >= 2:
                    sync.wait_ge(s_v, i - 1)
                srcm = {"g": gidx_in, "u": uidx_in, "d": didx_in}[wn]
                sync.dma_start(out=idxt[:, i % 2, 0:wid],
                               in_=srcm[ti * 128:(ti + 1) * 128, :]).then_inc([s_l0, s_l1][i % 2], 16)

        @block.gpsimd
        def _(gpsimd):
            for i, (wn, ti, wid) in enumerate(TILES):
                dstm = {"g": gw_out, "u": uw_out, "d": dw_out}[wn]
                gpsimd.wait_ge(s_v, i + 1)
                gpsimd.dma_start(out=dstm[ti * 128:(ti + 1) * 128, :],
                                 in_=acc[:, i % 2, 0:wid]).then_inc([s_st0, s_st1][i % 2], 16)

        @block.vector
        def _(vector):
            vector.wait_ge(s_c, 16)
            for i, (wn, ti, wid) in enumerate(TILES):
                vector.wait_ge([s_l0, s_l1][i % 2], 16 * (i // 2 + 1))
                if i >= 2:
                    vector.wait_ge([s_st0, s_st1][i % 2], 16 * (i // 2))
                for kk in range(16):
                    vector.tensor_scalar(out=tmp[:, 0:wid], in0=idxt[:, i % 2, 0:wid],
                                         scalar1=float(kk), scalar2=lutb[:, kk:kk + 1],
                                         op0=ALU.is_equal, op1=ALU.mult)
                    vector.drain()
                    if kk == 0:
                        vector.tensor_copy(out=acc[:, i % 2, 0:wid], in_=tmp[:, 0:wid])
                    else:
                        vector.tensor_tensor(out=acc[:, i % 2, 0:wid], in0=acc[:, i % 2, 0:wid],
                                             in1=tmp[:, 0:wid], op=ALU.add)
                    vector.drain()
                vector.memset(bar[:, 0:1], 0.0).then_inc(s_v, 1)

        @block.tensor
        def _(tensor):
            tensor.nop()

        @block.scalar
        def _(scalar):
            scalar.nop()

        @block.sync
        def _(sync):
            sync.wait_ge(s_st0, 16 * 7)
            sync.wait_ge(s_st1, 16 * 7)

    return nc


def run_L6(inputs):
    nc = build_L6()
    gidx = inputs["L1_ffn_gate_idx"]
    uidx = inputs["L1_ffn_up_idx"]
    didx = inputs["L1_ffn_down_idx"]
    lut = np.ascontiguousarray(inputs["L1_ffn_lut"].reshape(1, 16))
    in_maps = []
    for c in range(NCORES):
        in_maps.append(dict(
            gidx_in=_padrows(gidx[c * 688:(c + 1) * 688], 768),
            uidx_in=_padrows(uidx[c * 688:(c + 1) * 688], 768),
            didx_in=np.ascontiguousarray(didx[c * 256:(c + 1) * 256]),
            lut_in=lut))
    res = run_bass_kernel_spmd(nc, in_maps, core_ids=list(range(NCORES)))
    gw = np.empty((HGHOST, D), BF)
    uw = np.empty((HGHOST, D), BF)
    dw = np.empty((D, HGHOST), BF)
    for c, r in enumerate(res.results):
        gw[c * 688:(c + 1) * 688] = r["gw_out"][:688]
        uw[c * 688:(c + 1) * 688] = r["uw_out"][:688]
        dw[c * 256:(c + 1) * 256] = r["dw_out"]
    return gw, uw, dw


NQI = 8


def build_L7a():
    nc = bass.Bass("TRN2", target_bir_lowering=False, debug=False)
    xn2s_in = nc.dram_tensor("xn2s_in", [TOK, D], BF16, kind="ExternalInput").ap()
    x2s_in = nc.dram_tensor("x2s_in", [TOK, D], F32, kind="ExternalInput").ap()
    K_in = nc.dram_tensor("K_in", [D, B * L], BF16, kind="ExternalInput").ap()
    V_in = nc.dram_tensor("V_in", [B * L, D], BF16, kind="ExternalInput").ap()
    wqT_in = nc.dram_tensor("wqT_in", [D, D], F32, kind="ExternalInput").ap()
    woT_in = nc.dram_tensor("woT_in", [D, D], F32, kind="ExternalInput").ap()
    mask_in = nc.dram_tensor("mask_in", [TOK, L], BF16, kind="ExternalInput").ap()
    ident_in = nc.dram_tensor("ident_in", [128, 128], BF16, kind="ExternalInput").ap()

    x3s_out = nc.dram_tensor("x3s_out", [TOK, D], F32, kind="ExternalOutput").ap()

    o_scr = nc.dram_tensor("o_scr", [TOK, D], BF16).ap()
    wo_scr = nc.dram_tensor("wo_scr", [D, TOK], BF16).ap()

    KT = D // 128
    MPD = D // 256
    NT = TOK // 128
    NKC = L // 512
    NKK = L // 128
    SCL = 1.0 / float(np.sqrt(HD))

    with contextlib.ExitStack() as ctx:
        sb = lambda *a: ctx.enter_context(nc.sbuf_tensor(*a))
        pt = lambda *a: ctx.enter_context(nc.psum_tensor(*a))
        sem = lambda n: ctx.enter_context(nc.semaphore(n))
        xfm = sb("xfm", [128, KT, TOK], BF16)
        wA = sb("wA", [128, KT, 256], BF16)
        wB = sb("wB", [128, KT, 256], BF16)
        Qfm = sb("Qfm", [128, KT, TOK], BF16)
        Kslab = sb("Kslab", [128, L], BF16)
        Vslab = sb("Vslab", [128, NKK, 128], BF16)
        maskt = sb("maskt", [128, 4, L], BF16)
        Ssb = sb("Ssb", [128, L], F32)
        Psb = sb("Psb", [128, L], BF16)
        PTt = sb("PTt", [128, 2, 128], BF16)
        ident = sb("ident", [128, 128], BF16)
        negmax = sb("negmax", [128, 1], F32)
        sbar = sb("sbar", [1, 2], F32)
        rowsum = sb("rowsum", [128, 1], F32)
        rl = sb("rl", [128, 1], F32)
        otile = sb("otile", [128, 4, 2, 128], BF16)
        evg = sb("evg", [128, 2, 4, 512], BF16)
        xt = sb("xt", [128, D], F32)
        wob = sb("wob", [128, D], BF16)
        wo32 = sb("wo32", [128, D], F32)
        x3t = sb("x3t", [128, 2, D], F32)
        bar = sb("bar", [128, 1], F32)
        psS = pt("psS", [128, 2, 512], F32)
        psT = pt("psT", [128, 2, 128], BF16)
        psO = pt("psO", [128, 128], F32)
        psW = pt("psW", [128, 4, 512], F32)
        s_c = sem("s_c")
        s_tf = sem("s_tf")
        s_wq = sem("s_wq")
        s_mq = sem("s_mq")
        s_qe = sem("s_qe")
        s_kv = sem("s_kv")
        s_mk = sem("s_mk")
        s_ms = sem("s_ms")
        s_se = sem("s_se")
        s_ex = sem("s_ex")
        s_rv = sem("s_rv")
        s_pt = sem("s_pt")
        s_pe = sem("s_pe")
        s_ao = sem("s_ao")
        s_oe = sem("s_oe")
        s_os = sem("s_os")
        s_oq = sem("s_oq")
        s_tf2 = sem("s_tf2")
        s_ww = sem("s_ww")
        s_mw = sem("s_mw")
        s_we = sem("s_we")
        s_ws = sem("s_ws")
        s_x3l = sem("s_x3l")
        s_x3v = sem("s_x3v")
        s_x3s = sem("s_x3s")
        s_x3q = sem("s_x3q")
        block = ctx.enter_context(nc.Block())

        def qi_of(i):
            return i

        @block.sync
        def _(sync):
            sync.dma_start(out=ident[:], in_=ident_in[:]).then_inc(s_c, 16)
            for k in range(KT):
                sync.dma_start(out=xfm[:, k, :], in_=xn2s_in[:, k * 128:(k + 1) * 128],
                               transpose=True).then_inc(s_tf, 16)
            for b in range(B):
                if b > 0:
                    sync.wait_ge(s_oq, b * (HEADS // 2))
                sync.dma_start(out=maskt[:],
                               in_=mask_in.rearrange("(b q p) l -> p (b q) l", b=B, p=128)[
                                   :, b * 4:(b + 1) * 4, :]).then_inc(s_mk, 16)
                for h in range(HEADS):
                    ih = b * HEADS + h
                    if ih >= 1:
                        sync.wait_ge(s_ao, ih * NQI // 2)
                    sync.dma_start(out=Kslab[:],
                                   in_=K_in[h * 128:(h + 1) * 128, b * L:(b + 1) * L]
                                   ).then_inc(s_kv, 16)
                    sync.dma_start(out=Vslab[:],
                                   in_=V_in[b * L:(b + 1) * L, h * 128:(h + 1) * 128]
                                   .rearrange("(kk p) d -> p kk d", p=128)).then_inc(s_kv, 16)
            sync.wait_ge(s_oq, B * (HEADS // 2))
            for k in range(KT):
                sync.dma_start(out=xfm[:, k, :], in_=o_scr[:, k * 128:(k + 1) * 128],
                               transpose=True).then_inc(s_tf2, 16)
            for mp in range(MPD):
                pb = mp % 2
                for i in range(4):
                    mm, j = divmod(i, 2)
                    m = mp * 2 + mm
                    sync.wait_ge(s_we, 4 * mp + i + 1)
                    sync.dma_start(out=wo_scr[m * 128:(m + 1) * 128, j * 512:(j + 1) * 512],
                                   in_=evg[:, pb, i, :]).then_inc(s_ws, 16)
                sync.wait_ge(s_ws, 64 * (mp + 1) + mp)
                sync.sem_inc(s_ws, 1)
            for n in range(NT):
                if n >= 1:
                    sync.wait_ge(s_x3v, 2 * n)
                sync.dma_start(out=xt[:], in_=x2s_in[n * 128:(n + 1) * 128, :]).then_inc(s_x3l, 16)
                sync.dma_start(out=wob[:], in_=wo_scr[:, n * 128:(n + 1) * 128],
                               transpose=True).then_inc(s_x3l, 16)
                sync.wait_ge(s_x3v, 2 * n + 2)
                sync.dma_start(out=x3s_out[n * 128:(n + 1) * 128, :],
                               in_=x3t[:, n % 2, :]).then_inc(s_x3s, 16)
                sync.wait_ge(s_x3s, 16 * (n + 1))
                sync.sem_inc(s_x3q, 1)

        @block.gpsimd
        def _(gpsimd):
            for mp in range(MPD):
                if mp >= 1:
                    gpsimd.wait_ge(s_mq, 4 * mp)
                for k in range(KT):
                    gpsimd.dma_start(out=wA[:, k, :],
                                     in_=wqT_in[k * 128:(k + 1) * 128, mp * 256:(mp + 1) * 256]
                                     ).then_inc(s_wq, 16)
            for b in range(B):
                for h in range(HEADS):
                    ih = b * HEADS + h
                    if h % 2 == 1:
                        for qi in range(4):
                            iq = (ih // 2) * 4 + qi
                            gpsimd.wait_ge(s_oe, ih * 4 + qi + 1)
                            gq = b * 4 + qi
                            gpsimd.dma_start(
                                out=o_scr[gq * 128:(gq + 1) * 128, (h - 1) * 128:(h + 1) * 128],
                                in_=otile[:, qi, :, :]).then_inc(s_os, 16)
                        gpsimd.wait_ge(s_os, 16 * ((ih // 2) * 4 + 4))
                        gpsimd.sem_inc(s_oq, 1)
            for mp in range(MPD):
                if mp >= 1:
                    gpsimd.wait_ge(s_mw, 4 * mp)
                for k in range(KT):
                    gpsimd.dma_start(out=wB[:, k, :],
                                     in_=woT_in[k * 128:(k + 1) * 128, mp * 256:(mp + 1) * 256]
                                     ).then_inc(s_ww, 16)

        @block.tensor
        def _(tensor):
            tensor.wait_ge(s_c, 16)
            tensor.wait_ge(s_tf, 16 * KT)
            for mp in range(MPD):
                tensor.wait_ge(s_wq, 16 * KT * (mp + 1))
                if mp >= 1:
                    tensor.wait_ge(s_qe, 4 * mp)
                for i in range(4):
                    mm, j = divmod(i, 2)
                    for k in range(KT):
                        mq = tensor.matmul(psW[:, i, :], wA[:, k, mm * 128:(mm + 1) * 128],
                                           xfm[:, k, j * 512:(j + 1) * 512],
                                           start=(k == 0), stop=(k == KT - 1))
                    mq.then_inc(s_mq, 1)
            tensor.wait_ge(s_qe, 4 * MPD)
            for b in range(B):
                for h in range(HEADS):
                    ih = b * HEADS + h
                    tensor.wait_ge(s_kv, 32 * (ih + 1))
                    for qi in range(4):
                        gq = b * 4 + qi
                        iq = ih * 4 + qi
                        for kb in range(NKC):
                            if iq * NKC + kb >= 2:
                                tensor.wait_ge(s_se, iq * NKC + kb - 1)
                            ms = tensor.matmul(psS[:, kb % 2, :],
                                               Qfm[:, h, gq * 128:(gq + 1) * 128],
                                               Kslab[:, kb * 512:(kb + 1) * 512],
                                               start=True, stop=True)
                            ms.then_inc(s_ms, 1)
                        tensor.wait_ge(s_ex, iq + 1)
                        for kk in range(NKK):
                            if iq * NKK + kk >= 2:
                                tensor.wait_ge(s_pe, iq * NKK + kk - 1)
                            tensor.transpose(psT[:, kk % 2, :], Psb[:, kk * 128:(kk + 1) * 128],
                                             ident[:]).then_inc(s_pt, 1)
                            tensor.wait_ge(s_pe, iq * NKK + kk + 1)
                            mav = tensor.matmul(psO[:], PTt[:, kk % 2, :], Vslab[:, kk, :],
                                                start=(kk == 0), stop=(kk == NKK - 1))
                        mav.then_inc(s_ao, 1)
                        if True:
                            tensor.wait_ge(s_oe, iq + 1)
            tensor.wait_ge(s_tf2, 16 * KT)
            for mp in range(MPD):
                tensor.wait_ge(s_ww, 16 * KT * (mp + 1))
                if mp >= 1:
                    tensor.wait_ge(s_we, 4 * mp)
                for i in range(4):
                    mm, j = divmod(i, 2)
                    for k in range(KT):
                        mw = tensor.matmul(psW[:, i, :], wB[:, k, mm * 128:(mm + 1) * 128],
                                           xfm[:, k, j * 512:(j + 1) * 512],
                                           start=(k == 0), stop=(k == KT - 1))
                    mw.then_inc(s_mw, 1)

        @block.vector
        def _(vector):
            for mp in range(MPD):
                for i in range(4):
                    mm, j = divmod(i, 2)
                    m = mp * 2 + mm
                    vector.wait_ge(s_mq, 4 * mp + i + 1)
                    vector.tensor_scalar(out=Qfm[:, m, j * 512:(j + 1) * 512],
                                         in0=psW[:, i, :], scalar1=SCL, scalar2=None,
                                         op0=ALU.mult, op1=ALU.bypass)
                    vector.drain()
                    vector.memset(bar[:, 0:1], 0.0).then_inc(s_qe, 1)
            for b in range(B):
                vector.wait_ge(s_mk, 16 * (b + 1))
                for h in range(HEADS):
                    ih = b * HEADS + h
                    for qi in range(4):
                        iq = ih * 4 + qi
                        for kb in range(NKC):
                            vector.wait_ge(s_ms, iq * NKC + kb + 1)
                            vector.tensor_tensor(out=Ssb[:, kb * 512:(kb + 1) * 512],
                                                 in0=psS[:, kb % 2, :],
                                                 in1=maskt[:, qi, kb * 512:(kb + 1) * 512],
                                                 op=ALU.add)
                            vector.drain()
                            vector.memset(bar[:, 0:1], 0.0).then_inc(s_se, 1)
                        vector.reduce_max(out=negmax[:], in_=Ssb[:], axis=AX.X, negate=True)
                        vector.drain()
                        vector.memset(bar[:, 0:1], 0.0).then_inc(s_rv, 2)
                        vector.wait_ge(s_ex, iq + 1)
                        vector.reciprocal(out=rl[:], in_=rowsum[:])
                        vector.drain()
                        vector.memset(bar[:, 0:1], 0.0).then_inc(s_rv, 1)
                        vector.wait_ge(s_ao, iq + 1)
                        if ih >= 2:
                            vector.wait_ge(s_oq, ih // 2)
                        vector.tensor_scalar(out=otile[:, qi, h % 2, :], in0=psO[:],
                                             scalar1=rl[:], scalar2=None,
                                             op0=ALU.mult, op1=ALU.bypass)
                        vector.drain()
                        vector.memset(bar[:, 0:1], 0.0).then_inc(s_oe, 1)
            for mp in range(MPD):
                pb = mp % 2
                for i in range(4):
                    vector.wait_ge(s_mw, 4 * mp + i + 1)
                    if mp >= 2:
                        vector.wait_ge(s_ws, 64 * (mp - 1) + (mp - 1))
                    vector.tensor_copy(out=evg[:, pb, i, :], in_=psW[:, i, :])
                    vector.drain()
                    vector.memset(bar[:, 0:1], 0.0).then_inc(s_we, 1)
            for n in range(NT):
                vector.wait_ge(s_x3l, 32 * (n + 1))
                if n >= 2:
                    vector.wait_ge(s_x3q, n - 1)
                vector.tensor_copy(out=wo32[:], in_=wob[:])
                vector.drain()
                vector.memset(bar[:, 0:1], 0.0).then_inc(s_x3v, 1)
                vector.tensor_tensor(out=x3t[:, n % 2, :], in0=xt[:], in1=wo32[:],
                                     op=ALU.add)
                vector.drain()
                vector.memset(bar[:, 0:1], 0.0).then_inc(s_x3v, 1)

        @block.scalar
        def _(scalar):
            scalar.wait_ge(s_c, 16)
            for b in range(B):
                for h in range(HEADS):
                    ih = b * HEADS + h
                    for qi in range(4):
                        iq = ih * 4 + qi
                        scalar.wait_ge(s_rv, 3 * iq + 2)
                        scalar.activation(out=Psb[:], in_=Ssb[:], func=AF.Exp,
                                          bias=negmax[:], accum_out=rowsum[:])
                        scalar.drain()
                        scalar.copy(out=sbar[0:1, 0:1], in_=ident[0:1, 0:1]).then_inc(s_ex, 1)
                        for kk in range(NKK):
                            scalar.wait_ge(s_pt, iq * NKK + kk + 1)
                            scalar.copy(out=PTt[:, kk % 2, :], in_=psT[:, kk % 2, :])
                            scalar.drain()
                            scalar.copy(out=sbar[0:1, 0:1], in_=ident[0:1, 0:1]).then_inc(s_pe, 1)

    return nc


def build_L7b():
    nc = bass.Bass("TRN2", target_bir_lowering=False, debug=False)
    x3_in = nc.dram_tensor("x3_in", [TOK, D], F32, kind="ExternalInput").ap()
    gwT_in = nc.dram_tensor("gwT_in", [D, HB2], BF16, kind="ExternalInput").ap()
    uwT_in = nc.dram_tensor("uwT_in", [D, HB2], BF16, kind="ExternalInput").ap()
    dwT_in = nc.dram_tensor("dwT_in", [HB2, D], BF16, kind="ExternalInput").ap()
    cst_in = nc.dram_tensor("cst_in", [1, 2], F32, kind="ExternalInput").ap()

    x4_out = nc.dram_tensor("x4_out", [TOK, D], F32, kind="ExternalOutput").ap()

    xn3_scr = nc.dram_tensor("xn3_scr", [TOK, D], BF16).ap()
    h_scr = nc.dram_tensor("h_scr", [HB2, TOK], BF16).ap()
    dn_scr = nc.dram_tensor("dn_scr", [D, TOK], BF16).ap()

    KT = D // 128
    MPH = HB2 // 256
    MPD = D // 256
    KH = HB2 // 128
    NT = TOK // 128

    with contextlib.ExitStack() as ctx:
        sb = lambda *a: ctx.enter_context(nc.sbuf_tensor(*a))
        pt = lambda *a: ctx.enter_context(nc.psum_tensor(*a))
        sem = lambda n: ctx.enter_context(nc.semaphore(n))
        xt = sb("xt", [128, D], F32)
        dnb = sb("dnb", [128, D], BF16)
        dn32 = sb("dn32", [128, D], F32)
        x4t = sb("x4t", [128, 2, D], F32)
        sq = sb("sq", [128, D], F32)
        sq2 = sb("sq2", [128, D], F32)
        stat = sb("stat", [128, 4], F32)
        xnt = sb("xnt", [128, 2, D], BF16)
        cst = sb("cst", [128, 2], F32)
        bar = sb("bar", [128, 1], F32)
        sbar = sb("sbar", [1, 2], F32)
        xfm = sb("xfm", [128, KT, TOK], BF16)
        wA = sb("wA", [128, KT, 256], BF16)
        wB = sb("wB", [128, KT, 256], BF16)
        wC = sb("wC", [128, KH, 256], BF16)
        hst = sb("hst", [128, 2, 4, 512], BF16)
        hbuf = sb("hbuf", [128, 2, TOK], BF16)
        evg = sb("evg", [128, 2, 4, 512], BF16)
        sgt = sb("sgt", [128, 512], F32)
        ps = pt("ps", [128, 8, 512], F32)
        s_c = sem("s_c")
        s_lx = sem("s_lx")
        s_xs1 = sem("s_xs1")
        s_xq1 = sem("s_xq1")
        s_ln = sem("s_ln")
        s_lns = sem("s_lns")
        s_tf1 = sem("s_tf1")
        s_w = sem("s_w")
        s_mm = sem("s_mm")
        s_sg = sem("s_sg")
        s_hv = sem("s_hv")
        s_hs = sem("s_hs")
        s_hq = sem("s_hq")
        s_hl0 = sem("s_hl0")
        s_hl1 = sem("s_hl1")
        s_mdk = sem("s_mdk")
        s_dv = sem("s_dv")
        s_ds = sem("s_ds")
        s_x2 = sem("s_x2")
        s_xv = sem("s_xv")
        s_x2q = sem("s_x2q")
        s_x2s = sem("s_x2s")
        block = ctx.enter_context(nc.Block())

        def ln_vector(vector, in_ap, nn):
            vector.reduce_sum(out=stat[:, 0:1], in_=in_ap, axis=AX.X)
            vector.drain()
            vector.tensor_scalar(out=stat[:, 1:2], in0=stat[:, 0:1], scalar1=-1.0 / D,
                                 scalar2=None, op0=ALU.mult, op1=ALU.bypass)
            vector.drain()
            vector.tensor_scalar(out=sq[:], in0=in_ap, scalar1=stat[:, 1:2],
                                 scalar2=None, op0=ALU.add, op1=ALU.bypass)
            vector.drain()
            vector.memset(bar[:, 0:1], 0.0).then_inc(s_ln, 1)
            vector.wait_ge(s_lns, nn + 1)
            vector.reciprocal(out=stat[:, 3:4], in_=stat[:, 2:3])
            vector.drain()
            vector.tensor_scalar(out=xnt[:, nn % 2, :], in0=sq[:], scalar1=stat[:, 3:4],
                                 scalar2=None, op0=ALU.mult, op1=ALU.bypass)
            vector.drain()
            vector.memset(bar[:, 0:1], 0.0).then_inc(s_ln, 1)

        def ln_scalar(scalar, nn):
            scalar.wait_ge(s_ln, 2 * nn + 1)
            scalar.activation(out=sq2[:], in_=sq[:], func=AF.Square, accum_out=stat[:, 0:1])
            scalar.drain()
            scalar.activation(out=stat[:, 2:3], in_=stat[:, 0:1], func=AF.Sqrt,
                              scale=1.0 / D, bias=cst[:, 0:1])
            scalar.drain()
            scalar.copy(out=bar[0:1, 0:1], in_=cst[0:1, 0:1]).then_inc(s_lns, 1)

        @block.sync
        def _(sync):
            sync.dma_start(out=cst[:], in_=cst_in.partition_broadcast(128)).then_inc(s_c, 16)
            for n in range(NT):
                if n >= 1:
                    sync.wait_ge(s_ln, 2 * n)
                sync.dma_start(out=xt[:], in_=x3_in[n * 128:(n + 1) * 128, :]).then_inc(s_lx, 16)
                sync.wait_ge(s_ln, 2 * n + 2)
                sync.dma_start(out=xn3_scr[n * 128:(n + 1) * 128, :],
                               in_=xnt[:, n % 2, :]).then_inc(s_xs1, 16)
                sync.wait_ge(s_xs1, 16 * (n + 1))
                sync.sem_inc(s_xq1, 1)
            for k in range(KT):
                sync.dma_start(out=xfm[:, k, :], in_=xn3_scr[:, k * 128:(k + 1) * 128],
                               transpose=True).then_inc(s_tf1, 16)
            for mp in range(MPH):
                pb = mp % 2
                for i in range(4):
                    mm, j = divmod(i, 2)
                    m = mp * 2 + mm
                    sync.wait_ge(s_hv, 4 * mp + i + 1)
                    sync.dma_start(out=h_scr[m * 128:(m + 1) * 128, j * 512:(j + 1) * 512],
                                   in_=hst[:, pb, i, :]).then_inc(s_hs, 16)
                sync.wait_ge(s_hs, 64 * (mp + 1))
                sync.sem_inc(s_hq, 1)
            for mp in range(MPD):
                pb = mp % 2
                for i in range(4):
                    mm, j = divmod(i, 2)
                    m = mp * 2 + mm
                    sync.wait_ge(s_dv, 4 * mp + i + 1)
                    sync.dma_start(out=dn_scr[m * 128:(m + 1) * 128, j * 512:(j + 1) * 512],
                                   in_=evg[:, pb, i, :]).then_inc(s_ds, 16)
                sync.wait_ge(s_ds, 64 * (mp + 1) + mp)
                sync.sem_inc(s_ds, 1)
            sync.wait_ge(s_ds, 64 * MPD + MPD)
            for n in range(NT):
                if n >= 1:
                    sync.wait_ge(s_xv, 2 * n)
                sync.dma_start(out=xt[:], in_=x3_in[n * 128:(n + 1) * 128, :]).then_inc(s_x2, 16)
                sync.dma_start(out=dnb[:], in_=dn_scr[:, n * 128:(n + 1) * 128],
                               transpose=True).then_inc(s_x2, 16)
                sync.wait_ge(s_xv, 2 * n + 2)
                sync.dma_start(out=x4_out[n * 128:(n + 1) * 128, :],
                               in_=x4t[:, n % 2, :]).then_inc(s_x2s, 16)
                sync.wait_ge(s_x2s, 16 * (n + 1))
                sync.sem_inc(s_x2q, 1)

        @block.gpsimd
        def _(gpsimd):
            for mp in range(MPH):
                if mp >= 1:
                    gpsimd.wait_ge(s_mm, 8 * mp)
                for k in range(KT):
                    gpsimd.dma_start(out=wA[:, k, :],
                                     in_=gwT_in[k * 128:(k + 1) * 128, mp * 256:(mp + 1) * 256]
                                     ).then_inc(s_w, 16)
                    gpsimd.dma_start(out=wB[:, k, :],
                                     in_=uwT_in[k * 128:(k + 1) * 128, mp * 256:(mp + 1) * 256]
                                     ).then_inc(s_w, 16)
            gpsimd.wait_ge(s_hq, MPH)
            for mp in range(MPD):
                if mp >= 1:
                    gpsimd.wait_ge(s_dv, 4 * mp)
                for k in range(KH):
                    gpsimd.dma_start(out=wC[:, k, :],
                                     in_=dwT_in[k * 128:(k + 1) * 128, mp * 256:(mp + 1) * 256]
                                     ).then_inc(s_w, 16)
                for k in range(KH):
                    if mp * KH + k >= 2:
                        gpsimd.wait_ge(s_mdk, mp * KH + k - 1)
                    gpsimd.dma_start(out=hbuf[:, k % 2, :],
                                     in_=h_scr[k * 128:(k + 1) * 128, :]
                                     ).then_inc([s_hl0, s_hl1][k % 2], 16)

        @block.tensor
        def _(tensor):
            tensor.wait_ge(s_tf1, 16 * KT)
            for mp in range(MPH):
                tensor.wait_ge(s_w, 16 * 32 * (mp + 1))
                if mp >= 1:
                    tensor.wait_ge(s_hv, 4 * mp)
                for i in range(4):
                    mm, j = divmod(i, 2)
                    for k in range(KT):
                        mg = tensor.matmul(ps[:, i, :], wA[:, k, mm * 128:(mm + 1) * 128],
                                           xfm[:, k, j * 512:(j + 1) * 512],
                                           start=(k == 0), stop=(k == KT - 1))
                    mg.then_inc(s_mm, 1)
                    for k in range(KT):
                        mu = tensor.matmul(ps[:, 4 + i, :], wB[:, k, mm * 128:(mm + 1) * 128],
                                           xfm[:, k, j * 512:(j + 1) * 512],
                                           start=(k == 0), stop=(k == KT - 1))
                    mu.then_inc(s_mm, 1)
            tensor.wait_ge(s_sg, 4 * MPH)
            tensor.wait_ge(s_hv, 4 * MPH)
            W0 = 16 * 32 * MPH
            for mp in range(MPD):
                tensor.wait_ge(s_w, W0 + 16 * KH * (mp + 1))
                if mp >= 1:
                    tensor.wait_ge(s_dv, 4 * mp)
                for k in range(KH):
                    tensor.wait_ge([s_hl0, s_hl1][k % 2], 16 * (mp * 22 + k // 2 + 1))
                    for i in range(4):
                        mm, j = divmod(i, 2)
                        md = tensor.matmul(ps[:, i, :], wC[:, k, mm * 128:(mm + 1) * 128],
                                           hbuf[:, k % 2, j * 512:(j + 1) * 512],
                                           start=(k == 0), stop=(k == KH - 1))
                    md.then_inc(s_mdk, 1)

        @block.vector
        def _(vector):
            vector.wait_ge(s_c, 16)
            for n in range(NT):
                vector.wait_ge(s_lx, 16 * (n + 1))
                if n >= 2:
                    vector.wait_ge(s_xq1, n - 1)
                ln_vector(vector, xt[:], n)
            for mp in range(MPH):
                pb = mp % 2
                for i in range(4):
                    vector.wait_ge(s_sg, 4 * mp + i + 1)
                    if mp >= 2:
                        vector.wait_ge(s_hq, mp - 1)
                    vector.tensor_tensor(out=hst[:, pb, i, :],
                                         in0=sgt[:], in1=ps[:, 4 + i, :], op=ALU.mult)
                    vector.drain()
                    vector.memset(bar[:, 0:1], 0.0).then_inc(s_hv, 1)
            for mp in range(MPD):
                pb = mp % 2
                for i in range(4):
                    vector.wait_ge(s_mdk, (mp + 1) * KH)
                    if mp >= 2:
                        vector.wait_ge(s_ds, 64 * (mp - 1) + (mp - 1))
                    vector.tensor_copy(out=evg[:, pb, i, :], in_=ps[:, i, :])
                    vector.drain()
                    vector.memset(bar[:, 0:1], 0.0).then_inc(s_dv, 1)
            for n in range(NT):
                vector.wait_ge(s_x2, 32 * n + 32)
                if n >= 2:
                    vector.wait_ge(s_x2q, n - 1)
                vector.tensor_copy(out=dn32[:], in_=dnb[:])
                vector.drain()
                vector.memset(bar[:, 0:1], 0.0).then_inc(s_xv, 1)
                vector.tensor_tensor(out=x4t[:, n % 2, :], in0=xt[:], in1=dn32[:],
                                     op=ALU.add)
                vector.drain()
                vector.memset(bar[:, 0:1], 0.0).then_inc(s_xv, 1)

        @block.scalar
        def _(scalar):
            scalar.wait_ge(s_c, 16)
            for n in range(NT):
                ln_scalar(scalar, n)
            for mp in range(MPH):
                for i in range(4):
                    scalar.wait_ge(s_mm, 8 * mp + 2 * i + 2)
                    if 4 * mp + i >= 1:
                        scalar.wait_ge(s_hv, 4 * mp + i)
                    scalar.activation(out=sgt[:], in_=ps[:, i, :], func=AF.Silu)
                    scalar.drain()
                    scalar.copy(out=sbar[0:1, 0:1], in_=cst[0:1, 0:1]).then_inc(s_sg, 1)

    return nc


def _padrows(a, rows):
    out = np.zeros((rows,) + a.shape[1:], a.dtype)
    out[:a.shape[0]] = a
    return out


STRIPE_BLOCKS = {c: [c, 15 - c, 16 + c, 31 - c] for c in range(NCORES)}


def _stripe_rows(c):
    rows = []
    for b in range(B):
        for blk in STRIPE_BLOCKS[c]:
            for p in range(128):
                rows.append(b * L + blk * 128 + p)
    return np.array(rows)


def run_L7a(inputs, x2, xn2, Kf, Vf):
    nc = build_L7a()
    K_full = np.ascontiguousarray(Kf.reshape(D, B * L))
    V_full = np.ascontiguousarray(Kf.reshape(D, B * L).T) if False else \
        np.ascontiguousarray(Vf.reshape(D, B * L).T)
    wqT = np.ascontiguousarray(inputs["L1_attn_w_q"].T)
    woT = np.ascontiguousarray(inputs["L1_attn_w_o"].T)
    ident = np.eye(128, dtype=BF)
    cols = np.arange(L)
    x2f = x2.reshape(B * L, D)
    xn2f = xn2.reshape(B * L, D)
    in_maps = []
    for c in range(NCORES):
        rows = _stripe_rows(c)
        qpos = np.concatenate([np.arange(blk * 128, (blk + 1) * 128)
                               for _ in range(B) for blk in [0]] * 0) if False else None
        qp = []
        for b in range(B):
            for blk in STRIPE_BLOCKS[c]:
                qp.append(np.arange(blk * 128, (blk + 1) * 128))
        qp = np.concatenate(qp)
        mask = np.where(cols[None, :] > qp[:, None], np.float32(-30000.0),
                        np.float32(0.0)).astype(BF)
        in_maps.append(dict(
            xn2s_in=np.ascontiguousarray(xn2f[rows]),
            x2s_in=np.ascontiguousarray(x2f[rows]),
            K_in=K_full, V_in=V_full, wqT_in=wqT, woT_in=woT,
            mask_in=np.ascontiguousarray(mask), ident_in=ident))
    res = run_bass_kernel_spmd(nc, in_maps, core_ids=list(range(NCORES)))
    return [r["x3s_out"] for r in res.results]


def run_L7b(inputs, x3s_list, gw, uw, dw):
    nc = build_L7b()
    gwT = np.zeros((D, HB2), BF)
    gwT[:, :HGHOST] = gw.T
    uwT = np.zeros((D, HB2), BF)
    uwT[:, :HGHOST] = uw.T
    dwT = np.zeros((HB2, D), BF)
    dwT[:HGHOST, :] = dw.T
    cstv = np.array([[EPS_LN, 1.0]], np.float32)
    in_maps = []
    for c in range(NCORES):
        in_maps.append(dict(x3_in=np.ascontiguousarray(x3s_list[c]),
                            gwT_in=gwT, uwT_in=uwT, dwT_in=dwT, cst_in=cstv))
    res = run_bass_kernel_spmd(nc, in_maps, core_ids=list(range(NCORES)))
    x4 = np.empty((B * L, D), np.float32)
    for c, r in enumerate(res.results):
        x4[_stripe_rows(c)] = r["x4_out"]
    return x4.reshape(B, L, D)


def kernel(**inputs):
    inputs = {k: np.asarray(v) for k, v in inputs.items()}
    ln0, xz = run_L1(inputs)
    xc, xdbl_parts, gparts = run_L2(inputs, xz)
    y = run_L3(inputs, xz, xc, xdbl_parts)
    x1, qw1, qw2, qw3 = run_L4(inputs, y, ln0, gparts)
    x2, xn2, Kf, Vf = run_L5(inputs, x1, qw1, qw2, qw3)
    gw, uw, dw = run_L6(inputs)
    x3s_list = run_L7a(inputs, x2, xn2, Kf, Vf)
    x4 = run_L7b(inputs, x3s_list, gw, uw, dw)
    return x4


# version 03
# speedup vs baseline: 1.0151x; 1.0151x over previous
import contextlib
import numpy as np
import ml_dtypes
import concourse.bass as bass
import concourse.mybir as mybir
from concourse.bass_utils import run_bass_kernel_spmd

F32 = mybir.dt.float32
BF16 = mybir.dt.bfloat16
I32 = mybir.dt.int32
AF = mybir.ActivationFunctionType
ALU = mybir.AluOpType
AX = mybir.AxisListType

NCORES = 8
B = 2
L = 4096
D = 2048
DIN = 4096
DST = 16
DTR = 128
HEADS = 16
HD = 128
HBIT = 5632
HGHOST = 5504
TC = 512
TOK = B * TC
EPS_LN = 1e-5
EPS_BIT = 1e-8

BF = ml_dtypes.bfloat16


def build_L1():
    nc = bass.Bass("TRN2", target_bir_lowering=False, debug=False)
    x_in = nc.dram_tensor("x_in", [TOK, D], F32, kind="ExternalInput").ap()
    nsc_in = nc.dram_tensor("nsc_in", [1, D], F32, kind="ExternalInput").ap()
    winT_in = nc.dram_tensor("winT_in", [D, 2 * DIN], F32, kind="ExternalInput").ap()
    cst_in = nc.dram_tensor("cst_in", [1, 2], F32, kind="ExternalInput").ap()

    ln0_out = nc.dram_tensor("ln0_out", [TOK, D], F32, kind="ExternalOutput").ap()
    xz_out = nc.dram_tensor("xz_out", [2 * DIN, TOK], F32, kind="ExternalOutput").ap()

    xn_scr = nc.dram_tensor("xn_scr", [TOK, D], BF16).ap()

    NT = TOK // 128
    KT = D // 128
    MT = (2 * DIN) // 128

    with (
        nc.sbuf_tensor("xt", [128, NT, D], F32) as xt,
        nc.sbuf_tensor("nsc", [128, D], F32) as nsc,
        nc.sbuf_tensor("cst", [128, 2], F32) as cst,
        nc.sbuf_tensor("sq", [128, D], F32) as sq,
        nc.sbuf_tensor("stat", [128, NT, 6], F32) as stat,
        nc.sbuf_tensor("lnt", [128, NT, D], F32) as lnt,
        nc.sbuf_tensor("xnt", [128, D], BF16) as xnt,
        nc.sbuf_tensor("xnf", [128, KT, TOK], BF16) as xnf,
        nc.sbuf_tensor("wb", [128, 2, KT, 128], BF16) as wb,
        nc.sbuf_tensor("ev", [128, 2, 512], F32) as ev,
        nc.psum_tensor("ps", [128, 2, 512], F32) as ps,
        nc.semaphore("s_ld") as s_ld,
        nc.semaphore("s_v") as s_v,
        nc.semaphore("s_s") as s_s,
        nc.semaphore("s_t") as s_t,
        nc.semaphore("s_wl") as s_wl,
        nc.semaphore("s_xf") as s_xf,
        nc.semaphore("s_st") as s_st,
        nc.Block() as block,
    ):
        @block.sync
        def _(sync):
            sync.dma_start(out=xt[:], in_=x_in.rearrange("(n p) d -> p n d", p=128)).then_inc(s_ld, 16)
            sync.dma_start(out=nsc[:], in_=nsc_in.partition_broadcast(128)).then_inc(s_ld, 16)
            sync.dma_start(out=cst[:], in_=cst_in.partition_broadcast(128)).then_inc(s_ld, 16)
            sync.wait_ge(s_st, 16 * NT)
            for k in range(KT):
                sync.dma_start(out=xnf[:, k, :], in_=xn_scr[:, k * 128:(k + 1) * 128],
                               transpose=True).then_inc(s_xf, 16)

        @block.vector
        def _(vector):
            vector.wait_ge(s_ld, 48)
            for n in range(NT):
                vector.reduce_sum(out=stat[:, n, 0:1], in_=xt[:, n, :], axis=AX.X)
                vector.drain()
                vector.tensor_scalar(out=stat[:, n, 1:2], in0=stat[:, n, 0:1],
                                     scalar1=-1.0 / D, scalar2=None, op0=ALU.mult, op1=ALU.bypass)
                vector.drain()
                vector.tensor_scalar(out=lnt[:, n, :], in0=xt[:, n, :],
                                     scalar1=stat[:, n, 1:2], scalar2=None,
                                     op0=ALU.add, op1=ALU.bypass).then_inc(s_v, 1)
                vector.wait_ge(s_s, 2 * n + 1)
                vector.reciprocal(out=stat[:, n, 3:4], in_=stat[:, n, 2:3])
                vector.drain()
                vector.tensor_scalar(out=lnt[:, n, :], in0=lnt[:, n, :],
                                     scalar1=stat[:, n, 3:4], scalar2=None,
                                     op0=ALU.mult, op1=ALU.bypass).then_inc(s_v, 1)
                vector.wait_ge(s_s, 2 * n + 2)
                vector.reciprocal(out=stat[:, n, 5:6], in_=stat[:, n, 4:5])
                vector.drain()
                vector.tensor_scalar(out=sq[:], in0=lnt[:, n, :],
                                     scalar1=stat[:, n, 5:6], scalar2=None,
                                     op0=ALU.mult, op1=ALU.bypass)
                vector.drain()
                vector.tensor_tensor(out=xnt[:], in0=sq[:], in1=nsc[:],
                                     op=ALU.mult).then_inc(s_v, 1)
                if n < NT - 1:
                    vector.wait_ge(s_st, 16 * (n + 1))
            for m in range(MT):
                for j in range(2):
                    i = m * 2 + j
                    vector.wait_ge(s_t, i + 1)
                    if i >= 2:
                        vector.wait_ge(s_st, 16 * (NT + i))
                    vector.tensor_copy(out=ev[:, i % 2, :], in_=ps[:, j, :]).then_inc(s_v, 1)

        @block.scalar
        def _(scalar):
            scalar.wait_ge(s_ld, 48)
            for n in range(NT):
                scalar.wait_ge(s_v, 3 * n + 1)
                scalar.activation(out=sq[:], in_=lnt[:, n, :], func=AF.Square,
                                  accum_out=stat[:, n, 0:1])
                scalar.drain()
                scalar.activation(out=stat[:, n, 2:3], in_=stat[:, n, 0:1], func=AF.Sqrt,
                                  scale=1.0 / D, bias=cst[:, 0:1]).then_inc(s_s, 1)
                scalar.wait_ge(s_v, 3 * n + 2)
                scalar.activation(out=sq[:], in_=lnt[:, n, :], func=AF.Square,
                                  accum_out=stat[:, n, 0:1])
                scalar.drain()
                scalar.activation(out=stat[:, n, 4:5], in_=stat[:, n, 0:1], func=AF.Sqrt,
                                  scale=1.0 / D, bias=cst[:, 0:1]).then_inc(s_s, 1)

        @block.gpsimd
        def _(gpsimd):
            for n in range(NT):
                gpsimd.wait_ge(s_v, 3 * n + 3)
                gpsimd.dma_start(out=xn_scr[n * 128:(n + 1) * 128, :], in_=xnt[:]).then_inc(s_st, 16)
            for m in range(MT):
                pb = m % 2
                if m >= 2:
                    gpsimd.wait_ge(s_t, (m - 1) * 2)
                for k in range(KT):
                    gpsimd.dma_start(out=wb[:, pb, k, :],
                                     in_=winT_in[k * 128:(k + 1) * 128, m * 128:(m + 1) * 128]
                                     ).then_inc(s_wl, 16)

        @block.tensor
        def _(tensor):
            tensor.wait_ge(s_xf, 16 * KT)
            for m in range(MT):
                pb = m % 2
                tensor.wait_ge(s_wl, 16 * KT * (m + 1))
                for j in range(2):
                    if m >= 1:
                        tensor.wait_ge(s_v, 3 * NT + (m - 1) * 2 + j + 1)
                    for k in range(KT):
                        mm = tensor.matmul(ps[:, j, :], wb[:, pb, k, :],
                                           xnf[:, k, j * 512:(j + 1) * 512],
                                           start=(k == 0), stop=(k == KT - 1))
                    mm.then_inc(s_t, 1)

        @block.sync
        def _(sync):
            sync.wait_ge(s_v, 3 * NT)
            sync.dma_start(out=ln0_out.rearrange("(n p) d -> p n d", p=128), in_=lnt[:]).then_inc(s_st, 16)
            for m in range(MT):
                for j in range(2):
                    sync.wait_ge(s_v, 3 * NT + m * 2 + j + 1)
                    sync.dma_start(out=xz_out[m * 128:(m + 1) * 128, j * 512:(j + 1) * 512],
                                   in_=ev[:, (m * 2 + j) % 2, :]).then_inc(s_st, 16)
            sync.wait_ge(s_st, 16 * (NT + 1 + MT * 2))
    return nc


def run_L1(inputs):
    x = inputs["L0_attn_hidden_states"]
    nsc = np.ascontiguousarray(inputs["L0_attn_norm_scale"].reshape(1, D))
    winT = np.ascontiguousarray(inputs["L0_attn_in_proj_w"].T)
    nc = build_L1()
    in_maps = []
    for c in range(NCORES):
        xc = np.ascontiguousarray(
            x[:, c * TC:(c + 1) * TC, :].reshape(TOK, D))
        in_maps.append(dict(x_in=xc, nsc_in=nsc, winT_in=winT,
                            cst_in=np.array([[EPS_LN, 1.0]], np.float32)))
    res = run_bass_kernel_spmd(nc, in_maps, core_ids=list(range(NCORES)))
    ln0 = np.empty((B, L, D), np.float32)
    xz = np.empty((2 * DIN, B, L), np.float32)
    for c, r in enumerate(res.results):
        ln0[:, c * TC:(c + 1) * TC, :] = r["ln0_out"].reshape(B, TC, D)
        xz[:, :, c * TC:(c + 1) * TC] = r["xz_out"].reshape(2 * DIN, B, TC)
    return ln0, xz


LANES = 1024
NLG = LANES // 128


def build_L2():
    nc = bass.Bass("TRN2", target_bir_lowering=False, debug=False)
    xin_in = nc.dram_tensor("xin_in", [LANES, L], F32, kind="ExternalInput").ap()
    cw_in = nc.dram_tensor("cw_in", [128, NLG, 4], F32, kind="ExternalInput").ap()
    cb_in = nc.dram_tensor("cb_in", [128, NLG], F32, kind="ExternalInput").ap()
    xpwT_in = nc.dram_tensor("xpwT_in", [512, 160], F32, kind="ExternalInput").ap()
    w1_in = nc.dram_tensor("w1_in", [768, D], F32, kind="ExternalInput").ap()
    w2_in = nc.dram_tensor("w2_in", [768, D], F32, kind="ExternalInput").ap()
    w3s_in = nc.dram_tensor("w3s_in", [256, HBIT], F32, kind="ExternalInput").ap()

    xc_out = nc.dram_tensor("xc_out", [LANES, L], F32, kind="ExternalOutput").ap()
    xdbl_out = nc.dram_tensor("xdbl_out", [B, 160, L], F32, kind="ExternalOutput").ap()
    gp_out = nc.dram_tensor("gp_out", [1, 3], F32, kind="ExternalOutput").ap()

    NB = L // 512

    with contextlib.ExitStack() as ctx:
        sb = lambda *a: ctx.enter_context(nc.sbuf_tensor(*a))
        pt = lambda *a: ctx.enter_context(nc.psum_tensor(*a))
        sem = lambda n: ctx.enter_context(nc.semaphore(n))
        xpad = sb("xpad", [128, 2, 3 + L], F32)
        acc = sb("acc", [128, 2, L], F32)
        xcf = sb("xcf", [128, 2, L], F32)
        xcb = sb("xcb", [128, NLG, L], BF16)
        cw = sb("cw", [128, NLG, 4], F32)
        cb = sb("cb", [128, NLG], F32)
        xpw = sb("xpw", [128, 4, 160], BF16)
        evA = sb("evA", [128, 2, 512], F32)
        evB = sb("evB", [32, 2, 512], F32)
        gw = sb("gw", [128, 2, HBIT], BF16)
        gtmp = sb("gtmp", [128, 16], F32)
        gcol = sb("gcol", [128, 3], F32)
        ones = sb("ones", [128, 1], F32)
        gp = sb("gp", [1, 3], F32)
        psA = pt("psA", [128, 2, 512], F32)
        psB = pt("psB", [32, 2, 512], F32)
        psg = pt("psg", [1, 3], F32)
        s_ld = sem("s_ld"); s_c = sem("s_c"); s_v = sem("s_v")
        s_sl = sem("s_sl"); s_xs = sem("s_xs"); s_mm = sem("s_mm")
        s_gl = sem("s_gl"); s_gv = sem("s_gv"); s_st = sem("s_st")
        block = ctx.enter_context(nc.Block())
        @block.sync
        def _(sync):
            sync.dma_start(out=cw[:], in_=cw_in[:]).then_inc(s_c, 16)
            sync.dma_start(out=cb[:], in_=cb_in[:]).then_inc(s_c, 16)
            for g in range(NLG):
                pb = g % 2
                if g >= 2:
                    sync.wait_ge(s_v, 2 * (g - 2) + 1)
                sync.dma_start(out=xpad[:, pb, 3:3 + L],
                               in_=xin_in[g * 128:(g + 1) * 128, :]).then_inc(s_ld, 16)

        @block.gpsimd
        def _(gpsimd):
            for k in range(4):
                gpsimd.dma_start(out=xpw[:, k, :],
                                 in_=xpwT_in[k * 128:(k + 1) * 128, :]).then_inc(s_c, 16)
            for g in range(NLG):
                gpsimd.wait_ge(s_sl, g + 1)
                gpsimd.dma_start(out=xc_out[g * 128:(g + 1) * 128, :],
                                 in_=xcf[:, g % 2, :]).then_inc(s_xs, 16)
            gsrcs = ([w1_in[i * 128:(i + 1) * 128, :] for i in range(6)]
                     + [w2_in[i * 128:(i + 1) * 128, :] for i in range(6)]
                     + [w3s_in[i * 128:(i + 1) * 128, :] for i in range(2)])
            for i, srcap in enumerate(gsrcs):
                if i >= 2:
                    gpsimd.wait_ge(s_gv, i - 1)
                wid = srcap.shape[1]
                gpsimd.dma_start(out=gw[:, i % 2, 0:wid], in_=srcap).then_inc(s_gl, 16)

        @block.vector
        def _(vector):
            vector.wait_ge(s_c, 32)
            vector.memset(ones[:], 1.0)
            for g in range(NLG):
                pb = g % 2
                vector.memset(xpad[:, pb, 0:3], 0.0)
                vector.wait_ge(s_ld, 16 * (g + 1))
                vector.tensor_scalar(out=acc[:, 0, :], in0=xpad[:, pb, 0:L],
                                     scalar1=cw[:, g, 0:1], scalar2=None,
                                     op0=ALU.mult, op1=ALU.bypass)
                vector.drain()
                vector.scalar_tensor_tensor(out=acc[:, 1, :], in0=xpad[:, pb, 1:1 + L],
                                            scalar=cw[:, g, 1:2], in1=acc[:, 0, :],
                                            op0=ALU.mult, op1=ALU.add)
                vector.drain()
                vector.scalar_tensor_tensor(out=acc[:, 0, :], in0=xpad[:, pb, 2:2 + L],
                                            scalar=cw[:, g, 2:3], in1=acc[:, 1, :],
                                            op0=ALU.mult, op1=ALU.add)
                vector.drain()
                vector.scalar_tensor_tensor(out=acc[:, 1, :], in0=xpad[:, pb, 3:3 + L],
                                            scalar=cw[:, g, 3:4], in1=acc[:, 0, :],
                                            op0=ALU.mult, op1=ALU.add).then_inc(s_v, 1)
                vector.wait_ge(s_sl, g + 1)
                vector.tensor_copy(out=xcb[:, g, :], in_=xcf[:, pb, :]).then_inc(s_v, 1)
            nev = 0
            for i in range(B * NB):
                vector.wait_ge(s_mm, 2 * i + 2)
                pb = i % 2
                if i >= 2:
                    vector.wait_ge(s_st, 16 * (2 * (i - 2) + 2))
                vector.tensor_copy(out=evA[:, pb, :], in_=psA[:, pb, :]).then_inc(s_v, 1)
                vector.tensor_copy(out=evB[:, pb, :], in_=psB[:, pb, :]).then_inc(s_v, 1)
                nev += 2
            for i in range(14):
                vector.wait_ge(s_gl, 16 * (i + 1))
                wid = HBIT if i >= 12 else D
                col = i if i < 12 else 12 + (i - 12)
                vector.tensor_reduce(out=gtmp[:, col:col + 1], in_=gw[:, i % 2, 0:wid],
                                     axis=AX.X, op=ALU.add,
                                     apply_absolute_value=True).then_inc(s_gv, 1)
            vector.drain()
            vector.tensor_reduce(out=gcol[:, 0:1], in_=gtmp[:, 0:6], axis=AX.X, op=ALU.add)
            vector.tensor_reduce(out=gcol[:, 1:2], in_=gtmp[:, 6:12], axis=AX.X, op=ALU.add)
            vector.tensor_reduce(out=gcol[:, 2:3], in_=gtmp[:, 12:14], axis=AX.X,
                                 op=ALU.add).then_inc(s_gv, 1)
            vector.wait_ge(s_mm, 2 * B * NB + 1)
            vector.tensor_copy(out=gp[:], in_=psg[:]).then_inc(s_gv, 1)

        @block.scalar
        def _(scalar):
            scalar.wait_ge(s_c, 32)
            for g in range(NLG):
                pb = g % 2
                scalar.wait_ge(s_v, 2 * g + 1)
                if g >= 2:
                    scalar.wait_ge(s_xs, 16 * (g - 1))
                scalar.activation(out=xcf[:, pb, :], in_=acc[:, 1, :], func=AF.Silu,
                                  bias=cb[:, g:g + 1]).then_inc(s_sl, 1)

        @block.tensor
        def _(tensor):
            tensor.wait_ge(s_c, 16 * 6)
            tensor.wait_ge(s_v, 2 * NLG)
            for i in range(B * NB):
                b, n = divmod(i, NB)
                pb = i % 2
                if i >= 2:
                    tensor.wait_ge(s_v, 2 * NLG + 2 * (i - 2) + 2)
                for k in range(4):
                    mmA = tensor.matmul(psA[:, pb, :], xpw[:, k, 0:128],
                                        xcb[:, b * 4 + k, n * 512:(n + 1) * 512],
                                        start=(k == 0), stop=(k == 3))
                mmA.then_inc(s_mm, 1)
                for k in range(4):
                    mmB = tensor.matmul(psB[:, pb, :], xpw[:, k, 128:160],
                                        xcb[:, b * 4 + k, n * 512:(n + 1) * 512],
                                        start=(k == 0), stop=(k == 3))
                mmB.then_inc(s_mm, 1)
            tensor.wait_ge(s_gv, 15)
            for j in range(3):
                mg = tensor.matmul(psg[0:1, j:j + 1], ones[:], gcol[:, j:j + 1],
                                   start=True, stop=True)
            mg.then_inc(s_mm, 1)

        @block.sync
        def _(sync):
            for i in range(B * NB):
                b, n = divmod(i, NB)
                pb = i % 2
                sync.wait_ge(s_v, 2 * NLG + 2 * i + 1)
                sync.dma_start(out=xdbl_out[b, 0:128, n * 512:(n + 1) * 512],
                               in_=evA[:, pb, :]).then_inc(s_st, 16)
                sync.wait_ge(s_v, 2 * NLG + 2 * i + 2)
                sync.dma_start(out=xdbl_out[b, 128:160, n * 512:(n + 1) * 512],
                               in_=evB[:, pb, :]).then_inc(s_st, 16)
            sync.wait_ge(s_gv, 16)
            sync.dma_start(out=gp_out[:], in_=gp[:]).then_inc(s_st, 16)
            sync.wait_ge(s_st, 16 * (2 * B * NB + 1))
            sync.wait_ge(s_xs, 16 * NLG)

    return nc


def run_L2(inputs, xz):
    cwf = inputs["L0_attn_conv_w"][:, 0, :]
    nc = build_L2()
    in_maps = []
    xpwT = np.ascontiguousarray(inputs["L0_attn_x_proj_w"].T)
    w1 = inputs["L0_ffn_w1"]
    w2 = inputs["L0_ffn_w2"]
    w3 = inputs["L0_ffn_w3"]
    pad = np.zeros((64, D), np.float32)
    for c in range(NCORES):
        ch = slice(c * 512, (c + 1) * 512)
        xin = np.ascontiguousarray(xz[ch, :, :].transpose(1, 0, 2).reshape(LANES, L))
        cw_c = np.ascontiguousarray(
            np.concatenate([cwf[ch]] * B, 0).reshape(NLG, 128, 4).transpose(1, 0, 2))
        cb_c = np.ascontiguousarray(
            np.concatenate([inputs["L0_attn_conv_b"][ch]] * B).reshape(NLG, 128).T)
        w1_c = np.concatenate([w1[c * 704:(c + 1) * 704], pad], 0)
        w2_c = np.concatenate([w2[c * 704:(c + 1) * 704], pad], 0)
        w3_c = np.ascontiguousarray(w3[c * 256:(c + 1) * 256])
        in_maps.append(dict(xin_in=xin, cw_in=cw_c, cb_in=cb_c,
                            xpwT_in=np.ascontiguousarray(xpwT[ch]),
                            w1_in=np.ascontiguousarray(w1_c),
                            w2_in=np.ascontiguousarray(w2_c), w3s_in=w3_c))
    res = run_bass_kernel_spmd(nc, in_maps, core_ids=list(range(NCORES)))
    xc = np.empty((B, DIN, L), np.float32)
    xdbl_parts = np.empty((NCORES, B, 160, L), np.float32)
    gparts = np.empty((NCORES, 1, 3), np.float32)
    for c, r in enumerate(res.results):
        xc[:, c * 512:(c + 1) * 512, :] = r["xc_out"].reshape(B, 512, L)
        xdbl_parts[c] = r["xdbl_out"]
        gparts[c] = r["gp_out"]
    return xc, xdbl_parts, gparts


TCH = 128
NTC = L // TCH


def build_L3():
    nc = bass.Bass("TRN2", target_bir_lowering=False, debug=False)
    xc_in = nc.dram_tensor("xc_in", [LANES, L], F32, kind="ExternalInput").ap()
    z_in = nc.dram_tensor("z_in", [LANES, L], F32, kind="ExternalInput").ap()
    xdblp_in = nc.dram_tensor("xdblp_in", [NCORES, B, 160, L], F32, kind="ExternalInput").ap()
    dtwT_in = nc.dram_tensor("dtwT_in", [128, 512], F32, kind="ExternalInput").ap()
    dtb_in = nc.dram_tensor("dtb_in", [128, 4], F32, kind="ExternalInput").ap()
    A_in = nc.dram_tensor("A_in", [128, 4, DST], F32, kind="ExternalInput").ap()
    D_in = nc.dram_tensor("D_in", [128, 4], F32, kind="ExternalInput").ap()
    cst_in = nc.dram_tensor("cst_in", [1, 2], F32, kind="ExternalInput").ap()

    y_out = nc.dram_tensor("y_out", [LANES, L], F32, kind="ExternalOutput").ap()
    bc_scr = nc.dram_tensor("bc_scr", [32, B, L], BF16).ap()

    NG = 4
    NB = L // 512
    HL = L // 2

    with contextlib.ExitStack() as ctx:
        sb = lambda *a: ctx.enter_context(nc.sbuf_tensor(*a))
        pt = lambda *a: ctx.enter_context(nc.psum_tensor(*a))
        sem = lambda n: ctx.enter_context(nc.semaphore(n))
        pload = sb("pload", [128, 2, HL], F32)
        bcload = sb("bcload", [32, 2, HL], F32)
        dtl = sb("dtl", [128, L], F32)
        bca = sb("bca", [32, L], F32)
        dtlb = sb("dtlb", [128, L], BF16)
        dtw = sb("dtw", [128, 512], BF16)
        dtb = sb("dtb", [128, 4], F32)
        negA = sb("negA", [128, NG, DST], F32)
        Dt = sb("Dt", [128, NG], F32)
        cst = sb("cst", [128, 2], F32)
        delta = sb("delta", [128, NG, L], BF16)
        xcb = sb("xcb", [128, NG, L], BF16)
        etile = sb("etile", [128, 1, 512], F32)
        Bbc = sb("Bbc", [128, DST, TCH], F32)
        Cbc = sb("Cbc", [128, DST, TCH], F32)
        dA = sb("dA", [128, DST, TCH], F32)
        bArg = sb("bArg", [128, DST, TCH], BF16)
        du = sb("du", [128, TCH], BF16)
        hsb = sb("hsb", [128, NG, DST, TCH], F32)
        hc = sb("hc", [128, DST, TCH], BF16)
        yt = sb("yt", [128, TCH], F32)
        y2 = sb("y2", [128, TCH], F32)
        y3 = sb("y3", [128, 2, TCH], F32)
        zt = sb("zt", [128, 2, TCH], F32)
        sz = sb("sz", [128, TCH], F32)
        sbar = sb("sbar", [1, 2], F32)
        ps = pt("ps", [128, 2, 512], F32)
        s_c = sem("s_c")
        s_p = sem("s_p")
        s_q = sem("s_q")
        s_va = sem("s_va")
        s_vb = sem("s_vb")
        s_dl = sem("s_dl")
        s_bs = sem("s_bs")
        s_mm = sem("s_mm")
        s_sp = sem("s_sp")
        s_xc = sem("s_xc")
        s_bc = sem("s_bc")
        s_z = sem("s_z")
        s_da = sem("s_da")
        s_sl = sem("s_sl")
        s_vy = sem("s_vy")
        s_ys = sem("s_ys")
        s_neg = sem("s_neg")
        block = ctx.enter_context(nc.Block())

        NIT = NTC * NG

        @block.sync
        def _(sync):
            for b in range(B):
                for j in range(NCORES):
                    for h in range(2):
                        k = j * 2 + h
                        pb = k % 2
                        if k >= 2:
                            sync.wait_ge(s_va, b * 16 + k - 1)
                        sync.dma_start(out=pload[:, pb, :],
                                       in_=xdblp_in[j, b, 0:128, h * HL:(h + 1) * HL]
                                       ).then_inc(s_p, 16)
                        if k >= 2:
                            sync.wait_ge(s_vb, b * 17 + k - 1)
                        sync.dma_start(out=bcload[:, pb, :],
                                       in_=xdblp_in[j, b, 128:160, h * HL:(h + 1) * HL]
                                       ).then_inc(s_q, 16)
                sync.wait_ge(s_bs, 16 * (b + 1))
                for tc in range(NTC):
                    jj = b * NTC + tc
                    for g in range(NG):
                        it = jj * NG + g
                        if it >= 2:
                            sync.wait_ge(s_sl, it - 1)
                        sync.dma_start(out=zt[:, it % 2, :],
                                       in_=z_in[(b * NG + g) * 128:(b * NG + g + 1) * 128,
                                                tc * TCH:(tc + 1) * TCH]).then_inc(s_z, 16)

        @block.gpsimd
        def _(gpsimd):
            gpsimd.dma_start(out=dtw[:], in_=dtwT_in[:]).then_inc(s_c, 16)
            gpsimd.dma_start(out=dtb[:], in_=dtb_in[:]).then_inc(s_c, 16)
            gpsimd.dma_start(out=negA[:], in_=A_in[:]).then_inc(s_c, 16)
            gpsimd.dma_start(out=Dt[:], in_=D_in[:]).then_inc(s_c, 16)
            gpsimd.dma_start(out=cst[:], in_=cst_in.partition_broadcast(128)).then_inc(s_c, 16)
            for b in range(B):
                gpsimd.wait_ge(s_vb, b * 17 + 17)
                gpsimd.dma_start(out=bc_scr[:, b, :], in_=bca[:]).then_inc(s_bs, 16)
                if b > 0:
                    gpsimd.wait_ge(s_vy, b * NIT)
                for g in range(NG):
                    gpsimd.dma_start(out=xcb[:, g, :],
                                     in_=xc_in[(b * NG + g) * 128:(b * NG + g + 1) * 128, :]
                                     ).then_inc(s_xc, 16)
                for it0 in range(NIT):
                    it = b * NIT + it0
                    tc, g = divmod(it0, NG)
                    if g == 0:
                        jj = b * NTC + tc
                        if jj > 0:
                            gpsimd.wait_ge(s_vy, jj * NG)
                        for s in range(DST):
                            gpsimd.dma_start(
                                out=Bbc[:, s, :],
                                in_=bc_scr[s:s + 1, b, tc * TCH:(tc + 1) * TCH].partition_broadcast(128)
                            ).then_inc(s_bc, 16)
                            gpsimd.dma_start(
                                out=Cbc[:, s, :],
                                in_=bc_scr[DST + s:DST + s + 1, b, tc * TCH:(tc + 1) * TCH].partition_broadcast(128)
                            ).then_inc(s_bc, 16)
                    gpsimd.wait_ge(s_vy, it + 1)
                    gpsimd.dma_start(out=y_out[(b * NG + g) * 128:(b * NG + g + 1) * 128,
                                               tc * TCH:(tc + 1) * TCH],
                                     in_=y3[:, it % 2, :]).then_inc(s_ys, 16)

        @block.vector
        def _(vector):
            vector.wait_ge(s_c, 80)
            vector.wait_ge(s_da, 1)
            vector.tensor_scalar(out=negA[:], in0=negA[:], scalar1=-1.0, scalar2=None,
                                 op0=ALU.mult, op1=ALU.bypass)
            vector.drain()
            vector.memset(yt[:, 0:1], 0.0).then_inc(s_neg, 1)
            for b in range(B):
                if b > 0:
                    vector.wait_ge(s_bs, 16 * b)
                for j in range(NCORES):
                    for h in range(2):
                        k = j * 2 + h
                        pb = k % 2
                        hsl = slice(h * HL, (h + 1) * HL)
                        vector.wait_ge(s_p, 16 * (b * 16 + k) + 16)
                        if j == 0:
                            vector.tensor_copy(out=dtl[:, hsl], in_=pload[:, pb, :]).then_inc(s_va, 1)
                        else:
                            vector.tensor_tensor(out=dtl[:, hsl], in0=dtl[:, hsl],
                                                 in1=pload[:, pb, :], op=ALU.add).then_inc(s_va, 1)
                        vector.wait_ge(s_q, 16 * (b * 16 + k) + 16)
                        if j == 0:
                            vector.tensor_copy(out=bca[:, hsl], in_=bcload[:, pb, :]).then_inc(s_vb, 1)
                        else:
                            vector.tensor_tensor(out=bca[:, hsl], in0=bca[:, hsl],
                                                 in1=bcload[:, pb, :], op=ALU.add).then_inc(s_vb, 1)
                        if j == NCORES - 1 and h == 1:
                            vector.drain()
                            vector.memset(yt[:, 0:1], 0.0).then_inc(s_vb, 1)
                vector.drain()
                vector.tensor_copy(out=dtlb[:], in_=dtl[:])
                vector.drain()
                vector.memset(yt[:, 0:1], 0.0).then_inc(s_dl, 1)
                for it0 in range(NIT):
                    it = b * NIT + it0
                    tc, g = divmod(it0, NG)
                    tsl = slice(tc * TCH, (tc + 1) * TCH)
                    if it0 == 0:
                        vector.wait_ge(s_xc, 16 * (b + 1) * NG)
                        vector.memset(hsb[:], 0.0)
                        vector.drain()
                    vector.wait_ge(s_sp, b * 32 + 32)
                    vector.tensor_tensor(out=du[:], in0=delta[:, g, tsl], in1=xcb[:, g, tsl],
                                         op=ALU.mult)
                    vector.drain()
                    vector.wait_ge(s_bc, 32 * 16 * (b * NTC + tc) + 32 * 16)
                    for s in range(DST):
                        vector.tensor_tensor(out=bArg[:, s, :], in0=du[:], in1=Bbc[:, s, :],
                                             op=ALU.mult)
                    vector.drain()
                    vector.wait_ge(s_da, it + 2)
                    for s in range(DST):
                        vector.tensor_tensor_scan(out=hsb[:, g, s, :], data0=dA[:, s, :],
                                                  data1=bArg[:, s, :],
                                                  initial=hsb[:, g, s, TCH - 1:TCH],
                                                  op0=ALU.mult, op1=ALU.add)
                    vector.drain()
                    vector.tensor_tensor(out=hc[:], in0=hsb[:, g, :, :], in1=Cbc[:],
                                         op=ALU.mult)
                    vector.drain()
                    vector.tensor_tensor(out=dA[:, 8:16, :], in0=hc[:, 0:8, :],
                                         in1=hc[:, 8:16, :], op=ALU.add)
                    vector.drain()
                    vector.tensor_tensor(out=dA[:, 0:4, :], in0=dA[:, 8:12, :],
                                         in1=dA[:, 12:16, :], op=ALU.add)
                    vector.drain()
                    vector.tensor_tensor(out=dA[:, 4:6, :], in0=dA[:, 0:2, :],
                                         in1=dA[:, 2:4, :], op=ALU.add)
                    vector.drain()
                    vector.tensor_tensor(out=yt[:], in0=dA[:, 4, :], in1=dA[:, 5, :],
                                         op=ALU.add)
                    vector.drain()
                    vector.scalar_tensor_tensor(out=y2[:], in0=xcb[:, g, tsl],
                                                scalar=Dt[:, g:g + 1], in1=yt[:],
                                                op0=ALU.mult, op1=ALU.add)
                    vector.drain()
                    vector.wait_ge(s_sl, it + 1)
                    if it >= 2:
                        vector.wait_ge(s_ys, 16 * (it - 1))
                    vector.tensor_tensor(out=y3[:, it % 2, :], in0=y2[:], in1=sz[:],
                                         op=ALU.mult)
                    vector.drain()
                    vector.memset(yt[:, 0:1], 0.0).then_inc(s_vy, 1)

        @block.scalar
        def _(scalar):
            scalar.wait_ge(s_c, 80)
            scalar.activation(out=negA[:], in_=negA[:], func=AF.Exp).then_inc(s_da, 1)
            for b in range(B):
                for g in range(NG):
                    for nb2 in range(NB):
                        idx = b * 32 + g * NB + nb2
                        pb = idx % 2
                        scalar.wait_ge(s_mm, idx + 1)
                        scalar.activation(out=etile[:, 0, :], in_=ps[:, pb, :], func=AF.Exp,
                                          bias=dtb[:, g:g + 1])
                        scalar.drain()
                        scalar.activation(out=delta[:, g, nb2 * 512:(nb2 + 1) * 512],
                                          in_=etile[:, 0, :], func=AF.Ln,
                                          bias=cst[:, 1:2])
                        scalar.drain()
                        scalar.copy(out=sbar[0:1, 0:1], in_=sbar[0:1, 1:2]).then_inc(s_sp, 1)
                for it0 in range(NIT):
                    it = b * NIT + it0
                    tc, g = divmod(it0, NG)
                    tsl = slice(tc * TCH, (tc + 1) * TCH)
                    if it >= 1:
                        scalar.wait_ge(s_vy, it)
                    if it == 0:
                        scalar.wait_ge(s_neg, 1)
                    for s in range(DST):
                        scalar.activation(out=dA[:, s, :], in_=delta[:, g, tsl],
                                          func=AF.Exp, scale=negA[:, g, s:s + 1])
                    scalar.drain()
                    scalar.copy(out=sbar[0:1, 0:1], in_=sbar[0:1, 1:2]).then_inc(s_da, 1)
                    scalar.wait_ge(s_z, 16 * (it + 1))
                    scalar.activation(out=sz[:], in_=zt[:, it % 2, :], func=AF.Silu)
                    scalar.drain()
                    scalar.copy(out=sbar[0:1, 0:1], in_=sbar[0:1, 1:2]).then_inc(s_sl, 1)

        @block.tensor
        def _(tensor):
            tensor.wait_ge(s_c, 80)
            for b in range(B):
                tensor.wait_ge(s_dl, b + 1)
                if b > 0:
                    tensor.wait_ge(s_vy, b * NIT)
                for g in range(NG):
                    for nb2 in range(NB):
                        idx = b * 32 + g * NB + nb2
                        pb = idx % 2
                        if idx >= 2:
                            tensor.wait_ge(s_sp, idx - 1)
                        tensor.matmul(ps[:, pb, :], dtw[:, g * 128:(g + 1) * 128],
                                      dtlb[:, nb2 * 512:(nb2 + 1) * 512],
                                      start=True, stop=True).then_inc(s_mm, 1)

        @block.sync
        def _(sync):
            sync.wait_ge(s_ys, 16 * B * NIT)

    return nc


def run_L3(inputs, xz, xc, xdbl_parts):
    nc = build_L3()
    in_maps = []
    dtw = inputs["L0_attn_dt_proj_w"]
    dtbv = inputs["L0_attn_dt_proj_b"]
    Alog = inputs["L0_attn_A_log"]
    Dv = inputs["L0_attn_D"]
    cstv = np.array([[EPS_LN, 1.0]], np.float32)
    for c in range(NCORES):
        ch = slice(c * 512, (c + 1) * 512)
        xc_c = np.ascontiguousarray(xc[:, ch, :].reshape(B * 512, L))
        z_c = np.ascontiguousarray(xz[DIN + c * 512:DIN + (c + 1) * 512, :, :]
                                   .transpose(1, 0, 2).reshape(LANES, L))
        dtwT_c = np.ascontiguousarray(dtw[ch].T)
        dtb_c = np.ascontiguousarray(dtbv[ch].reshape(4, 128).T)
        A_c = np.ascontiguousarray(Alog[ch].reshape(4, 128, DST).transpose(1, 0, 2))
        D_c = np.ascontiguousarray(Dv[ch].reshape(4, 128).T)
        in_maps.append(dict(xc_in=xc_c, z_in=z_c, xdblp_in=xdbl_parts,
                            dtwT_in=dtwT_c, dtb_in=dtb_c, A_in=A_c, D_in=D_c,
                            cst_in=cstv))
    res = run_bass_kernel_spmd(nc, in_maps, core_ids=list(range(NCORES)))
    y = np.empty((B, DIN, L), np.float32)
    for c, r in enumerate(res.results):
        y[:, c * 512:(c + 1) * 512, :] = r["y_out"].reshape(B, 512, L)
    return y


def build_L4():
    nc = bass.Bass("TRN2", target_bir_lowering=False, debug=False)
    yfm_in = nc.dram_tensor("yfm_in", [DIN, TOK], F32, kind="ExternalInput").ap()
    woT_in = nc.dram_tensor("woT_in", [DIN, D], F32, kind="ExternalInput").ap()
    x_in = nc.dram_tensor("x_in", [TOK, D], F32, kind="ExternalInput").ap()
    ln0_in = nc.dram_tensor("ln0_in", [TOK, D], F32, kind="ExternalInput").ap()
    gparts_in = nc.dram_tensor("gparts_in", [1, 3 * NCORES], F32, kind="ExternalInput").ap()
    w1s_in = nc.dram_tensor("w1s_in", [768, D], F32, kind="ExternalInput").ap()
    w2s_in = nc.dram_tensor("w2s_in", [768, D], F32, kind="ExternalInput").ap()
    w3s_in = nc.dram_tensor("w3s_in", [256, HBIT], F32, kind="ExternalInput").ap()
    cst_in = nc.dram_tensor("cst_in", [1, 2], F32, kind="ExternalInput").ap()

    x1_out = nc.dram_tensor("x1_out", [TOK, D], F32, kind="ExternalOutput").ap()
    qw1_out = nc.dram_tensor("qw1_out", [768, D], BF16, kind="ExternalOutput").ap()
    qw2_out = nc.dram_tensor("qw2_out", [768, D], BF16, kind="ExternalOutput").ap()
    qw3_out = nc.dram_tensor("qw3_out", [256, HBIT], BF16, kind="ExternalOutput").ap()

    out_scr = nc.dram_tensor("out_scr", [D, TOK], BF16).ap()
    ginv_scr = nc.dram_tensor("ginv_scr", [1, 3], F32).ap()

    KT = DIN // 128
    NMP = D // 256
    NT = TOK // 128

    with contextlib.ExitStack() as ctx:
        sb = lambda *a: ctx.enter_context(nc.sbuf_tensor(*a))
        pt = lambda *a: ctx.enter_context(nc.psum_tensor(*a))
        sem = lambda n: ctx.enter_context(nc.semaphore(n))
        yfm = sb("yfm", [128, KT, TOK], BF16)
        wb = sb("wb", [128, 1, KT, 256], BF16)
        ofm = sb("ofm", [128, 2, 4, 512], BF16)
        obf = sb("obf", [128, D], BF16)
        of32 = sb("of32", [128, D], F32)
        xt = sb("xt", [128, D], F32)
        lnt = sb("lnt", [128, D], F32)
        x1t = sb("x1t", [128, 2, D], F32)
        gq = sb("gq", [1, 3 * NCORES], F32)
        gsum = sb("gsum", [1, 8, 3], F32)
        ginv = sb("ginv", [1, 3], F32)
        ginvb = sb("ginvb", [128, 3], F32)
        HH = HBIT // 2
        gw = sb("gw", [128, 2, HH], F32)
        sct = sb("sct", [128, HH], F32)
        itt = sb("itt", [128, HH], I32)
        qt = sb("qt", [128, 2, HH], BF16)
        cst = sb("cst", [128, 2], F32)
        bar = sb("bar", [128, 1], F32)
        ps = pt("ps", [128, 2, 4, 512], F32)
        s_y = sem("s_y")
        s_w = sem("s_w")
        s_mm = sem("s_mm")
        s_ev = sem("s_ev")
        s_os = sem("s_os")
        s_tl = sem("s_tl")
        s_v2 = sem("s_v2")
        s_x1 = sem("s_x1")
        s_g = sem("s_g")
        s_gv = sem("s_gv")
        s_gs = sem("s_gs")
        s_gr = sem("s_gr")
        s_sc = sem("s_sc")
        s_og = sem("s_og")
        s_x1g = sem("s_x1g")
        s_g1 = sem("s_g1")
        s_gs1 = sem("s_gs1")
        s_c = sem("s_c")
        block = ctx.enter_context(nc.Block())

        QTILES = ([("w1", i, 0, D) for i in range(6)] + [("w2", i, 0, D) for i in range(6)]
                  + [("w3", i, c * 2816, 2816) for i in range(2) for c in range(2)])

        @block.gpsimd
        def _(gpsimd):
            for k in range(KT):
                gpsimd.dma_start(out=yfm[:, k, :],
                                 in_=yfm_in[k * 128:(k + 1) * 128, :]).then_inc(s_y, 16)
            for mp in range(NMP):
                if mp >= 1:
                    gpsimd.wait_ge(s_mm, 4 * mp)
                for k in range(KT):
                    gpsimd.dma_start(out=wb[:, 0, k, :],
                                     in_=woT_in[k * 128:(k + 1) * 128, mp * 256:(mp + 1) * 256]
                                     ).then_inc(s_w, 16)
            def _qstore(i):
                wn, ti, c0, wid = QTILES[i]
                dst = {"w1": qw1_out, "w2": qw2_out, "w3": qw3_out}[wn]
                gpsimd.wait_ge(s_gv, i + 1)
                gpsimd.dma_start(out=dst[ti * 128:(ti + 1) * 128, c0:c0 + wid],
                                 in_=qt[:, i % 2, 0:wid]).then_inc([s_gs, s_gs1][i % 2], 16)
            for i, (wn, ti, c0, wid) in enumerate(QTILES):
                if i >= 2:
                    gpsimd.wait_ge(s_sc, i - 1)
                srcw = {"w1": w1s_in, "w2": w2s_in, "w3": w3s_in}[wn]
                gpsimd.dma_start(out=gw[:, i % 2, 0:wid],
                                 in_=srcw[ti * 128:(ti + 1) * 128, c0:c0 + wid]
                                 ).then_inc([s_g, s_g1][i % 2], 16)
                if i >= 1:
                    _qstore(i - 1)
            _qstore(len(QTILES) - 1)

        @block.sync
        def _(sync):
            sync.dma_start(out=cst[:], in_=cst_in.partition_broadcast(128)).then_inc(s_c, 16)
            sync.dma_start(out=gq[:], in_=gparts_in[:]).then_inc(s_c, 16)
            for mp in range(NMP):
                pb = mp % 2
                for i in range(4):
                    mm, j = divmod(i, 2)
                    m = mp * 2 + mm
                    sync.wait_ge(s_ev, 4 * mp + i + 1)
                    sync.dma_start(out=out_scr[m * 128:(m + 1) * 128, j * 512:(j + 1) * 512],
                                   in_=ofm[:, pb, i, :]).then_inc(s_os, 16)
                sync.wait_ge(s_os, 64 * (mp + 1))
                sync.sem_inc(s_og, 1)
            sync.wait_ge(s_os, 16 * 4 * NMP)
            for nt in range(NT):
                if nt >= 1:
                    sync.wait_ge(s_v2, 3 * nt)
                sync.dma_start(out=obf[:], in_=out_scr[:, nt * 128:(nt + 1) * 128],
                               transpose=True).then_inc(s_tl, 16)
                sync.dma_start(out=xt[:], in_=x_in[nt * 128:(nt + 1) * 128, :]).then_inc(s_tl, 16)
                sync.dma_start(out=lnt[:], in_=ln0_in[nt * 128:(nt + 1) * 128, :]).then_inc(s_tl, 16)
                sync.wait_ge(s_v2, 3 * nt + 3)
                sync.dma_start(out=x1_out[nt * 128:(nt + 1) * 128, :],
                               in_=x1t[:, nt % 2, :]).then_inc(s_x1, 16)
                sync.wait_ge(s_x1, 16 * (nt + 1))
                sync.sem_inc(s_x1g, 1)
            sync.wait_ge(s_gr, 3)
            sync.dma_start(out=ginv_scr[:], in_=ginv[:]).then_inc(s_c, 16)
            sync.wait_ge(s_c, 48)
            sync.dma_start(out=ginvb[:], in_=ginv_scr.partition_broadcast(128)).then_inc(s_c, 16)
            sync.wait_ge(s_gs, 16 * (len(QTILES) // 2))
            sync.wait_ge(s_gs1, 16 * (len(QTILES) // 2))

        @block.tensor
        def _(tensor):
            tensor.wait_ge(s_y, 16 * KT)
            for mp in range(NMP):
                pb = mp % 2
                tensor.wait_ge(s_w, 16 * KT * (mp + 1))
                for i in range(4):
                    mm, j = divmod(i, 2)
                    if mp >= 2:
                        tensor.wait_ge(s_ev, 4 * (mp - 1))
                    for k in range(KT):
                        mo = tensor.matmul(ps[:, pb, i, :],
                                           wb[:, 0, k, mm * 128:(mm + 1) * 128],
                                           yfm[:, k, j * 512:(j + 1) * 512],
                                           start=(k == 0), stop=(k == KT - 1))
                    mo.then_inc(s_mm, 1)

        @block.vector
        def _(vector):
            for mp in range(NMP):
                pb = mp % 2
                for i in range(4):
                    vector.wait_ge(s_mm, 4 * mp + i + 1)
                    if mp >= 2:
                        vector.wait_ge(s_og, mp - 1)
                    vector.tensor_copy(out=ofm[:, pb, i, :], in_=ps[:, pb, i, :])
                    vector.drain()
                    vector.memset(bar[:, 0:1], 0.0).then_inc(s_ev, 1)
            for nt in range(NT):
                vector.wait_ge(s_tl, 48 * nt + 48)
                if nt >= 2:
                    vector.wait_ge(s_x1g, nt - 1)
                vector.tensor_copy(out=of32[:], in_=obf[:]).then_inc(s_v2, 1)
                vector.tensor_tensor(out=x1t[:, nt % 2, :], in0=xt[:], in1=lnt[:],
                                     op=ALU.add).then_inc(s_v2, 1)
                vector.drain()
                vector.tensor_tensor(out=x1t[:, nt % 2, :], in0=x1t[:, nt % 2, :],
                                     in1=of32[:], op=ALU.add)
                vector.drain()
                vector.memset(bar[:, 0:1], 0.0).then_inc(s_v2, 1)
            vector.wait_ge(s_c, 32)
            vector.tensor_reduce(out=gsum[0:1, 0, :],
                                 in_=gq.ap().rearrange("p (c j) -> p j c", c=NCORES),
                                 axis=AX.X, op=ALU.add)
            vector.drain()
            vector.memset(bar[:, 0:1], 0.0).then_inc(s_gr, 1)
            vector.wait_ge(s_gr, 2)
            vector.reciprocal(out=ginv[:], in_=gsum[0:1, 1, :])
            vector.drain()
            vector.memset(bar[:, 0:1], 0.0).then_inc(s_gr, 1)
            for i, (wn, ti, c0, wid) in enumerate(QTILES):
                vector.wait_ge(s_sc, i + 1)
                if i >= 2:
                    vector.wait_ge([s_gs, s_gs1][i % 2], 16 * (i // 2))
                vector.tensor_copy(out=itt[:, 0:wid], in_=sct[:, 0:wid])
                vector.drain()
                vector.tensor_scalar(out=qt[:, i % 2, 0:wid], in0=itt[:, 0:wid],
                                     scalar1=-1.0, scalar2=1.0, op0=ALU.max, op1=ALU.min)
                vector.drain()
                vector.memset(bar[:, 0:1], 0.0).then_inc(s_gv, 1)

        @block.scalar
        def _(scalar):
            NW = float(HBIT * D)
            scalar.wait_ge(s_gr, 1)
            scalar.activation(out=gsum[0:1, 1, :], in_=gsum[0:1, 0, :], func=AF.Identity,
                              scale=1.0 / NW, bias=cst[0:1, 0:1])
            scalar.drain()
            scalar.copy(out=bar[0:1, 0:1], in_=cst[0:1, 0:1]).then_inc(s_gr, 1)
            scalar.wait_ge(s_c, 64)
            for i, (wn, ti, c0, wid) in enumerate(QTILES):
                j = {"w1": 0, "w2": 1, "w3": 2}[wn]
                scalar.wait_ge([s_g, s_g1][i % 2], 16 * (i // 2 + 1))
                if i >= 1:
                    scalar.wait_ge(s_gv, i)
                scalar.activation(out=sct[:, 0:wid], in_=gw[:, i % 2, 0:wid],
                                  func=AF.Identity, scale=ginvb[:, j:j + 1])
                scalar.drain()
                scalar.copy(out=bar[0:1, 0:1], in_=cst[0:1, 0:1]).then_inc(s_sc, 1)

    return nc


def run_L4(inputs, y, ln0, gparts):
    nc = build_L4()
    x = inputs["L0_attn_hidden_states"]
    woT = np.ascontiguousarray(inputs["L0_attn_out_proj_w"].T)
    gp = np.ascontiguousarray(gparts.reshape(1, 3 * NCORES))
    w1 = inputs["L0_ffn_w1"]
    w2 = inputs["L0_ffn_w2"]
    w3 = inputs["L0_ffn_w3"]
    pad = np.zeros((64, D), np.float32)
    cstv = np.array([[EPS_BIT, 1.0]], np.float32)
    in_maps = []
    for c in range(NCORES):
        tsl = slice(c * TC, (c + 1) * TC)
        yfm = np.ascontiguousarray(
            y[:, :, tsl].transpose(1, 0, 2).reshape(DIN, TOK))
        in_maps.append(dict(
            yfm_in=yfm, woT_in=woT,
            x_in=np.ascontiguousarray(x[:, tsl, :].reshape(TOK, D)),
            ln0_in=np.ascontiguousarray(ln0[:, tsl, :].reshape(TOK, D)),
            gparts_in=gp,
            w1s_in=np.ascontiguousarray(np.concatenate([w1[c * 704:(c + 1) * 704], pad], 0)),
            w2s_in=np.ascontiguousarray(np.concatenate([w2[c * 704:(c + 1) * 704], pad], 0)),
            w3s_in=np.ascontiguousarray(w3[c * 256:(c + 1) * 256]),
            cst_in=cstv))
    res = run_bass_kernel_spmd(nc, in_maps, core_ids=list(range(NCORES)))
    x1 = np.empty((B, L, D), np.float32)
    qw1 = np.empty((HBIT, D), BF)
    qw2 = np.empty((HBIT, D), BF)
    qw3 = np.empty((D, HBIT), BF)
    for c, r in enumerate(res.results):
        x1[:, c * TC:(c + 1) * TC, :] = r["x1_out"].reshape(B, TC, D)
        qw1[c * 704:(c + 1) * 704] = r["qw1_out"][:704]
        qw2[c * 704:(c + 1) * 704] = r["qw2_out"][:704]
        qw3[c * 256:(c + 1) * 256] = r["qw3_out"]
    return x1, qw1, qw2, qw3




def build_L5():
    nc = bass.Bass("TRN2", target_bir_lowering=False, debug=False)
    x1_in = nc.dram_tensor("x1_in", [TOK, D], F32, kind="ExternalInput").ap()
    qw1T_in = nc.dram_tensor("qw1T_in", [D, HBIT], BF16, kind="ExternalInput").ap()
    qw2T_in = nc.dram_tensor("qw2T_in", [D, HBIT], BF16, kind="ExternalInput").ap()
    qw3T_in = nc.dram_tensor("qw3T_in", [HBIT, D], BF16, kind="ExternalInput").ap()
    wkT_in = nc.dram_tensor("wkT_in", [D, D], F32, kind="ExternalInput").ap()
    wvT_in = nc.dram_tensor("wvT_in", [D, D], F32, kind="ExternalInput").ap()
    cst_in = nc.dram_tensor("cst_in", [1, 2], F32, kind="ExternalInput").ap()

    x2_out = nc.dram_tensor("x2_out", [TOK, D], F32, kind="ExternalOutput").ap()
    xn2_out = nc.dram_tensor("xn2_out", [TOK, D], BF16, kind="ExternalOutput").ap()
    K_out = nc.dram_tensor("K_out", [D, TOK], BF16, kind="ExternalOutput").ap()
    V_out = nc.dram_tensor("V_out", [D, TOK], BF16, kind="ExternalOutput").ap()

    xn1_scr = nc.dram_tensor("xn1_scr", [TOK, D], BF16).ap()
    h_scr = nc.dram_tensor("h_scr", [HBIT, TOK], BF16).ap()
    dn_scr = nc.dram_tensor("dn_scr", [D, TOK], BF16).ap()

    KT = D // 128
    MPH = HBIT // 256
    MPD = D // 256
    KH = HBIT // 128
    NT = TOK // 128

    with contextlib.ExitStack() as ctx:
        sb = lambda *a: ctx.enter_context(nc.sbuf_tensor(*a))
        pt = lambda *a: ctx.enter_context(nc.psum_tensor(*a))
        sem = lambda n: ctx.enter_context(nc.semaphore(n))
        xt = sb("xt", [128, D], F32)
        dnb = sb("dnb", [128, D], BF16)
        dn32 = sb("dn32", [128, D], F32)
        x2t = sb("x2t", [128, 2, D], F32)
        sq = sb("sq", [128, D], F32)
        sq2 = sb("sq2", [128, D], F32)
        stat = sb("stat", [128, 4], F32)
        xnt = sb("xnt", [128, 2, D], BF16)
        cst = sb("cst", [128, 2], F32)
        bar = sb("bar", [128, 1], F32)
        xfm = sb("xfm", [128, KT, TOK], BF16)
        wA = sb("wA", [128, KT, 256], BF16)
        wB = sb("wB", [128, KT, 256], BF16)
        wC = sb("wC", [128, KH, 256], BF16)
        hst = sb("hst", [128, 2, 4, 512], BF16)
        hbuf = sb("hbuf", [128, 2, TOK], BF16)
        evg = sb("evg", [128, 2, 4, 512], BF16)
        evu = sb("evu", [128, 2, 4, 512], BF16)
        sgt = sb("sgt", [128, 512], F32)
        ps = pt("ps", [128, 8, 512], F32)
        s_c = sem("s_c")
        s_lx = sem("s_lx")
        s_xs1 = sem("s_xs1")
        s_xq1 = sem("s_xq1")
        s_ln = sem("s_ln")
        s_lns = sem("s_lns")
        s_tf1 = sem("s_tf1")
        s_tf2 = sem("s_tf2")
        s_w = sem("s_w")
        s_mm = sem("s_mm")
        s_hv = sem("s_hv")
        s_md = sem("s_md")
        s_dv = sem("s_dv")
        s_ds = sem("s_ds")
        s_x2 = sem("s_x2")
        s_xv = sem("s_xv")
        s_x2q = sem("s_x2q")
        s_x2s = sem("s_x2s")
        s_hs = sem("s_hs")
        s_hq = sem("s_hq")
        s_hl0 = sem("s_hl0")
        s_hl1 = sem("s_hl1")
        s_mdk = sem("s_mdk")
        s_mk = sem("s_mk")
        s_sg = sem("s_sg")
        s_kv = sem("s_kv")
        s_kvs = sem("s_kvs")
        block = ctx.enter_context(nc.Block())

        def ln_vector(vector, in_ap, nn):
            vector.reduce_sum(out=stat[:, 0:1], in_=in_ap, axis=AX.X)
            vector.drain()
            vector.tensor_scalar(out=stat[:, 1:2], in0=stat[:, 0:1], scalar1=-1.0 / D,
                                 scalar2=None, op0=ALU.mult, op1=ALU.bypass)
            vector.drain()
            vector.tensor_scalar(out=sq[:], in0=in_ap, scalar1=stat[:, 1:2],
                                 scalar2=None, op0=ALU.add, op1=ALU.bypass)
            vector.drain()
            vector.memset(bar[:, 0:1], 0.0).then_inc(s_ln, 1)
            vector.wait_ge(s_lns, nn + 1)
            vector.reciprocal(out=stat[:, 3:4], in_=stat[:, 2:3])
            vector.drain()
            vector.tensor_scalar(out=xnt[:, nn % 2, :], in0=sq[:], scalar1=stat[:, 3:4],
                                 scalar2=None, op0=ALU.mult, op1=ALU.bypass)
            vector.drain()
            vector.memset(bar[:, 0:1], 0.0).then_inc(s_ln, 1)

        def ln_scalar(scalar, nn):
            scalar.wait_ge(s_ln, 2 * nn + 1)
            scalar.activation(out=sq2[:], in_=sq[:], func=AF.Square, accum_out=stat[:, 0:1])
            scalar.drain()
            scalar.activation(out=stat[:, 2:3], in_=stat[:, 0:1], func=AF.Sqrt,
                              scale=1.0 / D, bias=cst[:, 0:1])
            scalar.drain()
            scalar.copy(out=bar[0:1, 0:1], in_=cst[0:1, 0:1]).then_inc(s_lns, 1)

        @block.sync
        def _(sync):
            sync.dma_start(out=cst[:], in_=cst_in.partition_broadcast(128)).then_inc(s_c, 16)
            for n in range(NT):
                if n >= 1:
                    sync.wait_ge(s_ln, 2 * n)
                sync.dma_start(out=xt[:], in_=x1_in[n * 128:(n + 1) * 128, :]).then_inc(s_lx, 16)
                sync.wait_ge(s_ln, 2 * n + 2)
                sync.dma_start(out=xn1_scr[n * 128:(n + 1) * 128, :],
                               in_=xnt[:, n % 2, :]).then_inc(s_xs1, 16)
                sync.wait_ge(s_xs1, 16 * (n + 1))
                sync.sem_inc(s_xq1, 1)
            for k in range(KT):
                sync.dma_start(out=xfm[:, k, :], in_=xn1_scr[:, k * 128:(k + 1) * 128],
                               transpose=True).then_inc(s_tf1, 16)
            for mp in range(MPH):
                pb = mp % 2
                for i in range(4):
                    mm, j = divmod(i, 2)
                    m = mp * 2 + mm
                    sync.wait_ge(s_hv, 4 * mp + i + 1)
                    sync.dma_start(out=h_scr[m * 128:(m + 1) * 128, j * 512:(j + 1) * 512],
                                   in_=hst[:, pb, i, :]).then_inc(s_hs, 16)
                sync.wait_ge(s_hs, 64 * (mp + 1))
                sync.sem_inc(s_hq, 1)
            for mp in range(MPD):
                pb = mp % 2
                for i in range(4):
                    mm, j = divmod(i, 2)
                    m = mp * 2 + mm
                    sync.wait_ge(s_dv, 4 * mp + i + 1)
                    sync.dma_start(out=dn_scr[m * 128:(m + 1) * 128, j * 512:(j + 1) * 512],
                                   in_=evg[:, pb, i, :]).then_inc(s_ds, 16)
                sync.wait_ge(s_ds, 64 * (mp + 1) + mp)
                sync.sem_inc(s_ds, 1)
            sync.wait_ge(s_ds, 64 * MPD + MPD)
            for n in range(NT):
                if n >= 1:
                    sync.wait_ge(s_ln, 2 * (NT + n))
                sync.dma_start(out=xt[:], in_=x1_in[n * 128:(n + 1) * 128, :]).then_inc(s_x2, 16)
                sync.dma_start(out=dnb[:], in_=dn_scr[:, n * 128:(n + 1) * 128],
                               transpose=True).then_inc(s_x2, 16)
                sync.wait_ge(s_xv, n + 1)
                sync.dma_start(out=x2_out[n * 128:(n + 1) * 128, :],
                               in_=x2t[:, n % 2, :]).then_inc(s_x2s, 16)
                sync.wait_ge(s_ln, 2 * (NT + n) + 2)
                sync.dma_start(out=xn2_out[n * 128:(n + 1) * 128, :],
                               in_=xnt[:, n % 2, :]).then_inc(s_x2s, 16)
                sync.wait_ge(s_x2s, 32 * (n + 1))
                sync.sem_inc(s_x2q, 1)
            for k in range(KT):
                sync.dma_start(out=xfm[:, k, :], in_=xn2_out[:, k * 128:(k + 1) * 128],
                               transpose=True).then_inc(s_tf2, 16)
            for mp in range(MPD):
                pb = mp % 2
                for i in range(4):
                    sync.wait_ge(s_kv, 8 * mp + 2 * i + 2)
                    mm, j = divmod(i, 2)
                    m = mp * 2 + mm
                    sync.dma_start(out=K_out[m * 128:(m + 1) * 128, j * 512:(j + 1) * 512],
                                   in_=evg[:, pb, i, :]).then_inc(s_kvs, 16)
                    sync.dma_start(out=V_out[m * 128:(m + 1) * 128, j * 512:(j + 1) * 512],
                                   in_=evu[:, pb, i, :]).then_inc(s_kvs, 16)
                sync.wait_ge(s_kvs, 128 * (mp + 1) + mp)
                sync.sem_inc(s_kvs, 1)
            sync.wait_ge(s_kvs, 128 * MPD + MPD)

        @block.gpsimd
        def _(gpsimd):
            for mp in range(MPH):
                if mp >= 1:
                    gpsimd.wait_ge(s_mm, 8 * mp)
                for k in range(KT):
                    gpsimd.dma_start(out=wA[:, k, :],
                                     in_=qw1T_in[k * 128:(k + 1) * 128, mp * 256:(mp + 1) * 256]
                                     ).then_inc(s_w, 16)
                    gpsimd.dma_start(out=wB[:, k, :],
                                     in_=qw2T_in[k * 128:(k + 1) * 128, mp * 256:(mp + 1) * 256]
                                     ).then_inc(s_w, 16)
            gpsimd.wait_ge(s_hq, MPH)
            for mp in range(MPD):
                if mp >= 1:
                    gpsimd.wait_ge(s_dv, 4 * mp)
                for k in range(KH):
                    gpsimd.dma_start(out=wC[:, k, :],
                                     in_=qw3T_in[k * 128:(k + 1) * 128, mp * 256:(mp + 1) * 256]
                                     ).then_inc(s_w, 16)
                for k in range(KH):
                    if mp * KH + k >= 2:
                        gpsimd.wait_ge(s_mdk, mp * KH + k - 1)
                    gpsimd.dma_start(out=hbuf[:, k % 2, :],
                                     in_=h_scr[k * 128:(k + 1) * 128, :]
                                     ).then_inc([s_hl0, s_hl1][k % 2], 16)
            for mp in range(MPD):
                if mp >= 1:
                    gpsimd.wait_ge(s_mk, 8 * mp)
                for k in range(KT):
                    gpsimd.dma_start(out=wA[:, k, :],
                                     in_=wkT_in[k * 128:(k + 1) * 128, mp * 256:(mp + 1) * 256]
                                     ).then_inc(s_w, 16)
                    gpsimd.dma_start(out=wB[:, k, :],
                                     in_=wvT_in[k * 128:(k + 1) * 128, mp * 256:(mp + 1) * 256]
                                     ).then_inc(s_w, 16)

        @block.tensor
        def _(tensor):
            tensor.wait_ge(s_tf1, 16 * KT)
            for mp in range(MPH):
                tensor.wait_ge(s_w, 16 * 32 * (mp + 1))
                if mp >= 1:
                    tensor.wait_ge(s_hv, 4 * mp)
                for i in range(4):
                    mm, j = divmod(i, 2)
                    for k in range(KT):
                        mg = tensor.matmul(ps[:, i, :], wA[:, k, mm * 128:(mm + 1) * 128],
                                           xfm[:, k, j * 512:(j + 1) * 512],
                                           start=(k == 0), stop=(k == KT - 1))
                    mg.then_inc(s_mm, 1)
                    for k in range(KT):
                        mu = tensor.matmul(ps[:, 4 + i, :], wB[:, k, mm * 128:(mm + 1) * 128],
                                           xfm[:, k, j * 512:(j + 1) * 512],
                                           start=(k == 0), stop=(k == KT - 1))
                    mu.then_inc(s_mm, 1)
            tensor.wait_ge(s_sg, 4 * MPH)
            tensor.wait_ge(s_hv, 4 * MPH)
            W0 = 16 * 32 * MPH
            for mp in range(MPD):
                tensor.wait_ge(s_w, W0 + 16 * KH * (mp + 1))
                if mp >= 1:
                    tensor.wait_ge(s_dv, 4 * mp)
                for k in range(KH):
                    tensor.wait_ge([s_hl0, s_hl1][k % 2], 16 * (mp * 22 + k // 2 + 1))
                    for i in range(4):
                        mm, j = divmod(i, 2)
                        md = tensor.matmul(ps[:, i, :], wC[:, k, mm * 128:(mm + 1) * 128],
                                           hbuf[:, k % 2, j * 512:(j + 1) * 512],
                                           start=(k == 0), stop=(k == KH - 1))
                    md.then_inc(s_mdk, 1)
            tensor.wait_ge(s_dv, 4 * MPD)
            tensor.wait_ge(s_tf2, 16 * KT)
            W1 = W0 + 16 * KH * MPD
            for mp in range(MPD):
                tensor.wait_ge(s_w, W1 + 16 * 32 * (mp + 1))
                if mp >= 1:
                    tensor.wait_ge(s_kv, 8 * mp)
                for i in range(4):
                    mm, j = divmod(i, 2)
                    for k in range(KT):
                        mk1 = tensor.matmul(ps[:, i, :], wA[:, k, mm * 128:(mm + 1) * 128],
                                            xfm[:, k, j * 512:(j + 1) * 512],
                                            start=(k == 0), stop=(k == KT - 1))
                    mk1.then_inc(s_mk, 1)
                    for k in range(KT):
                        mv1 = tensor.matmul(ps[:, 4 + i, :], wB[:, k, mm * 128:(mm + 1) * 128],
                                            xfm[:, k, j * 512:(j + 1) * 512],
                                            start=(k == 0), stop=(k == KT - 1))
                    mv1.then_inc(s_mk, 1)

        @block.vector
        def _(vector):
            vector.wait_ge(s_c, 16)
            for n in range(NT):
                vector.wait_ge(s_lx, 16 * (n + 1))
                if n >= 2:
                    vector.wait_ge(s_xq1, n - 1)
                ln_vector(vector, xt[:], n)
            for mp in range(MPH):
                pb = mp % 2
                for i in range(4):
                    vector.wait_ge(s_sg, 4 * mp + i + 1)
                    if mp >= 2:
                        vector.wait_ge(s_hq, mp - 1)
                    vector.tensor_tensor(out=hst[:, pb, i, :],
                                         in0=sgt[:], in1=ps[:, 4 + i, :], op=ALU.mult)
                    vector.drain()
                    vector.memset(bar[:, 0:1], 0.0).then_inc(s_hv, 1)
            for mp in range(MPD):
                pb = mp % 2
                for i in range(4):
                    vector.wait_ge(s_mdk, (mp + 1) * KH)
                    if mp >= 2:
                        vector.wait_ge(s_ds, 64 * (mp - 1) + (mp - 1))
                    vector.tensor_copy(out=evg[:, pb, i, :], in_=ps[:, i, :])
                    vector.drain()
                    vector.memset(bar[:, 0:1], 0.0).then_inc(s_dv, 1)
            for n in range(NT):
                vector.wait_ge(s_x2, 32 * n + 32)
                if n >= 2:
                    vector.wait_ge(s_x2q, n - 1)
                vector.tensor_copy(out=dn32[:], in_=dnb[:])
                vector.drain()
                vector.tensor_tensor(out=x2t[:, n % 2, :], in0=xt[:], in1=dn32[:],
                                     op=ALU.add)
                vector.drain()
                vector.memset(bar[:, 0:1], 0.0).then_inc(s_xv, 1)
                ln_vector(vector, x2t[:, n % 2, :], NT + n)
            vector.wait_ge(s_ds, 64 * MPD + MPD)
            for mp in range(MPD):
                pb = mp % 2
                for i in range(4):
                    vector.wait_ge(s_mk, 8 * mp + 2 * i + 2)
                    if mp >= 2:
                        vector.wait_ge(s_kvs, 128 * (mp - 1) + (mp - 1))
                    vector.tensor_copy(out=evg[:, pb, i, :], in_=ps[:, i, :])
                    vector.drain()
                    vector.tensor_copy(out=evu[:, pb, i, :], in_=ps[:, 4 + i, :])
                    vector.drain()
                    vector.memset(bar[:, 0:1], 0.0).then_inc(s_kv, 2)

        @block.scalar
        def _(scalar):
            scalar.wait_ge(s_c, 16)
            for n in range(NT):
                ln_scalar(scalar, n)
            for mp in range(MPH):
                for i in range(4):
                    scalar.wait_ge(s_mm, 8 * mp + 2 * i + 2)
                    if 4 * mp + i >= 1:
                        scalar.wait_ge(s_hv, 4 * mp + i)
                    scalar.activation(out=sgt[:], in_=ps[:, i, :], func=AF.Sigmoid)
                    scalar.drain()
                    scalar.copy(out=bar[0:1, 0:1], in_=cst[0:1, 0:1]).then_inc(s_sg, 1)
            for n in range(NT):
                ln_scalar(scalar, NT + n)

    return nc


def run_L5(inputs, x1, qw1, qw2, qw3):
    nc = build_L5()
    qw1T = np.ascontiguousarray(qw1.T)
    qw2T = np.ascontiguousarray(qw2.T)
    qw3T = np.ascontiguousarray(qw3.T)
    wkT = np.ascontiguousarray(inputs["L1_attn_w_k"].T)
    wvT = np.ascontiguousarray(inputs["L1_attn_w_v"].T)
    cstv = np.array([[EPS_LN, 1.0]], np.float32)
    in_maps = []
    for c in range(NCORES):
        tsl = slice(c * TC, (c + 1) * TC)
        in_maps.append(dict(
            x1_in=np.ascontiguousarray(x1[:, tsl, :].reshape(TOK, D)),
            qw1T_in=qw1T, qw2T_in=qw2T, qw3T_in=qw3T,
            wkT_in=wkT, wvT_in=wvT, cst_in=cstv))
    res = run_bass_kernel_spmd(nc, in_maps, core_ids=list(range(NCORES)))
    x2 = np.empty((B, L, D), np.float32)
    xn2 = np.empty((B, L, D), BF)
    Kf = np.empty((D, B, L), BF)
    Vf = np.empty((D, B, L), BF)
    for c, r in enumerate(res.results):
        x2[:, c * TC:(c + 1) * TC, :] = r["x2_out"].reshape(B, TC, D)
        xn2[:, c * TC:(c + 1) * TC, :] = r["xn2_out"].reshape(B, TC, D)
        Kf[:, :, c * TC:(c + 1) * TC] = r["K_out"].reshape(D, B, TC)
        Vf[:, :, c * TC:(c + 1) * TC] = r["V_out"].reshape(D, B, TC)
    return x2, xn2, Kf, Vf


HB2 = 5632


def build_L6():
    nc = bass.Bass("TRN2", target_bir_lowering=False, debug=False)
    gidx_in = nc.dram_tensor("gidx_in", [768, D], I32, kind="ExternalInput").ap()
    uidx_in = nc.dram_tensor("uidx_in", [768, D], I32, kind="ExternalInput").ap()
    didx_in = nc.dram_tensor("didx_in", [256, HGHOST], I32, kind="ExternalInput").ap()
    lut_in = nc.dram_tensor("lut_in", [1, 16], F32, kind="ExternalInput").ap()

    gw_out = nc.dram_tensor("gw_out", [768, D], BF16, kind="ExternalOutput").ap()
    uw_out = nc.dram_tensor("uw_out", [768, D], BF16, kind="ExternalOutput").ap()
    dw_out = nc.dram_tensor("dw_out", [256, HGHOST], BF16, kind="ExternalOutput").ap()

    TILES = ([("g", i, D) for i in range(6)] + [("u", i, D) for i in range(6)]
             + [("d", i, HGHOST) for i in range(2)])

    with contextlib.ExitStack() as ctx:
        sb = lambda *a: ctx.enter_context(nc.sbuf_tensor(*a))
        sem = lambda n: ctx.enter_context(nc.semaphore(n))
        lutb = sb("lutb", [128, 16], F32)
        idxt = sb("idxt", [128, 2, HGHOST], I32)
        acc = sb("acc", [128, 2, HGHOST], BF16)
        tmp = sb("tmp", [128, HGHOST], BF16)
        bar = sb("bar", [128, 1], F32)
        s_c = sem("s_c")
        s_l0 = sem("s_l0")
        s_l1 = sem("s_l1")
        s_v = sem("s_v")
        s_st0 = sem("s_st0")
        s_st1 = sem("s_st1")
        block = ctx.enter_context(nc.Block())

        @block.sync
        def _(sync):
            sync.dma_start(out=lutb[:], in_=lut_in.partition_broadcast(128)).then_inc(s_c, 16)
            for i, (wn, ti, wid) in enumerate(TILES):
                if i >= 2:
                    sync.wait_ge(s_v, i - 1)
                srcm = {"g": gidx_in, "u": uidx_in, "d": didx_in}[wn]
                sync.dma_start(out=idxt[:, i % 2, 0:wid],
                               in_=srcm[ti * 128:(ti + 1) * 128, :]).then_inc([s_l0, s_l1][i % 2], 16)

        @block.gpsimd
        def _(gpsimd):
            for i, (wn, ti, wid) in enumerate(TILES):
                dstm = {"g": gw_out, "u": uw_out, "d": dw_out}[wn]
                gpsimd.wait_ge(s_v, i + 1)
                gpsimd.dma_start(out=dstm[ti * 128:(ti + 1) * 128, :],
                                 in_=acc[:, i % 2, 0:wid]).then_inc([s_st0, s_st1][i % 2], 16)

        @block.vector
        def _(vector):
            vector.wait_ge(s_c, 16)
            for i, (wn, ti, wid) in enumerate(TILES):
                vector.wait_ge([s_l0, s_l1][i % 2], 16 * (i // 2 + 1))
                if i >= 2:
                    vector.wait_ge([s_st0, s_st1][i % 2], 16 * (i // 2))
                for kk in range(16):
                    vector.tensor_scalar(out=tmp[:, 0:wid], in0=idxt[:, i % 2, 0:wid],
                                         scalar1=float(kk), scalar2=lutb[:, kk:kk + 1],
                                         op0=ALU.is_equal, op1=ALU.mult)
                    vector.drain()
                    if kk == 0:
                        vector.tensor_copy(out=acc[:, i % 2, 0:wid], in_=tmp[:, 0:wid])
                    else:
                        vector.tensor_tensor(out=acc[:, i % 2, 0:wid], in0=acc[:, i % 2, 0:wid],
                                             in1=tmp[:, 0:wid], op=ALU.add)
                    vector.drain()
                vector.memset(bar[:, 0:1], 0.0).then_inc(s_v, 1)

        @block.tensor
        def _(tensor):
            tensor.nop()

        @block.scalar
        def _(scalar):
            scalar.nop()

        @block.sync
        def _(sync):
            sync.wait_ge(s_st0, 16 * 7)
            sync.wait_ge(s_st1, 16 * 7)

    return nc


def run_L6(inputs):
    nc = build_L6()
    gidx = inputs["L1_ffn_gate_idx"]
    uidx = inputs["L1_ffn_up_idx"]
    didx = inputs["L1_ffn_down_idx"]
    lut = np.ascontiguousarray(inputs["L1_ffn_lut"].reshape(1, 16))
    in_maps = []
    for c in range(NCORES):
        in_maps.append(dict(
            gidx_in=_padrows(gidx[c * 688:(c + 1) * 688], 768),
            uidx_in=_padrows(uidx[c * 688:(c + 1) * 688], 768),
            didx_in=np.ascontiguousarray(didx[c * 256:(c + 1) * 256]),
            lut_in=lut))
    res = run_bass_kernel_spmd(nc, in_maps, core_ids=list(range(NCORES)))
    gw = np.empty((HGHOST, D), BF)
    uw = np.empty((HGHOST, D), BF)
    dw = np.empty((D, HGHOST), BF)
    for c, r in enumerate(res.results):
        gw[c * 688:(c + 1) * 688] = r["gw_out"][:688]
        uw[c * 688:(c + 1) * 688] = r["uw_out"][:688]
        dw[c * 256:(c + 1) * 256] = r["dw_out"]
    return gw, uw, dw


NQI = 8


def build_L7a():
    nc = bass.Bass("TRN2", target_bir_lowering=False, debug=False)
    xn2s_in = nc.dram_tensor("xn2s_in", [TOK, D], BF16, kind="ExternalInput").ap()
    x2s_in = nc.dram_tensor("x2s_in", [TOK, D], F32, kind="ExternalInput").ap()
    K_in = nc.dram_tensor("K_in", [D, B * L], BF16, kind="ExternalInput").ap()
    V_in = nc.dram_tensor("V_in", [B * L, D], BF16, kind="ExternalInput").ap()
    wqT_in = nc.dram_tensor("wqT_in", [D, D], F32, kind="ExternalInput").ap()
    woT_in = nc.dram_tensor("woT_in", [D, D], F32, kind="ExternalInput").ap()
    mask_in = nc.dram_tensor("mask_in", [TOK, L], BF16, kind="ExternalInput").ap()
    ident_in = nc.dram_tensor("ident_in", [128, 128], BF16, kind="ExternalInput").ap()

    x3s_out = nc.dram_tensor("x3s_out", [TOK, D], F32, kind="ExternalOutput").ap()

    o_scr = nc.dram_tensor("o_scr", [TOK, D], BF16).ap()
    wo_scr = nc.dram_tensor("wo_scr", [D, TOK], BF16).ap()

    KT = D // 128
    MPD = D // 256
    NT = TOK // 128
    NKC = L // 512
    NKK = L // 128
    SCL = 1.0 / float(np.sqrt(HD))

    with contextlib.ExitStack() as ctx:
        sb = lambda *a: ctx.enter_context(nc.sbuf_tensor(*a))
        pt = lambda *a: ctx.enter_context(nc.psum_tensor(*a))
        sem = lambda n: ctx.enter_context(nc.semaphore(n))
        xfm = sb("xfm", [128, KT, TOK], BF16)
        wA = sb("wA", [128, KT, 256], BF16)
        wB = sb("wB", [128, KT, 256], BF16)
        Qfm = sb("Qfm", [128, KT, TOK], BF16)
        Kslab = sb("Kslab", [128, L], BF16)
        Vslab = sb("Vslab", [128, NKK, 128], BF16)
        maskt = sb("maskt", [128, 4, L], BF16)
        Ssb = sb("Ssb", [128, L], F32)
        Psb = sb("Psb", [128, L], BF16)
        PTt = sb("PTt", [128, 2, 128], BF16)
        ident = sb("ident", [128, 128], BF16)
        negmax = sb("negmax", [128, 1], F32)
        sbar = sb("sbar", [1, 2], F32)
        rowsum = sb("rowsum", [128, 1], F32)
        rl = sb("rl", [128, 1], F32)
        otile = sb("otile", [128, 4, 2, 128], BF16)
        evg = sb("evg", [128, 2, 4, 512], BF16)
        xt = sb("xt", [128, D], F32)
        wob = sb("wob", [128, D], BF16)
        wo32 = sb("wo32", [128, D], F32)
        x3t = sb("x3t", [128, 2, D], F32)
        bar = sb("bar", [128, 1], F32)
        psS = pt("psS", [128, 2, 512], F32)
        psT = pt("psT", [128, 2, 128], BF16)
        psO = pt("psO", [128, 128], F32)
        psW = pt("psW", [128, 4, 512], F32)
        s_c = sem("s_c")
        s_tf = sem("s_tf")
        s_wq = sem("s_wq")
        s_mq = sem("s_mq")
        s_qe = sem("s_qe")
        s_kv = sem("s_kv")
        s_mk = sem("s_mk")
        s_ms = sem("s_ms")
        s_se = sem("s_se")
        s_ex = sem("s_ex")
        s_rv = sem("s_rv")
        s_pt = sem("s_pt")
        s_pw = sem("s_pw")
        s_pe = sem("s_pe")
        s_pv = sem("s_pv")
        s_ao = sem("s_ao")
        s_oe = sem("s_oe")
        s_os = sem("s_os")
        s_oq = sem("s_oq")
        s_tf2 = sem("s_tf2")
        s_ww = sem("s_ww")
        s_mw = sem("s_mw")
        s_we = sem("s_we")
        s_ws = sem("s_ws")
        s_x3l = sem("s_x3l")
        s_x3v = sem("s_x3v")
        s_x3s = sem("s_x3s")
        s_x3q = sem("s_x3q")
        block = ctx.enter_context(nc.Block())

        def qi_of(i):
            return i

        @block.sync
        def _(sync):
            sync.dma_start(out=ident[:], in_=ident_in[:]).then_inc(s_c, 16)
            for k in range(KT):
                sync.dma_start(out=xfm[:, k, :], in_=xn2s_in[:, k * 128:(k + 1) * 128],
                               transpose=True).then_inc(s_tf, 16)
            for b in range(B):
                if b > 0:
                    sync.wait_ge(s_oq, b * (HEADS // 2))
                sync.dma_start(out=maskt[:],
                               in_=mask_in.rearrange("(b q p) l -> p (b q) l", b=B, p=128)[
                                   :, b * 4:(b + 1) * 4, :]).then_inc(s_mk, 16)
                for h in range(HEADS):
                    ih = b * HEADS + h
                    if ih >= 1:
                        sync.wait_ge(s_ao, ih * NQI // 2)
                    sync.dma_start(out=Kslab[:],
                                   in_=K_in[h * 128:(h + 1) * 128, b * L:(b + 1) * L]
                                   ).then_inc(s_kv, 16)
                    sync.dma_start(out=Vslab[:],
                                   in_=V_in[b * L:(b + 1) * L, h * 128:(h + 1) * 128]
                                   .rearrange("(kk p) d -> p kk d", p=128)).then_inc(s_kv, 16)
            sync.wait_ge(s_oq, B * (HEADS // 2))
            for k in range(KT):
                sync.dma_start(out=xfm[:, k, :], in_=o_scr[:, k * 128:(k + 1) * 128],
                               transpose=True).then_inc(s_tf2, 16)
            for mp in range(MPD):
                pb = mp % 2
                for i in range(4):
                    mm, j = divmod(i, 2)
                    m = mp * 2 + mm
                    sync.wait_ge(s_we, 4 * mp + i + 1)
                    sync.dma_start(out=wo_scr[m * 128:(m + 1) * 128, j * 512:(j + 1) * 512],
                                   in_=evg[:, pb, i, :]).then_inc(s_ws, 16)
                sync.wait_ge(s_ws, 64 * (mp + 1) + mp)
                sync.sem_inc(s_ws, 1)
            for n in range(NT):
                if n >= 1:
                    sync.wait_ge(s_x3v, 2 * n)
                sync.dma_start(out=xt[:], in_=x2s_in[n * 128:(n + 1) * 128, :]).then_inc(s_x3l, 16)
                sync.dma_start(out=wob[:], in_=wo_scr[:, n * 128:(n + 1) * 128],
                               transpose=True).then_inc(s_x3l, 16)
                sync.wait_ge(s_x3v, 2 * n + 2)
                sync.dma_start(out=x3s_out[n * 128:(n + 1) * 128, :],
                               in_=x3t[:, n % 2, :]).then_inc(s_x3s, 16)
                sync.wait_ge(s_x3s, 16 * (n + 1))
                sync.sem_inc(s_x3q, 1)

        @block.gpsimd
        def _(gpsimd):
            for mp in range(MPD):
                if mp >= 1:
                    gpsimd.wait_ge(s_mq, 4 * mp)
                for k in range(KT):
                    gpsimd.dma_start(out=wA[:, k, :],
                                     in_=wqT_in[k * 128:(k + 1) * 128, mp * 256:(mp + 1) * 256]
                                     ).then_inc(s_wq, 16)
            for b in range(B):
                for h in range(HEADS):
                    ih = b * HEADS + h
                    if h % 2 == 1:
                        for qi in range(4):
                            iq = (ih // 2) * 4 + qi
                            gpsimd.wait_ge(s_oe, ih * 4 + qi + 1)
                            gq = b * 4 + qi
                            gpsimd.dma_start(
                                out=o_scr[gq * 128:(gq + 1) * 128, (h - 1) * 128:(h + 1) * 128],
                                in_=otile[:, qi, :, :]).then_inc(s_os, 16)
                        gpsimd.wait_ge(s_os, 16 * ((ih // 2) * 4 + 4))
                        gpsimd.sem_inc(s_oq, 1)
            for mp in range(MPD):
                if mp >= 1:
                    gpsimd.wait_ge(s_mw, 4 * mp)
                for k in range(KT):
                    gpsimd.dma_start(out=wB[:, k, :],
                                     in_=woT_in[k * 128:(k + 1) * 128, mp * 256:(mp + 1) * 256]
                                     ).then_inc(s_ww, 16)

        @block.tensor
        def _(tensor):
            tensor.wait_ge(s_c, 16)
            tensor.wait_ge(s_tf, 16 * KT)
            for mp in range(MPD):
                tensor.wait_ge(s_wq, 16 * KT * (mp + 1))
                if mp >= 1:
                    tensor.wait_ge(s_qe, 4 * mp)
                for i in range(4):
                    mm, j = divmod(i, 2)
                    for k in range(KT):
                        mq = tensor.matmul(psW[:, i, :], wA[:, k, mm * 128:(mm + 1) * 128],
                                           xfm[:, k, j * 512:(j + 1) * 512],
                                           start=(k == 0), stop=(k == KT - 1))
                    mq.then_inc(s_mq, 1)
            tensor.wait_ge(s_qe, 4 * MPD)
            for b in range(B):
                for h in range(HEADS):
                    ih = b * HEADS + h
                    tensor.wait_ge(s_kv, 32 * (ih + 1))
                    for qi in range(4):
                        gq = b * 4 + qi
                        iq = ih * 4 + qi
                        for kb in range(NKC):
                            if iq * NKC + kb >= 2:
                                tensor.wait_ge(s_se, iq * NKC + kb - 1)
                            ms = tensor.matmul(psS[:, kb % 2, :],
                                               Qfm[:, h, gq * 128:(gq + 1) * 128],
                                               Kslab[:, kb * 512:(kb + 1) * 512],
                                               start=True, stop=True)
                            ms.then_inc(s_ms, 1)
                        tensor.wait_ge(s_ex, iq + 1)
                        for kk in range(NKK):
                            sem_e = [s_pe, s_pv][kk % 2]
                            cnt = iq * (NKK // 2) + kk // 2
                            if cnt >= 1:
                                tensor.wait_ge(sem_e, cnt)
                            tensor.transpose(psT[:, kk % 2, :], Psb[:, kk * 128:(kk + 1) * 128],
                                             ident[:]).then_inc([s_pt, s_pw][kk % 2], 1)
                            tensor.wait_ge(sem_e, cnt + 1)
                            mav = tensor.matmul(psO[:], PTt[:, kk % 2, :], Vslab[:, kk, :],
                                                start=(kk == 0), stop=(kk == NKK - 1))
                        mav.then_inc(s_ao, 1)
                        if True:
                            tensor.wait_ge(s_oe, iq + 1)
            tensor.wait_ge(s_tf2, 16 * KT)
            for mp in range(MPD):
                tensor.wait_ge(s_ww, 16 * KT * (mp + 1))
                if mp >= 1:
                    tensor.wait_ge(s_we, 4 * mp)
                for i in range(4):
                    mm, j = divmod(i, 2)
                    for k in range(KT):
                        mw = tensor.matmul(psW[:, i, :], wB[:, k, mm * 128:(mm + 1) * 128],
                                           xfm[:, k, j * 512:(j + 1) * 512],
                                           start=(k == 0), stop=(k == KT - 1))
                    mw.then_inc(s_mw, 1)

        @block.vector
        def _(vector):
            for mp in range(MPD):
                for i in range(4):
                    mm, j = divmod(i, 2)
                    m = mp * 2 + mm
                    vector.wait_ge(s_mq, 4 * mp + i + 1)
                    vector.tensor_scalar(out=Qfm[:, m, j * 512:(j + 1) * 512],
                                         in0=psW[:, i, :], scalar1=SCL, scalar2=None,
                                         op0=ALU.mult, op1=ALU.bypass)
                    vector.drain()
                    vector.memset(bar[:, 0:1], 0.0).then_inc(s_qe, 1)
            for b in range(B):
                vector.wait_ge(s_mk, 16 * (b + 1))
                for h in range(HEADS):
                    ih = b * HEADS + h
                    for qi in range(4):
                        iq = ih * 4 + qi
                        for kb in range(NKC):
                            vector.wait_ge(s_ms, iq * NKC + kb + 1)
                            vector.tensor_tensor(out=Ssb[:, kb * 512:(kb + 1) * 512],
                                                 in0=psS[:, kb % 2, :],
                                                 in1=maskt[:, qi, kb * 512:(kb + 1) * 512],
                                                 op=ALU.add)
                            vector.drain()
                            vector.memset(bar[:, 0:1], 0.0).then_inc(s_se, 1)
                        vector.reduce_max(out=negmax[:], in_=Ssb[:], axis=AX.X, negate=True)
                        vector.drain()
                        vector.memset(bar[:, 0:1], 0.0).then_inc(s_rv, 2)
                        vector.wait_ge(s_ex, iq + 1)
                        vector.reciprocal(out=rl[:], in_=rowsum[:])
                        vector.drain()
                        vector.memset(bar[:, 0:1], 0.0).then_inc(s_rv, 1)
                        for kk in range(1, NKK, 2):
                            vector.wait_ge(s_pw, iq * (NKK // 2) + kk // 2 + 1)
                            vector.tensor_copy(out=PTt[:, 1, :], in_=psT[:, 1, :])
                            vector.drain()
                            vector.memset(bar[:, 0:1], 0.0).then_inc(s_pv, 1)
                        vector.wait_ge(s_ao, iq + 1)
                        if ih >= 2:
                            vector.wait_ge(s_oq, ih // 2)
                        vector.tensor_scalar(out=otile[:, qi, h % 2, :], in0=psO[:],
                                             scalar1=rl[:], scalar2=None,
                                             op0=ALU.mult, op1=ALU.bypass)
                        vector.drain()
                        vector.memset(bar[:, 0:1], 0.0).then_inc(s_oe, 1)
            for mp in range(MPD):
                pb = mp % 2
                for i in range(4):
                    vector.wait_ge(s_mw, 4 * mp + i + 1)
                    if mp >= 2:
                        vector.wait_ge(s_ws, 64 * (mp - 1) + (mp - 1))
                    vector.tensor_copy(out=evg[:, pb, i, :], in_=psW[:, i, :])
                    vector.drain()
                    vector.memset(bar[:, 0:1], 0.0).then_inc(s_we, 1)
            for n in range(NT):
                vector.wait_ge(s_x3l, 32 * (n + 1))
                if n >= 2:
                    vector.wait_ge(s_x3q, n - 1)
                vector.tensor_copy(out=wo32[:], in_=wob[:])
                vector.drain()
                vector.memset(bar[:, 0:1], 0.0).then_inc(s_x3v, 1)
                vector.tensor_tensor(out=x3t[:, n % 2, :], in0=xt[:], in1=wo32[:],
                                     op=ALU.add)
                vector.drain()
                vector.memset(bar[:, 0:1], 0.0).then_inc(s_x3v, 1)

        @block.scalar
        def _(scalar):
            scalar.wait_ge(s_c, 16)
            for b in range(B):
                for h in range(HEADS):
                    ih = b * HEADS + h
                    for qi in range(4):
                        iq = ih * 4 + qi
                        scalar.wait_ge(s_rv, 3 * iq + 2)
                        scalar.activation(out=Psb[:], in_=Ssb[:], func=AF.Exp,
                                          bias=negmax[:], accum_out=rowsum[:])
                        scalar.drain()
                        scalar.copy(out=sbar[0:1, 0:1], in_=ident[0:1, 0:1]).then_inc(s_ex, 1)
                        for kk in range(0, NKK, 2):
                            scalar.wait_ge(s_pt, iq * (NKK // 2) + kk // 2 + 1)
                            scalar.copy(out=PTt[:, 0, :], in_=psT[:, 0, :])
                            scalar.drain()
                            scalar.copy(out=sbar[0:1, 0:1], in_=ident[0:1, 0:1]).then_inc(s_pe, 1)

    return nc


def build_L7b():
    nc = bass.Bass("TRN2", target_bir_lowering=False, debug=False)
    x3_in = nc.dram_tensor("x3_in", [TOK, D], F32, kind="ExternalInput").ap()
    gwT_in = nc.dram_tensor("gwT_in", [D, HB2], BF16, kind="ExternalInput").ap()
    uwT_in = nc.dram_tensor("uwT_in", [D, HB2], BF16, kind="ExternalInput").ap()
    dwT_in = nc.dram_tensor("dwT_in", [HB2, D], BF16, kind="ExternalInput").ap()
    cst_in = nc.dram_tensor("cst_in", [1, 2], F32, kind="ExternalInput").ap()

    x4_out = nc.dram_tensor("x4_out", [TOK, D], F32, kind="ExternalOutput").ap()

    xn3_scr = nc.dram_tensor("xn3_scr", [TOK, D], BF16).ap()
    h_scr = nc.dram_tensor("h_scr", [HB2, TOK], BF16).ap()
    dn_scr = nc.dram_tensor("dn_scr", [D, TOK], BF16).ap()

    KT = D // 128
    MPH = HB2 // 256
    MPD = D // 256
    KH = HB2 // 128
    NT = TOK // 128

    with contextlib.ExitStack() as ctx:
        sb = lambda *a: ctx.enter_context(nc.sbuf_tensor(*a))
        pt = lambda *a: ctx.enter_context(nc.psum_tensor(*a))
        sem = lambda n: ctx.enter_context(nc.semaphore(n))
        xt = sb("xt", [128, D], F32)
        dnb = sb("dnb", [128, D], BF16)
        dn32 = sb("dn32", [128, D], F32)
        x4t = sb("x4t", [128, 2, D], F32)
        sq = sb("sq", [128, D], F32)
        sq2 = sb("sq2", [128, D], F32)
        stat = sb("stat", [128, 4], F32)
        xnt = sb("xnt", [128, 2, D], BF16)
        cst = sb("cst", [128, 2], F32)
        bar = sb("bar", [128, 1], F32)
        sbar = sb("sbar", [1, 2], F32)
        xfm = sb("xfm", [128, KT, TOK], BF16)
        wA = sb("wA", [128, KT, 256], BF16)
        wB = sb("wB", [128, KT, 256], BF16)
        wC = sb("wC", [128, KH, 256], BF16)
        hst = sb("hst", [128, 2, 4, 512], BF16)
        hbuf = sb("hbuf", [128, 2, TOK], BF16)
        evg = sb("evg", [128, 2, 4, 512], BF16)
        sgt = sb("sgt", [128, 512], F32)
        ps = pt("ps", [128, 8, 512], F32)
        s_c = sem("s_c")
        s_lx = sem("s_lx")
        s_xs1 = sem("s_xs1")
        s_xq1 = sem("s_xq1")
        s_ln = sem("s_ln")
        s_lns = sem("s_lns")
        s_tf1 = sem("s_tf1")
        s_w = sem("s_w")
        s_mm = sem("s_mm")
        s_sg = sem("s_sg")
        s_hv = sem("s_hv")
        s_hs = sem("s_hs")
        s_hq = sem("s_hq")
        s_hl0 = sem("s_hl0")
        s_hl1 = sem("s_hl1")
        s_mdk = sem("s_mdk")
        s_dv = sem("s_dv")
        s_ds = sem("s_ds")
        s_x2 = sem("s_x2")
        s_xv = sem("s_xv")
        s_x2q = sem("s_x2q")
        s_x2s = sem("s_x2s")
        block = ctx.enter_context(nc.Block())

        def ln_vector(vector, in_ap, nn):
            vector.reduce_sum(out=stat[:, 0:1], in_=in_ap, axis=AX.X)
            vector.drain()
            vector.tensor_scalar(out=stat[:, 1:2], in0=stat[:, 0:1], scalar1=-1.0 / D,
                                 scalar2=None, op0=ALU.mult, op1=ALU.bypass)
            vector.drain()
            vector.tensor_scalar(out=sq[:], in0=in_ap, scalar1=stat[:, 1:2],
                                 scalar2=None, op0=ALU.add, op1=ALU.bypass)
            vector.drain()
            vector.memset(bar[:, 0:1], 0.0).then_inc(s_ln, 1)
            vector.wait_ge(s_lns, nn + 1)
            vector.reciprocal(out=stat[:, 3:4], in_=stat[:, 2:3])
            vector.drain()
            vector.tensor_scalar(out=xnt[:, nn % 2, :], in0=sq[:], scalar1=stat[:, 3:4],
                                 scalar2=None, op0=ALU.mult, op1=ALU.bypass)
            vector.drain()
            vector.memset(bar[:, 0:1], 0.0).then_inc(s_ln, 1)

        def ln_scalar(scalar, nn):
            scalar.wait_ge(s_ln, 2 * nn + 1)
            scalar.activation(out=sq2[:], in_=sq[:], func=AF.Square, accum_out=stat[:, 0:1])
            scalar.drain()
            scalar.activation(out=stat[:, 2:3], in_=stat[:, 0:1], func=AF.Sqrt,
                              scale=1.0 / D, bias=cst[:, 0:1])
            scalar.drain()
            scalar.copy(out=bar[0:1, 0:1], in_=cst[0:1, 0:1]).then_inc(s_lns, 1)

        @block.sync
        def _(sync):
            sync.dma_start(out=cst[:], in_=cst_in.partition_broadcast(128)).then_inc(s_c, 16)
            for n in range(NT):
                if n >= 1:
                    sync.wait_ge(s_ln, 2 * n)
                sync.dma_start(out=xt[:], in_=x3_in[n * 128:(n + 1) * 128, :]).then_inc(s_lx, 16)
                sync.wait_ge(s_ln, 2 * n + 2)
                sync.dma_start(out=xn3_scr[n * 128:(n + 1) * 128, :],
                               in_=xnt[:, n % 2, :]).then_inc(s_xs1, 16)
                sync.wait_ge(s_xs1, 16 * (n + 1))
                sync.sem_inc(s_xq1, 1)
            for k in range(KT):
                sync.dma_start(out=xfm[:, k, :], in_=xn3_scr[:, k * 128:(k + 1) * 128],
                               transpose=True).then_inc(s_tf1, 16)
            for mp in range(MPH):
                pb = mp % 2
                for i in range(4):
                    mm, j = divmod(i, 2)
                    m = mp * 2 + mm
                    sync.wait_ge(s_hv, 4 * mp + i + 1)
                    sync.dma_start(out=h_scr[m * 128:(m + 1) * 128, j * 512:(j + 1) * 512],
                                   in_=hst[:, pb, i, :]).then_inc(s_hs, 16)
                sync.wait_ge(s_hs, 64 * (mp + 1))
                sync.sem_inc(s_hq, 1)
            for mp in range(MPD):
                pb = mp % 2
                for i in range(4):
                    mm, j = divmod(i, 2)
                    m = mp * 2 + mm
                    sync.wait_ge(s_dv, 4 * mp + i + 1)
                    sync.dma_start(out=dn_scr[m * 128:(m + 1) * 128, j * 512:(j + 1) * 512],
                                   in_=evg[:, pb, i, :]).then_inc(s_ds, 16)
                sync.wait_ge(s_ds, 64 * (mp + 1) + mp)
                sync.sem_inc(s_ds, 1)
            sync.wait_ge(s_ds, 64 * MPD + MPD)
            for n in range(NT):
                if n >= 1:
                    sync.wait_ge(s_xv, 2 * n)
                sync.dma_start(out=xt[:], in_=x3_in[n * 128:(n + 1) * 128, :]).then_inc(s_x2, 16)
                sync.dma_start(out=dnb[:], in_=dn_scr[:, n * 128:(n + 1) * 128],
                               transpose=True).then_inc(s_x2, 16)
                sync.wait_ge(s_xv, 2 * n + 2)
                sync.dma_start(out=x4_out[n * 128:(n + 1) * 128, :],
                               in_=x4t[:, n % 2, :]).then_inc(s_x2s, 16)
                sync.wait_ge(s_x2s, 16 * (n + 1))
                sync.sem_inc(s_x2q, 1)

        @block.gpsimd
        def _(gpsimd):
            for mp in range(MPH):
                if mp >= 1:
                    gpsimd.wait_ge(s_mm, 8 * mp)
                for k in range(KT):
                    gpsimd.dma_start(out=wA[:, k, :],
                                     in_=gwT_in[k * 128:(k + 1) * 128, mp * 256:(mp + 1) * 256]
                                     ).then_inc(s_w, 16)
                    gpsimd.dma_start(out=wB[:, k, :],
                                     in_=uwT_in[k * 128:(k + 1) * 128, mp * 256:(mp + 1) * 256]
                                     ).then_inc(s_w, 16)
            gpsimd.wait_ge(s_hq, MPH)
            for mp in range(MPD):
                if mp >= 1:
                    gpsimd.wait_ge(s_dv, 4 * mp)
                for k in range(KH):
                    gpsimd.dma_start(out=wC[:, k, :],
                                     in_=dwT_in[k * 128:(k + 1) * 128, mp * 256:(mp + 1) * 256]
                                     ).then_inc(s_w, 16)
                for k in range(KH):
                    if mp * KH + k >= 2:
                        gpsimd.wait_ge(s_mdk, mp * KH + k - 1)
                    gpsimd.dma_start(out=hbuf[:, k % 2, :],
                                     in_=h_scr[k * 128:(k + 1) * 128, :]
                                     ).then_inc([s_hl0, s_hl1][k % 2], 16)

        @block.tensor
        def _(tensor):
            tensor.wait_ge(s_tf1, 16 * KT)
            for mp in range(MPH):
                tensor.wait_ge(s_w, 16 * 32 * (mp + 1))
                if mp >= 1:
                    tensor.wait_ge(s_hv, 4 * mp)
                for i in range(4):
                    mm, j = divmod(i, 2)
                    for k in range(KT):
                        mg = tensor.matmul(ps[:, i, :], wA[:, k, mm * 128:(mm + 1) * 128],
                                           xfm[:, k, j * 512:(j + 1) * 512],
                                           start=(k == 0), stop=(k == KT - 1))
                    mg.then_inc(s_mm, 1)
                    for k in range(KT):
                        mu = tensor.matmul(ps[:, 4 + i, :], wB[:, k, mm * 128:(mm + 1) * 128],
                                           xfm[:, k, j * 512:(j + 1) * 512],
                                           start=(k == 0), stop=(k == KT - 1))
                    mu.then_inc(s_mm, 1)
            tensor.wait_ge(s_sg, 4 * MPH)
            tensor.wait_ge(s_hv, 4 * MPH)
            W0 = 16 * 32 * MPH
            for mp in range(MPD):
                tensor.wait_ge(s_w, W0 + 16 * KH * (mp + 1))
                if mp >= 1:
                    tensor.wait_ge(s_dv, 4 * mp)
                for k in range(KH):
                    tensor.wait_ge([s_hl0, s_hl1][k % 2], 16 * (mp * 22 + k // 2 + 1))
                    for i in range(4):
                        mm, j = divmod(i, 2)
                        md = tensor.matmul(ps[:, i, :], wC[:, k, mm * 128:(mm + 1) * 128],
                                           hbuf[:, k % 2, j * 512:(j + 1) * 512],
                                           start=(k == 0), stop=(k == KH - 1))
                    md.then_inc(s_mdk, 1)

        @block.vector
        def _(vector):
            vector.wait_ge(s_c, 16)
            for n in range(NT):
                vector.wait_ge(s_lx, 16 * (n + 1))
                if n >= 2:
                    vector.wait_ge(s_xq1, n - 1)
                ln_vector(vector, xt[:], n)
            for mp in range(MPH):
                pb = mp % 2
                for i in range(4):
                    vector.wait_ge(s_sg, 4 * mp + i + 1)
                    if mp >= 2:
                        vector.wait_ge(s_hq, mp - 1)
                    vector.tensor_tensor(out=hst[:, pb, i, :],
                                         in0=sgt[:], in1=ps[:, 4 + i, :], op=ALU.mult)
                    vector.drain()
                    vector.memset(bar[:, 0:1], 0.0).then_inc(s_hv, 1)
            for mp in range(MPD):
                pb = mp % 2
                for i in range(4):
                    vector.wait_ge(s_mdk, (mp + 1) * KH)
                    if mp >= 2:
                        vector.wait_ge(s_ds, 64 * (mp - 1) + (mp - 1))
                    vector.tensor_copy(out=evg[:, pb, i, :], in_=ps[:, i, :])
                    vector.drain()
                    vector.memset(bar[:, 0:1], 0.0).then_inc(s_dv, 1)
            for n in range(NT):
                vector.wait_ge(s_x2, 32 * n + 32)
                if n >= 2:
                    vector.wait_ge(s_x2q, n - 1)
                vector.tensor_copy(out=dn32[:], in_=dnb[:])
                vector.drain()
                vector.memset(bar[:, 0:1], 0.0).then_inc(s_xv, 1)
                vector.tensor_tensor(out=x4t[:, n % 2, :], in0=xt[:], in1=dn32[:],
                                     op=ALU.add)
                vector.drain()
                vector.memset(bar[:, 0:1], 0.0).then_inc(s_xv, 1)

        @block.scalar
        def _(scalar):
            scalar.wait_ge(s_c, 16)
            for n in range(NT):
                ln_scalar(scalar, n)
            for mp in range(MPH):
                for i in range(4):
                    scalar.wait_ge(s_mm, 8 * mp + 2 * i + 2)
                    if 4 * mp + i >= 1:
                        scalar.wait_ge(s_hv, 4 * mp + i)
                    scalar.activation(out=sgt[:], in_=ps[:, i, :], func=AF.Silu)
                    scalar.drain()
                    scalar.copy(out=sbar[0:1, 0:1], in_=cst[0:1, 0:1]).then_inc(s_sg, 1)

    return nc


def _padrows(a, rows):
    out = np.zeros((rows,) + a.shape[1:], a.dtype)
    out[:a.shape[0]] = a
    return out


STRIPE_BLOCKS = {c: [c, 15 - c, 16 + c, 31 - c] for c in range(NCORES)}


def _stripe_rows(c):
    rows = []
    for b in range(B):
        for blk in STRIPE_BLOCKS[c]:
            for p in range(128):
                rows.append(b * L + blk * 128 + p)
    return np.array(rows)


def run_L7a(inputs, x2, xn2, Kf, Vf):
    nc = build_L7a()
    K_full = np.ascontiguousarray(Kf.reshape(D, B * L))
    V_full = np.ascontiguousarray(Kf.reshape(D, B * L).T) if False else \
        np.ascontiguousarray(Vf.reshape(D, B * L).T)
    wqT = np.ascontiguousarray(inputs["L1_attn_w_q"].T)
    woT = np.ascontiguousarray(inputs["L1_attn_w_o"].T)
    ident = np.eye(128, dtype=BF)
    cols = np.arange(L)
    x2f = x2.reshape(B * L, D)
    xn2f = xn2.reshape(B * L, D)
    in_maps = []
    for c in range(NCORES):
        rows = _stripe_rows(c)
        qpos = np.concatenate([np.arange(blk * 128, (blk + 1) * 128)
                               for _ in range(B) for blk in [0]] * 0) if False else None
        qp = []
        for b in range(B):
            for blk in STRIPE_BLOCKS[c]:
                qp.append(np.arange(blk * 128, (blk + 1) * 128))
        qp = np.concatenate(qp)
        mask = np.where(cols[None, :] > qp[:, None], np.float32(-30000.0),
                        np.float32(0.0)).astype(BF)
        in_maps.append(dict(
            xn2s_in=np.ascontiguousarray(xn2f[rows]),
            x2s_in=np.ascontiguousarray(x2f[rows]),
            K_in=K_full, V_in=V_full, wqT_in=wqT, woT_in=woT,
            mask_in=np.ascontiguousarray(mask), ident_in=ident))
    res = run_bass_kernel_spmd(nc, in_maps, core_ids=list(range(NCORES)))
    return [r["x3s_out"] for r in res.results]


def run_L7b(inputs, x3s_list, gw, uw, dw):
    nc = build_L7b()
    gwT = np.zeros((D, HB2), BF)
    gwT[:, :HGHOST] = gw.T
    uwT = np.zeros((D, HB2), BF)
    uwT[:, :HGHOST] = uw.T
    dwT = np.zeros((HB2, D), BF)
    dwT[:HGHOST, :] = dw.T
    cstv = np.array([[EPS_LN, 1.0]], np.float32)
    in_maps = []
    for c in range(NCORES):
        in_maps.append(dict(x3_in=np.ascontiguousarray(x3s_list[c]),
                            gwT_in=gwT, uwT_in=uwT, dwT_in=dwT, cst_in=cstv))
    res = run_bass_kernel_spmd(nc, in_maps, core_ids=list(range(NCORES)))
    x4 = np.empty((B * L, D), np.float32)
    for c, r in enumerate(res.results):
        x4[_stripe_rows(c)] = r["x4_out"]
    return x4.reshape(B, L, D)


def kernel(**inputs):
    inputs = {k: np.asarray(v) for k, v in inputs.items()}
    ln0, xz = run_L1(inputs)
    xc, xdbl_parts, gparts = run_L2(inputs, xz)
    y = run_L3(inputs, xz, xc, xdbl_parts)
    x1, qw1, qw2, qw3 = run_L4(inputs, y, ln0, gparts)
    x2, xn2, Kf, Vf = run_L5(inputs, x1, qw1, qw2, qw3)
    gw, uw, dw = run_L6(inputs)
    x3s_list = run_L7a(inputs, x2, xn2, Kf, Vf)
    x4 = run_L7b(inputs, x3s_list, gw, uw, dw)
    return x4


# version 04 (final)
# speedup vs baseline: 1.0155x; 1.0004x over previous
import contextlib
import numpy as np
import ml_dtypes
import concourse.bass as bass
import concourse.mybir as mybir
from concourse.bass_utils import run_bass_kernel_spmd

F32 = mybir.dt.float32
BF16 = mybir.dt.bfloat16
I32 = mybir.dt.int32
AF = mybir.ActivationFunctionType
ALU = mybir.AluOpType
AX = mybir.AxisListType

NCORES = 8
B = 2
L = 4096
D = 2048
DIN = 4096
DST = 16
DTR = 128
HEADS = 16
HD = 128
HBIT = 5632
HGHOST = 5504
TC = 512
TOK = B * TC
EPS_LN = 1e-5
EPS_BIT = 1e-8

BF = ml_dtypes.bfloat16


def build_L1():
    nc = bass.Bass("TRN2", target_bir_lowering=False, debug=False)
    x_in = nc.dram_tensor("x_in", [TOK, D], F32, kind="ExternalInput").ap()
    nsc_in = nc.dram_tensor("nsc_in", [1, D], F32, kind="ExternalInput").ap()
    winT_in = nc.dram_tensor("winT_in", [D, 2 * DIN], F32, kind="ExternalInput").ap()
    cst_in = nc.dram_tensor("cst_in", [1, 2], F32, kind="ExternalInput").ap()

    ln0_out = nc.dram_tensor("ln0_out", [TOK, D], F32, kind="ExternalOutput").ap()
    xz_out = nc.dram_tensor("xz_out", [2 * DIN, TOK], F32, kind="ExternalOutput").ap()

    xn_scr = nc.dram_tensor("xn_scr", [TOK, D], BF16).ap()

    NT = TOK // 128
    KT = D // 128
    MT = (2 * DIN) // 128

    with (
        nc.sbuf_tensor("xt", [128, NT, D], F32) as xt,
        nc.sbuf_tensor("nsc", [128, D], F32) as nsc,
        nc.sbuf_tensor("cst", [128, 2], F32) as cst,
        nc.sbuf_tensor("sq", [128, D], F32) as sq,
        nc.sbuf_tensor("stat", [128, NT, 6], F32) as stat,
        nc.sbuf_tensor("lnt", [128, NT, D], F32) as lnt,
        nc.sbuf_tensor("xnt", [128, D], BF16) as xnt,
        nc.sbuf_tensor("xnf", [128, KT, TOK], BF16) as xnf,
        nc.sbuf_tensor("wb", [128, 2, KT, 128], BF16) as wb,
        nc.sbuf_tensor("ev", [128, 2, 512], F32) as ev,
        nc.psum_tensor("ps", [128, 2, 512], F32) as ps,
        nc.semaphore("s_ld") as s_ld,
        nc.semaphore("s_v") as s_v,
        nc.semaphore("s_s") as s_s,
        nc.semaphore("s_t") as s_t,
        nc.semaphore("s_wl") as s_wl,
        nc.semaphore("s_wl1") as s_wl1,
        nc.semaphore("s_xf") as s_xf,
        nc.semaphore("s_st") as s_st,
        nc.Block() as block,
    ):
        @block.sync
        def _(sync):
            sync.dma_start(out=xt[:], in_=x_in.rearrange("(n p) d -> p n d", p=128)).then_inc(s_ld, 16)
            sync.dma_start(out=nsc[:], in_=nsc_in.partition_broadcast(128)).then_inc(s_ld, 16)
            sync.dma_start(out=cst[:], in_=cst_in.partition_broadcast(128)).then_inc(s_ld, 16)
            sync.wait_ge(s_st, 16 * NT)
            for k in range(KT):
                sync.dma_start(out=xnf[:, k, :], in_=xn_scr[:, k * 128:(k + 1) * 128],
                               transpose=True).then_inc(s_xf, 16)

        @block.vector
        def _(vector):
            vector.wait_ge(s_ld, 48)
            for n in range(NT):
                vector.reduce_sum(out=stat[:, n, 0:1], in_=xt[:, n, :], axis=AX.X)
                vector.drain()
                vector.tensor_scalar(out=stat[:, n, 1:2], in0=stat[:, n, 0:1],
                                     scalar1=-1.0 / D, scalar2=None, op0=ALU.mult, op1=ALU.bypass)
                vector.drain()
                vector.tensor_scalar(out=lnt[:, n, :], in0=xt[:, n, :],
                                     scalar1=stat[:, n, 1:2], scalar2=None,
                                     op0=ALU.add, op1=ALU.bypass).then_inc(s_v, 1)
                vector.wait_ge(s_s, 2 * n + 1)
                vector.reciprocal(out=stat[:, n, 3:4], in_=stat[:, n, 2:3])
                vector.drain()
                vector.tensor_scalar(out=lnt[:, n, :], in0=lnt[:, n, :],
                                     scalar1=stat[:, n, 3:4], scalar2=None,
                                     op0=ALU.mult, op1=ALU.bypass).then_inc(s_v, 1)
                vector.wait_ge(s_s, 2 * n + 2)
                vector.reciprocal(out=stat[:, n, 5:6], in_=stat[:, n, 4:5])
                vector.drain()
                vector.tensor_scalar(out=sq[:], in0=lnt[:, n, :],
                                     scalar1=stat[:, n, 5:6], scalar2=None,
                                     op0=ALU.mult, op1=ALU.bypass)
                vector.drain()
                vector.tensor_tensor(out=xnt[:], in0=sq[:], in1=nsc[:],
                                     op=ALU.mult).then_inc(s_v, 1)
                if n < NT - 1:
                    vector.wait_ge(s_st, 16 * (n + 1))
            for m in range(MT):
                for j in range(2):
                    i = m * 2 + j
                    vector.wait_ge(s_t, i + 1)
                    if i >= 2:
                        vector.wait_ge(s_st, 16 * (NT + i))
                    vector.tensor_copy(out=ev[:, i % 2, :], in_=ps[:, j, :]).then_inc(s_v, 1)

        @block.scalar
        def _(scalar):
            scalar.wait_ge(s_ld, 48)
            for n in range(NT):
                scalar.wait_ge(s_v, 3 * n + 1)
                scalar.activation(out=sq[:], in_=lnt[:, n, :], func=AF.Square,
                                  accum_out=stat[:, n, 0:1])
                scalar.drain()
                scalar.activation(out=stat[:, n, 2:3], in_=stat[:, n, 0:1], func=AF.Sqrt,
                                  scale=1.0 / D, bias=cst[:, 0:1]).then_inc(s_s, 1)
                scalar.wait_ge(s_v, 3 * n + 2)
                scalar.activation(out=sq[:], in_=lnt[:, n, :], func=AF.Square,
                                  accum_out=stat[:, n, 0:1])
                scalar.drain()
                scalar.activation(out=stat[:, n, 4:5], in_=stat[:, n, 0:1], func=AF.Sqrt,
                                  scale=1.0 / D, bias=cst[:, 0:1]).then_inc(s_s, 1)

        @block.gpsimd
        def _(gpsimd):
            for n in range(NT):
                gpsimd.wait_ge(s_v, 3 * n + 3)
                gpsimd.dma_start(out=xn_scr[n * 128:(n + 1) * 128, :], in_=xnt[:]).then_inc(s_st, 16)
            for m in range(MT):
                pb = m % 2
                if m >= 2:
                    gpsimd.wait_ge(s_t, (m - 1) * 2)
                for k in range(KT):
                    gpsimd.dma_start(out=wb[:, pb, k, :],
                                     in_=winT_in[k * 128:(k + 1) * 128, m * 128:(m + 1) * 128]
                                     ).then_inc([s_wl, s_wl1][m % 2], 16)

        @block.tensor
        def _(tensor):
            tensor.wait_ge(s_xf, 16 * KT)
            for m in range(MT):
                pb = m % 2
                tensor.wait_ge([s_wl, s_wl1][m % 2], 16 * KT * (m // 2 + 1))
                for j in range(2):
                    if m >= 1:
                        tensor.wait_ge(s_v, 3 * NT + (m - 1) * 2 + j + 1)
                    for k in range(KT):
                        mm = tensor.matmul(ps[:, j, :], wb[:, pb, k, :],
                                           xnf[:, k, j * 512:(j + 1) * 512],
                                           start=(k == 0), stop=(k == KT - 1))
                    mm.then_inc(s_t, 1)

        @block.sync
        def _(sync):
            sync.wait_ge(s_v, 3 * NT)
            sync.dma_start(out=ln0_out.rearrange("(n p) d -> p n d", p=128), in_=lnt[:]).then_inc(s_st, 16)
            for m in range(MT):
                for j in range(2):
                    sync.wait_ge(s_v, 3 * NT + m * 2 + j + 1)
                    sync.dma_start(out=xz_out[m * 128:(m + 1) * 128, j * 512:(j + 1) * 512],
                                   in_=ev[:, (m * 2 + j) % 2, :]).then_inc(s_st, 16)
            sync.wait_ge(s_st, 16 * (NT + 1 + MT * 2))
    return nc


def run_L1(inputs):
    x = inputs["L0_attn_hidden_states"]
    nsc = np.ascontiguousarray(inputs["L0_attn_norm_scale"].reshape(1, D))
    winT = np.ascontiguousarray(inputs["L0_attn_in_proj_w"].T)
    nc = build_L1()
    in_maps = []
    for c in range(NCORES):
        xc = np.ascontiguousarray(
            x[:, c * TC:(c + 1) * TC, :].reshape(TOK, D))
        in_maps.append(dict(x_in=xc, nsc_in=nsc, winT_in=winT,
                            cst_in=np.array([[EPS_LN, 1.0]], np.float32)))
    res = run_bass_kernel_spmd(nc, in_maps, core_ids=list(range(NCORES)))
    ln0 = np.empty((B, L, D), np.float32)
    xz = np.empty((2 * DIN, B, L), np.float32)
    for c, r in enumerate(res.results):
        ln0[:, c * TC:(c + 1) * TC, :] = r["ln0_out"].reshape(B, TC, D)
        xz[:, :, c * TC:(c + 1) * TC] = r["xz_out"].reshape(2 * DIN, B, TC)
    return ln0, xz


LANES = 1024
NLG = LANES // 128


def build_L2():
    nc = bass.Bass("TRN2", target_bir_lowering=False, debug=False)
    xin_in = nc.dram_tensor("xin_in", [LANES, L], F32, kind="ExternalInput").ap()
    cw_in = nc.dram_tensor("cw_in", [128, NLG, 4], F32, kind="ExternalInput").ap()
    cb_in = nc.dram_tensor("cb_in", [128, NLG], F32, kind="ExternalInput").ap()
    xpwT_in = nc.dram_tensor("xpwT_in", [512, 160], F32, kind="ExternalInput").ap()
    w1_in = nc.dram_tensor("w1_in", [768, D], F32, kind="ExternalInput").ap()
    w2_in = nc.dram_tensor("w2_in", [768, D], F32, kind="ExternalInput").ap()
    w3s_in = nc.dram_tensor("w3s_in", [256, HBIT], F32, kind="ExternalInput").ap()

    xc_out = nc.dram_tensor("xc_out", [LANES, L], F32, kind="ExternalOutput").ap()
    xdbl_out = nc.dram_tensor("xdbl_out", [B, 160, L], F32, kind="ExternalOutput").ap()
    gp_out = nc.dram_tensor("gp_out", [1, 3], F32, kind="ExternalOutput").ap()

    NB = L // 512

    with contextlib.ExitStack() as ctx:
        sb = lambda *a: ctx.enter_context(nc.sbuf_tensor(*a))
        pt = lambda *a: ctx.enter_context(nc.psum_tensor(*a))
        sem = lambda n: ctx.enter_context(nc.semaphore(n))
        xpad = sb("xpad", [128, 2, 3 + L], F32)
        acc = sb("acc", [128, 2, L], F32)
        xcf = sb("xcf", [128, 2, L], F32)
        xcb = sb("xcb", [128, NLG, L], BF16)
        cw = sb("cw", [128, NLG, 4], F32)
        cb = sb("cb", [128, NLG], F32)
        xpw = sb("xpw", [128, 4, 160], BF16)
        evA = sb("evA", [128, 2, 512], F32)
        evB = sb("evB", [32, 2, 512], F32)
        gw = sb("gw", [128, 2, HBIT], BF16)
        gtmp = sb("gtmp", [128, 16], F32)
        gcol = sb("gcol", [128, 3], F32)
        ones = sb("ones", [128, 1], F32)
        gp = sb("gp", [1, 3], F32)
        psA = pt("psA", [128, 2, 512], F32)
        psB = pt("psB", [32, 2, 512], F32)
        psg = pt("psg", [1, 3], F32)
        s_ld = sem("s_ld"); s_c = sem("s_c"); s_v = sem("s_v")
        s_sl = sem("s_sl"); s_xs = sem("s_xs"); s_mm = sem("s_mm")
        s_gl = sem("s_gl"); s_gv = sem("s_gv"); s_st = sem("s_st")
        block = ctx.enter_context(nc.Block())
        @block.sync
        def _(sync):
            sync.dma_start(out=cw[:], in_=cw_in[:]).then_inc(s_c, 16)
            sync.dma_start(out=cb[:], in_=cb_in[:]).then_inc(s_c, 16)
            for g in range(NLG):
                pb = g % 2
                if g >= 2:
                    sync.wait_ge(s_v, 2 * (g - 2) + 1)
                sync.dma_start(out=xpad[:, pb, 3:3 + L],
                               in_=xin_in[g * 128:(g + 1) * 128, :]).then_inc(s_ld, 16)

        @block.gpsimd
        def _(gpsimd):
            for k in range(4):
                gpsimd.dma_start(out=xpw[:, k, :],
                                 in_=xpwT_in[k * 128:(k + 1) * 128, :]).then_inc(s_c, 16)
            for g in range(NLG):
                gpsimd.wait_ge(s_sl, g + 1)
                gpsimd.dma_start(out=xc_out[g * 128:(g + 1) * 128, :],
                                 in_=xcf[:, g % 2, :]).then_inc(s_xs, 16)
            gsrcs = ([w1_in[i * 128:(i + 1) * 128, :] for i in range(6)]
                     + [w2_in[i * 128:(i + 1) * 128, :] for i in range(6)]
                     + [w3s_in[i * 128:(i + 1) * 128, :] for i in range(2)])
            for i, srcap in enumerate(gsrcs):
                if i >= 2:
                    gpsimd.wait_ge(s_gv, i - 1)
                wid = srcap.shape[1]
                gpsimd.dma_start(out=gw[:, i % 2, 0:wid], in_=srcap).then_inc(s_gl, 16)

        @block.vector
        def _(vector):
            vector.wait_ge(s_c, 32)
            vector.memset(ones[:], 1.0)
            for g in range(NLG):
                pb = g % 2
                vector.memset(xpad[:, pb, 0:3], 0.0)
                vector.wait_ge(s_ld, 16 * (g + 1))
                vector.tensor_scalar(out=acc[:, 0, :], in0=xpad[:, pb, 0:L],
                                     scalar1=cw[:, g, 0:1], scalar2=None,
                                     op0=ALU.mult, op1=ALU.bypass)
                vector.drain()
                vector.scalar_tensor_tensor(out=acc[:, 1, :], in0=xpad[:, pb, 1:1 + L],
                                            scalar=cw[:, g, 1:2], in1=acc[:, 0, :],
                                            op0=ALU.mult, op1=ALU.add)
                vector.drain()
                vector.scalar_tensor_tensor(out=acc[:, 0, :], in0=xpad[:, pb, 2:2 + L],
                                            scalar=cw[:, g, 2:3], in1=acc[:, 1, :],
                                            op0=ALU.mult, op1=ALU.add)
                vector.drain()
                vector.scalar_tensor_tensor(out=acc[:, 1, :], in0=xpad[:, pb, 3:3 + L],
                                            scalar=cw[:, g, 3:4], in1=acc[:, 0, :],
                                            op0=ALU.mult, op1=ALU.add).then_inc(s_v, 1)
                vector.wait_ge(s_sl, g + 1)
                vector.tensor_copy(out=xcb[:, g, :], in_=xcf[:, pb, :]).then_inc(s_v, 1)
            nev = 0
            for i in range(B * NB):
                vector.wait_ge(s_mm, 2 * i + 2)
                pb = i % 2
                if i >= 2:
                    vector.wait_ge(s_st, 16 * (2 * (i - 2) + 2))
                vector.tensor_copy(out=evA[:, pb, :], in_=psA[:, pb, :]).then_inc(s_v, 1)
                vector.tensor_copy(out=evB[:, pb, :], in_=psB[:, pb, :]).then_inc(s_v, 1)
                nev += 2
            for i in range(14):
                vector.wait_ge(s_gl, 16 * (i + 1))
                wid = HBIT if i >= 12 else D
                col = i if i < 12 else 12 + (i - 12)
                vector.tensor_reduce(out=gtmp[:, col:col + 1], in_=gw[:, i % 2, 0:wid],
                                     axis=AX.X, op=ALU.add,
                                     apply_absolute_value=True).then_inc(s_gv, 1)
            vector.drain()
            vector.tensor_reduce(out=gcol[:, 0:1], in_=gtmp[:, 0:6], axis=AX.X, op=ALU.add)
            vector.tensor_reduce(out=gcol[:, 1:2], in_=gtmp[:, 6:12], axis=AX.X, op=ALU.add)
            vector.tensor_reduce(out=gcol[:, 2:3], in_=gtmp[:, 12:14], axis=AX.X,
                                 op=ALU.add).then_inc(s_gv, 1)
            vector.wait_ge(s_mm, 2 * B * NB + 1)
            vector.tensor_copy(out=gp[:], in_=psg[:]).then_inc(s_gv, 1)

        @block.scalar
        def _(scalar):
            scalar.wait_ge(s_c, 32)
            for g in range(NLG):
                pb = g % 2
                scalar.wait_ge(s_v, 2 * g + 1)
                if g >= 2:
                    scalar.wait_ge(s_xs, 16 * (g - 1))
                scalar.activation(out=xcf[:, pb, :], in_=acc[:, 1, :], func=AF.Silu,
                                  bias=cb[:, g:g + 1]).then_inc(s_sl, 1)

        @block.tensor
        def _(tensor):
            tensor.wait_ge(s_c, 16 * 6)
            tensor.wait_ge(s_v, 2 * NLG)
            for i in range(B * NB):
                b, n = divmod(i, NB)
                pb = i % 2
                if i >= 2:
                    tensor.wait_ge(s_v, 2 * NLG + 2 * (i - 2) + 2)
                for k in range(4):
                    mmA = tensor.matmul(psA[:, pb, :], xpw[:, k, 0:128],
                                        xcb[:, b * 4 + k, n * 512:(n + 1) * 512],
                                        start=(k == 0), stop=(k == 3))
                mmA.then_inc(s_mm, 1)
                for k in range(4):
                    mmB = tensor.matmul(psB[:, pb, :], xpw[:, k, 128:160],
                                        xcb[:, b * 4 + k, n * 512:(n + 1) * 512],
                                        start=(k == 0), stop=(k == 3))
                mmB.then_inc(s_mm, 1)
            tensor.wait_ge(s_gv, 15)
            for j in range(3):
                mg = tensor.matmul(psg[0:1, j:j + 1], ones[:], gcol[:, j:j + 1],
                                   start=True, stop=True)
            mg.then_inc(s_mm, 1)

        @block.sync
        def _(sync):
            for i in range(B * NB):
                b, n = divmod(i, NB)
                pb = i % 2
                sync.wait_ge(s_v, 2 * NLG + 2 * i + 1)
                sync.dma_start(out=xdbl_out[b, 0:128, n * 512:(n + 1) * 512],
                               in_=evA[:, pb, :]).then_inc(s_st, 16)
                sync.wait_ge(s_v, 2 * NLG + 2 * i + 2)
                sync.dma_start(out=xdbl_out[b, 128:160, n * 512:(n + 1) * 512],
                               in_=evB[:, pb, :]).then_inc(s_st, 16)
            sync.wait_ge(s_gv, 16)
            sync.dma_start(out=gp_out[:], in_=gp[:]).then_inc(s_st, 16)
            sync.wait_ge(s_st, 16 * (2 * B * NB + 1))
            sync.wait_ge(s_xs, 16 * NLG)

    return nc


def run_L2(inputs, xz):
    cwf = inputs["L0_attn_conv_w"][:, 0, :]
    nc = build_L2()
    in_maps = []
    xpwT = np.ascontiguousarray(inputs["L0_attn_x_proj_w"].T)
    w1 = inputs["L0_ffn_w1"]
    w2 = inputs["L0_ffn_w2"]
    w3 = inputs["L0_ffn_w3"]
    pad = np.zeros((64, D), np.float32)
    for c in range(NCORES):
        ch = slice(c * 512, (c + 1) * 512)
        xin = np.ascontiguousarray(xz[ch, :, :].transpose(1, 0, 2).reshape(LANES, L))
        cw_c = np.ascontiguousarray(
            np.concatenate([cwf[ch]] * B, 0).reshape(NLG, 128, 4).transpose(1, 0, 2))
        cb_c = np.ascontiguousarray(
            np.concatenate([inputs["L0_attn_conv_b"][ch]] * B).reshape(NLG, 128).T)
        w1_c = np.concatenate([w1[c * 704:(c + 1) * 704], pad], 0)
        w2_c = np.concatenate([w2[c * 704:(c + 1) * 704], pad], 0)
        w3_c = np.ascontiguousarray(w3[c * 256:(c + 1) * 256])
        in_maps.append(dict(xin_in=xin, cw_in=cw_c, cb_in=cb_c,
                            xpwT_in=np.ascontiguousarray(xpwT[ch]),
                            w1_in=np.ascontiguousarray(w1_c),
                            w2_in=np.ascontiguousarray(w2_c), w3s_in=w3_c))
    res = run_bass_kernel_spmd(nc, in_maps, core_ids=list(range(NCORES)))
    xc = np.empty((B, DIN, L), np.float32)
    xdbl_parts = np.empty((NCORES, B, 160, L), np.float32)
    gparts = np.empty((NCORES, 1, 3), np.float32)
    for c, r in enumerate(res.results):
        xc[:, c * 512:(c + 1) * 512, :] = r["xc_out"].reshape(B, 512, L)
        xdbl_parts[c] = r["xdbl_out"]
        gparts[c] = r["gp_out"]
    return xc, xdbl_parts, gparts


TCH = 128
NTC = L // TCH


def build_L3():
    nc = bass.Bass("TRN2", target_bir_lowering=False, debug=False)
    xc_in = nc.dram_tensor("xc_in", [LANES, L], F32, kind="ExternalInput").ap()
    z_in = nc.dram_tensor("z_in", [LANES, L], F32, kind="ExternalInput").ap()
    xdblp_in = nc.dram_tensor("xdblp_in", [NCORES, B, 160, L], F32, kind="ExternalInput").ap()
    dtwT_in = nc.dram_tensor("dtwT_in", [128, 512], F32, kind="ExternalInput").ap()
    dtb_in = nc.dram_tensor("dtb_in", [128, 4], F32, kind="ExternalInput").ap()
    A_in = nc.dram_tensor("A_in", [128, 4, DST], F32, kind="ExternalInput").ap()
    D_in = nc.dram_tensor("D_in", [128, 4], F32, kind="ExternalInput").ap()
    cst_in = nc.dram_tensor("cst_in", [1, 2], F32, kind="ExternalInput").ap()

    y_out = nc.dram_tensor("y_out", [LANES, L], F32, kind="ExternalOutput").ap()
    bc_scr = nc.dram_tensor("bc_scr", [32, B, L], BF16).ap()

    NG = 4
    NB = L // 512
    HL = L // 2

    with contextlib.ExitStack() as ctx:
        sb = lambda *a: ctx.enter_context(nc.sbuf_tensor(*a))
        pt = lambda *a: ctx.enter_context(nc.psum_tensor(*a))
        sem = lambda n: ctx.enter_context(nc.semaphore(n))
        pload = sb("pload", [128, 2, HL], F32)
        bcload = sb("bcload", [32, 2, HL], F32)
        dtl = sb("dtl", [128, L], F32)
        bca = sb("bca", [32, L], F32)
        dtlb = sb("dtlb", [128, L], BF16)
        dtw = sb("dtw", [128, 512], BF16)
        dtb = sb("dtb", [128, 4], F32)
        negA = sb("negA", [128, NG, DST], F32)
        Dt = sb("Dt", [128, NG], F32)
        cst = sb("cst", [128, 2], F32)
        delta = sb("delta", [128, NG, L], BF16)
        xcb = sb("xcb", [128, NG, L], BF16)
        etile = sb("etile", [128, 1, 512], F32)
        Bbc = sb("Bbc", [128, DST, TCH], F32)
        Cbc = sb("Cbc", [128, DST, TCH], F32)
        dA = sb("dA", [128, DST, TCH], F32)
        bArg = sb("bArg", [128, DST, TCH], BF16)
        du = sb("du", [128, TCH], BF16)
        hsb = sb("hsb", [128, NG, DST, TCH], F32)
        hc = sb("hc", [128, DST, TCH], BF16)
        yt = sb("yt", [128, TCH], F32)
        y2 = sb("y2", [128, TCH], F32)
        y3 = sb("y3", [128, 2, TCH], F32)
        zt = sb("zt", [128, 2, TCH], F32)
        sz = sb("sz", [128, TCH], F32)
        sbar = sb("sbar", [1, 2], F32)
        ps = pt("ps", [128, 2, 512], F32)
        s_c = sem("s_c")
        s_p = sem("s_p")
        s_q = sem("s_q")
        s_va = sem("s_va")
        s_vb = sem("s_vb")
        s_dl = sem("s_dl")
        s_bs = sem("s_bs")
        s_mm = sem("s_mm")
        s_sp = sem("s_sp")
        s_xc = sem("s_xc")
        s_bc = sem("s_bc")
        s_z = sem("s_z")
        s_da = sem("s_da")
        s_sl = sem("s_sl")
        s_vy = sem("s_vy")
        s_ys = sem("s_ys")
        s_neg = sem("s_neg")
        block = ctx.enter_context(nc.Block())

        NIT = NTC * NG

        @block.sync
        def _(sync):
            for b in range(B):
                for j in range(NCORES):
                    for h in range(2):
                        k = j * 2 + h
                        pb = k % 2
                        if k >= 2:
                            sync.wait_ge(s_va, b * 16 + k - 1)
                        sync.dma_start(out=pload[:, pb, :],
                                       in_=xdblp_in[j, b, 0:128, h * HL:(h + 1) * HL]
                                       ).then_inc(s_p, 16)
                        if k >= 2:
                            sync.wait_ge(s_vb, b * 17 + k - 1)
                        sync.dma_start(out=bcload[:, pb, :],
                                       in_=xdblp_in[j, b, 128:160, h * HL:(h + 1) * HL]
                                       ).then_inc(s_q, 16)
                sync.wait_ge(s_bs, 16 * (b + 1))
                for tc in range(NTC):
                    jj = b * NTC + tc
                    for g in range(NG):
                        it = jj * NG + g
                        if it >= 2:
                            sync.wait_ge(s_sl, it - 1)
                        sync.dma_start(out=zt[:, it % 2, :],
                                       in_=z_in[(b * NG + g) * 128:(b * NG + g + 1) * 128,
                                                tc * TCH:(tc + 1) * TCH]).then_inc(s_z, 16)

        @block.gpsimd
        def _(gpsimd):
            gpsimd.dma_start(out=dtw[:], in_=dtwT_in[:]).then_inc(s_c, 16)
            gpsimd.dma_start(out=dtb[:], in_=dtb_in[:]).then_inc(s_c, 16)
            gpsimd.dma_start(out=negA[:], in_=A_in[:]).then_inc(s_c, 16)
            gpsimd.dma_start(out=Dt[:], in_=D_in[:]).then_inc(s_c, 16)
            gpsimd.dma_start(out=cst[:], in_=cst_in.partition_broadcast(128)).then_inc(s_c, 16)
            for b in range(B):
                gpsimd.wait_ge(s_vb, b * 17 + 17)
                gpsimd.dma_start(out=bc_scr[:, b, :], in_=bca[:]).then_inc(s_bs, 16)
                if b > 0:
                    gpsimd.wait_ge(s_vy, b * NIT)
                for g in range(NG):
                    gpsimd.dma_start(out=xcb[:, g, :],
                                     in_=xc_in[(b * NG + g) * 128:(b * NG + g + 1) * 128, :]
                                     ).then_inc(s_xc, 16)
                for it0 in range(NIT):
                    it = b * NIT + it0
                    tc, g = divmod(it0, NG)
                    if g == 0:
                        jj = b * NTC + tc
                        if jj > 0:
                            gpsimd.wait_ge(s_vy, jj * NG)
                        for s in range(DST):
                            gpsimd.dma_start(
                                out=Bbc[:, s, :],
                                in_=bc_scr[s:s + 1, b, tc * TCH:(tc + 1) * TCH].partition_broadcast(128)
                            ).then_inc(s_bc, 16)
                            gpsimd.dma_start(
                                out=Cbc[:, s, :],
                                in_=bc_scr[DST + s:DST + s + 1, b, tc * TCH:(tc + 1) * TCH].partition_broadcast(128)
                            ).then_inc(s_bc, 16)
                    gpsimd.wait_ge(s_vy, it + 1)
                    gpsimd.dma_start(out=y_out[(b * NG + g) * 128:(b * NG + g + 1) * 128,
                                               tc * TCH:(tc + 1) * TCH],
                                     in_=y3[:, it % 2, :]).then_inc(s_ys, 16)

        @block.vector
        def _(vector):
            vector.wait_ge(s_c, 80)
            vector.wait_ge(s_da, 1)
            vector.tensor_scalar(out=negA[:], in0=negA[:], scalar1=-1.0, scalar2=None,
                                 op0=ALU.mult, op1=ALU.bypass)
            vector.drain()
            vector.memset(yt[:, 0:1], 0.0).then_inc(s_neg, 1)
            for b in range(B):
                if b > 0:
                    vector.wait_ge(s_bs, 16 * b)
                for j in range(NCORES):
                    for h in range(2):
                        k = j * 2 + h
                        pb = k % 2
                        hsl = slice(h * HL, (h + 1) * HL)
                        vector.wait_ge(s_p, 16 * (b * 16 + k) + 16)
                        if j == 0:
                            vector.tensor_copy(out=dtl[:, hsl], in_=pload[:, pb, :]).then_inc(s_va, 1)
                        else:
                            vector.tensor_tensor(out=dtl[:, hsl], in0=dtl[:, hsl],
                                                 in1=pload[:, pb, :], op=ALU.add).then_inc(s_va, 1)
                        vector.wait_ge(s_q, 16 * (b * 16 + k) + 16)
                        if j == 0:
                            vector.tensor_copy(out=bca[:, hsl], in_=bcload[:, pb, :]).then_inc(s_vb, 1)
                        else:
                            vector.tensor_tensor(out=bca[:, hsl], in0=bca[:, hsl],
                                                 in1=bcload[:, pb, :], op=ALU.add).then_inc(s_vb, 1)
                        if j == NCORES - 1 and h == 1:
                            vector.drain()
                            vector.memset(yt[:, 0:1], 0.0).then_inc(s_vb, 1)
                vector.drain()
                vector.tensor_copy(out=dtlb[:], in_=dtl[:])
                vector.drain()
                vector.memset(yt[:, 0:1], 0.0).then_inc(s_dl, 1)
                for it0 in range(NIT):
                    it = b * NIT + it0
                    tc, g = divmod(it0, NG)
                    tsl = slice(tc * TCH, (tc + 1) * TCH)
                    if it0 == 0:
                        vector.wait_ge(s_xc, 16 * (b + 1) * NG)
                        vector.memset(hsb[:], 0.0)
                        vector.drain()
                    vector.wait_ge(s_sp, b * 32 + 32)
                    vector.tensor_tensor(out=du[:], in0=delta[:, g, tsl], in1=xcb[:, g, tsl],
                                         op=ALU.mult)
                    vector.drain()
                    vector.wait_ge(s_bc, 32 * 16 * (b * NTC + tc) + 32 * 16)
                    for s in range(DST):
                        vector.tensor_tensor(out=bArg[:, s, :], in0=du[:], in1=Bbc[:, s, :],
                                             op=ALU.mult)
                    vector.drain()
                    vector.wait_ge(s_da, it + 2)
                    for s in range(DST):
                        vector.tensor_tensor_scan(out=hsb[:, g, s, :], data0=dA[:, s, :],
                                                  data1=bArg[:, s, :],
                                                  initial=hsb[:, g, s, TCH - 1:TCH],
                                                  op0=ALU.mult, op1=ALU.add)
                    vector.drain()
                    vector.tensor_tensor(out=hc[:], in0=hsb[:, g, :, :], in1=Cbc[:],
                                         op=ALU.mult)
                    vector.drain()
                    vector.tensor_tensor(out=dA[:, 8:16, :], in0=hc[:, 0:8, :],
                                         in1=hc[:, 8:16, :], op=ALU.add)
                    vector.drain()
                    vector.tensor_tensor(out=dA[:, 0:4, :], in0=dA[:, 8:12, :],
                                         in1=dA[:, 12:16, :], op=ALU.add)
                    vector.drain()
                    vector.tensor_tensor(out=dA[:, 4:6, :], in0=dA[:, 0:2, :],
                                         in1=dA[:, 2:4, :], op=ALU.add)
                    vector.drain()
                    vector.tensor_tensor(out=yt[:], in0=dA[:, 4, :], in1=dA[:, 5, :],
                                         op=ALU.add)
                    vector.drain()
                    vector.scalar_tensor_tensor(out=y2[:], in0=xcb[:, g, tsl],
                                                scalar=Dt[:, g:g + 1], in1=yt[:],
                                                op0=ALU.mult, op1=ALU.add)
                    vector.drain()
                    vector.wait_ge(s_sl, it + 1)
                    if it >= 2:
                        vector.wait_ge(s_ys, 16 * (it - 1))
                    vector.tensor_tensor(out=y3[:, it % 2, :], in0=y2[:], in1=sz[:],
                                         op=ALU.mult)
                    vector.drain()
                    vector.memset(yt[:, 0:1], 0.0).then_inc(s_vy, 1)

        @block.scalar
        def _(scalar):
            scalar.wait_ge(s_c, 80)
            scalar.activation(out=negA[:], in_=negA[:], func=AF.Exp).then_inc(s_da, 1)
            for b in range(B):
                for g in range(NG):
                    for nb2 in range(NB):
                        idx = b * 32 + g * NB + nb2
                        pb = idx % 2
                        scalar.wait_ge(s_mm, idx + 1)
                        scalar.activation(out=etile[:, 0, :], in_=ps[:, pb, :], func=AF.Exp,
                                          bias=dtb[:, g:g + 1])
                        scalar.drain()
                        scalar.activation(out=delta[:, g, nb2 * 512:(nb2 + 1) * 512],
                                          in_=etile[:, 0, :], func=AF.Ln,
                                          bias=cst[:, 1:2])
                        scalar.drain()
                        scalar.copy(out=sbar[0:1, 0:1], in_=sbar[0:1, 1:2]).then_inc(s_sp, 1)
                for it0 in range(NIT):
                    it = b * NIT + it0
                    tc, g = divmod(it0, NG)
                    tsl = slice(tc * TCH, (tc + 1) * TCH)
                    if it >= 1:
                        scalar.wait_ge(s_vy, it)
                    if it == 0:
                        scalar.wait_ge(s_neg, 1)
                    for s in range(DST):
                        scalar.activation(out=dA[:, s, :], in_=delta[:, g, tsl],
                                          func=AF.Exp, scale=negA[:, g, s:s + 1])
                    scalar.drain()
                    scalar.copy(out=sbar[0:1, 0:1], in_=sbar[0:1, 1:2]).then_inc(s_da, 1)
                    scalar.wait_ge(s_z, 16 * (it + 1))
                    scalar.activation(out=sz[:], in_=zt[:, it % 2, :], func=AF.Silu)
                    scalar.drain()
                    scalar.copy(out=sbar[0:1, 0:1], in_=sbar[0:1, 1:2]).then_inc(s_sl, 1)

        @block.tensor
        def _(tensor):
            tensor.wait_ge(s_c, 80)
            for b in range(B):
                tensor.wait_ge(s_dl, b + 1)
                if b > 0:
                    tensor.wait_ge(s_vy, b * NIT)
                for g in range(NG):
                    for nb2 in range(NB):
                        idx = b * 32 + g * NB + nb2
                        pb = idx % 2
                        if idx >= 2:
                            tensor.wait_ge(s_sp, idx - 1)
                        tensor.matmul(ps[:, pb, :], dtw[:, g * 128:(g + 1) * 128],
                                      dtlb[:, nb2 * 512:(nb2 + 1) * 512],
                                      start=True, stop=True).then_inc(s_mm, 1)

        @block.sync
        def _(sync):
            sync.wait_ge(s_ys, 16 * B * NIT)

    return nc


def run_L3(inputs, xz, xc, xdbl_parts):
    nc = build_L3()
    in_maps = []
    dtw = inputs["L0_attn_dt_proj_w"]
    dtbv = inputs["L0_attn_dt_proj_b"]
    Alog = inputs["L0_attn_A_log"]
    Dv = inputs["L0_attn_D"]
    cstv = np.array([[EPS_LN, 1.0]], np.float32)
    for c in range(NCORES):
        ch = slice(c * 512, (c + 1) * 512)
        xc_c = np.ascontiguousarray(xc[:, ch, :].reshape(B * 512, L))
        z_c = np.ascontiguousarray(xz[DIN + c * 512:DIN + (c + 1) * 512, :, :]
                                   .transpose(1, 0, 2).reshape(LANES, L))
        dtwT_c = np.ascontiguousarray(dtw[ch].T)
        dtb_c = np.ascontiguousarray(dtbv[ch].reshape(4, 128).T)
        A_c = np.ascontiguousarray(Alog[ch].reshape(4, 128, DST).transpose(1, 0, 2))
        D_c = np.ascontiguousarray(Dv[ch].reshape(4, 128).T)
        in_maps.append(dict(xc_in=xc_c, z_in=z_c, xdblp_in=xdbl_parts,
                            dtwT_in=dtwT_c, dtb_in=dtb_c, A_in=A_c, D_in=D_c,
                            cst_in=cstv))
    res = run_bass_kernel_spmd(nc, in_maps, core_ids=list(range(NCORES)))
    y = np.empty((B, DIN, L), np.float32)
    for c, r in enumerate(res.results):
        y[:, c * 512:(c + 1) * 512, :] = r["y_out"].reshape(B, 512, L)
    return y


def build_L4():
    nc = bass.Bass("TRN2", target_bir_lowering=False, debug=False)
    yfm_in = nc.dram_tensor("yfm_in", [DIN, TOK], F32, kind="ExternalInput").ap()
    woT_in = nc.dram_tensor("woT_in", [DIN, D], F32, kind="ExternalInput").ap()
    x_in = nc.dram_tensor("x_in", [TOK, D], F32, kind="ExternalInput").ap()
    ln0_in = nc.dram_tensor("ln0_in", [TOK, D], F32, kind="ExternalInput").ap()
    gparts_in = nc.dram_tensor("gparts_in", [1, 3 * NCORES], F32, kind="ExternalInput").ap()
    w1s_in = nc.dram_tensor("w1s_in", [768, D], F32, kind="ExternalInput").ap()
    w2s_in = nc.dram_tensor("w2s_in", [768, D], F32, kind="ExternalInput").ap()
    w3s_in = nc.dram_tensor("w3s_in", [256, HBIT], F32, kind="ExternalInput").ap()
    cst_in = nc.dram_tensor("cst_in", [1, 2], F32, kind="ExternalInput").ap()

    x1_out = nc.dram_tensor("x1_out", [TOK, D], F32, kind="ExternalOutput").ap()
    qw1_out = nc.dram_tensor("qw1_out", [768, D], BF16, kind="ExternalOutput").ap()
    qw2_out = nc.dram_tensor("qw2_out", [768, D], BF16, kind="ExternalOutput").ap()
    qw3_out = nc.dram_tensor("qw3_out", [256, HBIT], BF16, kind="ExternalOutput").ap()

    out_scr = nc.dram_tensor("out_scr", [D, TOK], BF16).ap()
    ginv_scr = nc.dram_tensor("ginv_scr", [1, 3], F32).ap()

    KT = DIN // 128
    NMP = D // 256
    NT = TOK // 128

    with contextlib.ExitStack() as ctx:
        sb = lambda *a: ctx.enter_context(nc.sbuf_tensor(*a))
        pt = lambda *a: ctx.enter_context(nc.psum_tensor(*a))
        sem = lambda n: ctx.enter_context(nc.semaphore(n))
        yfm = sb("yfm", [128, KT, TOK], BF16)
        wb = sb("wb", [128, 1, KT, 256], BF16)
        ofm = sb("ofm", [128, 2, 4, 512], BF16)
        obf = sb("obf", [128, D], BF16)
        of32 = sb("of32", [128, D], F32)
        xt = sb("xt", [128, D], F32)
        lnt = sb("lnt", [128, D], F32)
        x1t = sb("x1t", [128, 2, D], F32)
        gq = sb("gq", [1, 3 * NCORES], F32)
        gsum = sb("gsum", [1, 8, 3], F32)
        ginv = sb("ginv", [1, 3], F32)
        ginvb = sb("ginvb", [128, 3], F32)
        HH = HBIT // 2
        gw = sb("gw", [128, 2, HH], F32)
        sct = sb("sct", [128, HH], F32)
        itt = sb("itt", [128, HH], I32)
        qt = sb("qt", [128, 2, HH], BF16)
        cst = sb("cst", [128, 2], F32)
        bar = sb("bar", [128, 1], F32)
        ps = pt("ps", [128, 2, 4, 512], F32)
        s_y = sem("s_y")
        s_w = sem("s_w")
        s_mm = sem("s_mm")
        s_ev = sem("s_ev")
        s_os = sem("s_os")
        s_tl = sem("s_tl")
        s_v2 = sem("s_v2")
        s_x1 = sem("s_x1")
        s_g = sem("s_g")
        s_gv = sem("s_gv")
        s_gs = sem("s_gs")
        s_gr = sem("s_gr")
        s_sc = sem("s_sc")
        s_og = sem("s_og")
        s_x1g = sem("s_x1g")
        s_g1 = sem("s_g1")
        s_gs1 = sem("s_gs1")
        s_c = sem("s_c")
        block = ctx.enter_context(nc.Block())

        QTILES = ([("w1", i, 0, D) for i in range(6)] + [("w2", i, 0, D) for i in range(6)]
                  + [("w3", i, c * 2816, 2816) for i in range(2) for c in range(2)])

        @block.gpsimd
        def _(gpsimd):
            for k in range(KT):
                gpsimd.dma_start(out=yfm[:, k, :],
                                 in_=yfm_in[k * 128:(k + 1) * 128, :]).then_inc(s_y, 16)
            for mp in range(NMP):
                if mp >= 1:
                    gpsimd.wait_ge(s_mm, 4 * mp)
                for k in range(KT):
                    gpsimd.dma_start(out=wb[:, 0, k, :],
                                     in_=woT_in[k * 128:(k + 1) * 128, mp * 256:(mp + 1) * 256]
                                     ).then_inc(s_w, 16)
            def _qstore(i):
                wn, ti, c0, wid = QTILES[i]
                dst = {"w1": qw1_out, "w2": qw2_out, "w3": qw3_out}[wn]
                gpsimd.wait_ge(s_gv, i + 1)
                gpsimd.dma_start(out=dst[ti * 128:(ti + 1) * 128, c0:c0 + wid],
                                 in_=qt[:, i % 2, 0:wid]).then_inc([s_gs, s_gs1][i % 2], 16)
            for i, (wn, ti, c0, wid) in enumerate(QTILES):
                if i >= 2:
                    gpsimd.wait_ge(s_sc, i - 1)
                srcw = {"w1": w1s_in, "w2": w2s_in, "w3": w3s_in}[wn]
                gpsimd.dma_start(out=gw[:, i % 2, 0:wid],
                                 in_=srcw[ti * 128:(ti + 1) * 128, c0:c0 + wid]
                                 ).then_inc([s_g, s_g1][i % 2], 16)
                if i >= 1:
                    _qstore(i - 1)
            _qstore(len(QTILES) - 1)

        @block.sync
        def _(sync):
            sync.dma_start(out=cst[:], in_=cst_in.partition_broadcast(128)).then_inc(s_c, 16)
            sync.dma_start(out=gq[:], in_=gparts_in[:]).then_inc(s_c, 16)
            for mp in range(NMP):
                pb = mp % 2
                for i in range(4):
                    mm, j = divmod(i, 2)
                    m = mp * 2 + mm
                    sync.wait_ge(s_ev, 4 * mp + i + 1)
                    sync.dma_start(out=out_scr[m * 128:(m + 1) * 128, j * 512:(j + 1) * 512],
                                   in_=ofm[:, pb, i, :]).then_inc(s_os, 16)
                sync.wait_ge(s_os, 64 * (mp + 1))
                sync.sem_inc(s_og, 1)
            sync.wait_ge(s_os, 16 * 4 * NMP)
            for nt in range(NT):
                if nt >= 1:
                    sync.wait_ge(s_v2, 3 * nt)
                sync.dma_start(out=obf[:], in_=out_scr[:, nt * 128:(nt + 1) * 128],
                               transpose=True).then_inc(s_tl, 16)
                sync.dma_start(out=xt[:], in_=x_in[nt * 128:(nt + 1) * 128, :]).then_inc(s_tl, 16)
                sync.dma_start(out=lnt[:], in_=ln0_in[nt * 128:(nt + 1) * 128, :]).then_inc(s_tl, 16)
                sync.wait_ge(s_v2, 3 * nt + 3)
                sync.dma_start(out=x1_out[nt * 128:(nt + 1) * 128, :],
                               in_=x1t[:, nt % 2, :]).then_inc(s_x1, 16)
                sync.wait_ge(s_x1, 16 * (nt + 1))
                sync.sem_inc(s_x1g, 1)
            sync.wait_ge(s_gr, 3)
            sync.dma_start(out=ginv_scr[:], in_=ginv[:]).then_inc(s_c, 16)
            sync.wait_ge(s_c, 48)
            sync.dma_start(out=ginvb[:], in_=ginv_scr.partition_broadcast(128)).then_inc(s_c, 16)
            sync.wait_ge(s_gs, 16 * (len(QTILES) // 2))
            sync.wait_ge(s_gs1, 16 * (len(QTILES) // 2))

        @block.tensor
        def _(tensor):
            tensor.wait_ge(s_y, 16 * KT)
            for mp in range(NMP):
                pb = mp % 2
                tensor.wait_ge(s_w, 16 * KT * (mp + 1))
                for i in range(4):
                    mm, j = divmod(i, 2)
                    if mp >= 2:
                        tensor.wait_ge(s_ev, 4 * (mp - 1))
                    for k in range(KT):
                        mo = tensor.matmul(ps[:, pb, i, :],
                                           wb[:, 0, k, mm * 128:(mm + 1) * 128],
                                           yfm[:, k, j * 512:(j + 1) * 512],
                                           start=(k == 0), stop=(k == KT - 1))
                    mo.then_inc(s_mm, 1)

        @block.vector
        def _(vector):
            for mp in range(NMP):
                pb = mp % 2
                for i in range(4):
                    vector.wait_ge(s_mm, 4 * mp + i + 1)
                    if mp >= 2:
                        vector.wait_ge(s_og, mp - 1)
                    vector.tensor_copy(out=ofm[:, pb, i, :], in_=ps[:, pb, i, :])
                    vector.drain()
                    vector.memset(bar[:, 0:1], 0.0).then_inc(s_ev, 1)
            for nt in range(NT):
                vector.wait_ge(s_tl, 48 * nt + 48)
                if nt >= 2:
                    vector.wait_ge(s_x1g, nt - 1)
                vector.tensor_copy(out=of32[:], in_=obf[:]).then_inc(s_v2, 1)
                vector.tensor_tensor(out=x1t[:, nt % 2, :], in0=xt[:], in1=lnt[:],
                                     op=ALU.add).then_inc(s_v2, 1)
                vector.drain()
                vector.tensor_tensor(out=x1t[:, nt % 2, :], in0=x1t[:, nt % 2, :],
                                     in1=of32[:], op=ALU.add)
                vector.drain()
                vector.memset(bar[:, 0:1], 0.0).then_inc(s_v2, 1)
            vector.wait_ge(s_c, 32)
            vector.tensor_reduce(out=gsum[0:1, 0, :],
                                 in_=gq.ap().rearrange("p (c j) -> p j c", c=NCORES),
                                 axis=AX.X, op=ALU.add)
            vector.drain()
            vector.memset(bar[:, 0:1], 0.0).then_inc(s_gr, 1)
            vector.wait_ge(s_gr, 2)
            vector.reciprocal(out=ginv[:], in_=gsum[0:1, 1, :])
            vector.drain()
            vector.memset(bar[:, 0:1], 0.0).then_inc(s_gr, 1)
            for i, (wn, ti, c0, wid) in enumerate(QTILES):
                vector.wait_ge(s_sc, i + 1)
                if i >= 2:
                    vector.wait_ge([s_gs, s_gs1][i % 2], 16 * (i // 2))
                vector.tensor_copy(out=itt[:, 0:wid], in_=sct[:, 0:wid])
                vector.drain()
                vector.tensor_scalar(out=qt[:, i % 2, 0:wid], in0=itt[:, 0:wid],
                                     scalar1=-1.0, scalar2=1.0, op0=ALU.max, op1=ALU.min)
                vector.drain()
                vector.memset(bar[:, 0:1], 0.0).then_inc(s_gv, 1)

        @block.scalar
        def _(scalar):
            NW = float(HBIT * D)
            scalar.wait_ge(s_gr, 1)
            scalar.activation(out=gsum[0:1, 1, :], in_=gsum[0:1, 0, :], func=AF.Identity,
                              scale=1.0 / NW, bias=cst[0:1, 0:1])
            scalar.drain()
            scalar.copy(out=bar[0:1, 0:1], in_=cst[0:1, 0:1]).then_inc(s_gr, 1)
            scalar.wait_ge(s_c, 64)
            for i, (wn, ti, c0, wid) in enumerate(QTILES):
                j = {"w1": 0, "w2": 1, "w3": 2}[wn]
                scalar.wait_ge([s_g, s_g1][i % 2], 16 * (i // 2 + 1))
                if i >= 1:
                    scalar.wait_ge(s_gv, i)
                scalar.activation(out=sct[:, 0:wid], in_=gw[:, i % 2, 0:wid],
                                  func=AF.Identity, scale=ginvb[:, j:j + 1])
                scalar.drain()
                scalar.copy(out=bar[0:1, 0:1], in_=cst[0:1, 0:1]).then_inc(s_sc, 1)

    return nc


def run_L4(inputs, y, ln0, gparts):
    nc = build_L4()
    x = inputs["L0_attn_hidden_states"]
    woT = np.ascontiguousarray(inputs["L0_attn_out_proj_w"].T)
    gp = np.ascontiguousarray(gparts.reshape(1, 3 * NCORES))
    w1 = inputs["L0_ffn_w1"]
    w2 = inputs["L0_ffn_w2"]
    w3 = inputs["L0_ffn_w3"]
    pad = np.zeros((64, D), np.float32)
    cstv = np.array([[EPS_BIT, 1.0]], np.float32)
    in_maps = []
    for c in range(NCORES):
        tsl = slice(c * TC, (c + 1) * TC)
        yfm = np.ascontiguousarray(
            y[:, :, tsl].transpose(1, 0, 2).reshape(DIN, TOK))
        in_maps.append(dict(
            yfm_in=yfm, woT_in=woT,
            x_in=np.ascontiguousarray(x[:, tsl, :].reshape(TOK, D)),
            ln0_in=np.ascontiguousarray(ln0[:, tsl, :].reshape(TOK, D)),
            gparts_in=gp,
            w1s_in=np.ascontiguousarray(np.concatenate([w1[c * 704:(c + 1) * 704], pad], 0)),
            w2s_in=np.ascontiguousarray(np.concatenate([w2[c * 704:(c + 1) * 704], pad], 0)),
            w3s_in=np.ascontiguousarray(w3[c * 256:(c + 1) * 256]),
            cst_in=cstv))
    res = run_bass_kernel_spmd(nc, in_maps, core_ids=list(range(NCORES)))
    x1 = np.empty((B, L, D), np.float32)
    qw1 = np.empty((HBIT, D), BF)
    qw2 = np.empty((HBIT, D), BF)
    qw3 = np.empty((D, HBIT), BF)
    for c, r in enumerate(res.results):
        x1[:, c * TC:(c + 1) * TC, :] = r["x1_out"].reshape(B, TC, D)
        qw1[c * 704:(c + 1) * 704] = r["qw1_out"][:704]
        qw2[c * 704:(c + 1) * 704] = r["qw2_out"][:704]
        qw3[c * 256:(c + 1) * 256] = r["qw3_out"]
    return x1, qw1, qw2, qw3




def build_L5():
    nc = bass.Bass("TRN2", target_bir_lowering=False, debug=False)
    x1_in = nc.dram_tensor("x1_in", [TOK, D], F32, kind="ExternalInput").ap()
    qw1T_in = nc.dram_tensor("qw1T_in", [D, HBIT], BF16, kind="ExternalInput").ap()
    qw2T_in = nc.dram_tensor("qw2T_in", [D, HBIT], BF16, kind="ExternalInput").ap()
    qw3T_in = nc.dram_tensor("qw3T_in", [HBIT, D], BF16, kind="ExternalInput").ap()
    wkT_in = nc.dram_tensor("wkT_in", [D, D], F32, kind="ExternalInput").ap()
    wvT_in = nc.dram_tensor("wvT_in", [D, D], F32, kind="ExternalInput").ap()
    cst_in = nc.dram_tensor("cst_in", [1, 2], F32, kind="ExternalInput").ap()

    x2_out = nc.dram_tensor("x2_out", [TOK, D], F32, kind="ExternalOutput").ap()
    xn2_out = nc.dram_tensor("xn2_out", [TOK, D], BF16, kind="ExternalOutput").ap()
    K_out = nc.dram_tensor("K_out", [D, TOK], BF16, kind="ExternalOutput").ap()
    V_out = nc.dram_tensor("V_out", [D, TOK], BF16, kind="ExternalOutput").ap()

    xn1_scr = nc.dram_tensor("xn1_scr", [TOK, D], BF16).ap()
    h_scr = nc.dram_tensor("h_scr", [HBIT, TOK], BF16).ap()
    dn_scr = nc.dram_tensor("dn_scr", [D, TOK], BF16).ap()

    KT = D // 128
    MPH = HBIT // 256
    MPD = D // 256
    KH = HBIT // 128
    NT = TOK // 128

    with contextlib.ExitStack() as ctx:
        sb = lambda *a: ctx.enter_context(nc.sbuf_tensor(*a))
        pt = lambda *a: ctx.enter_context(nc.psum_tensor(*a))
        sem = lambda n: ctx.enter_context(nc.semaphore(n))
        xt = sb("xt", [128, D], F32)
        dnb = sb("dnb", [128, D], BF16)
        dn32 = sb("dn32", [128, D], F32)
        x2t = sb("x2t", [128, 2, D], F32)
        sq = sb("sq", [128, D], F32)
        sq2 = sb("sq2", [128, D], F32)
        stat = sb("stat", [128, 4], F32)
        xnt = sb("xnt", [128, 2, D], BF16)
        cst = sb("cst", [128, 2], F32)
        bar = sb("bar", [128, 1], F32)
        xfm = sb("xfm", [128, KT, TOK], BF16)
        wA = sb("wA", [128, KT, 256], BF16)
        wB = sb("wB", [128, KT, 256], BF16)
        wC = sb("wC", [128, KH, 256], BF16)
        hst = sb("hst", [128, 2, 4, 512], BF16)
        hbuf = sb("hbuf", [128, 2, TOK], BF16)
        evg = sb("evg", [128, 2, 4, 512], BF16)
        evu = sb("evu", [128, 2, 4, 512], BF16)
        sgt = sb("sgt", [128, 512], F32)
        ps = pt("ps", [128, 8, 512], F32)
        s_c = sem("s_c")
        s_lx = sem("s_lx")
        s_xs1 = sem("s_xs1")
        s_xq1 = sem("s_xq1")
        s_ln = sem("s_ln")
        s_lns = sem("s_lns")
        s_tf1 = sem("s_tf1")
        s_tf2 = sem("s_tf2")
        s_w = sem("s_w")
        s_mm = sem("s_mm")
        s_hv = sem("s_hv")
        s_md = sem("s_md")
        s_dv = sem("s_dv")
        s_ds = sem("s_ds")
        s_x2 = sem("s_x2")
        s_xv = sem("s_xv")
        s_x2q = sem("s_x2q")
        s_x2s = sem("s_x2s")
        s_hs = sem("s_hs")
        s_hq = sem("s_hq")
        s_hl0 = sem("s_hl0")
        s_hl1 = sem("s_hl1")
        s_mdk = sem("s_mdk")
        s_mk = sem("s_mk")
        s_sg = sem("s_sg")
        s_kv = sem("s_kv")
        s_kvs = sem("s_kvs")
        block = ctx.enter_context(nc.Block())

        def ln_vector(vector, in_ap, nn):
            vector.reduce_sum(out=stat[:, 0:1], in_=in_ap, axis=AX.X)
            vector.drain()
            vector.tensor_scalar(out=stat[:, 1:2], in0=stat[:, 0:1], scalar1=-1.0 / D,
                                 scalar2=None, op0=ALU.mult, op1=ALU.bypass)
            vector.drain()
            vector.tensor_scalar(out=sq[:], in0=in_ap, scalar1=stat[:, 1:2],
                                 scalar2=None, op0=ALU.add, op1=ALU.bypass)
            vector.drain()
            vector.memset(bar[:, 0:1], 0.0).then_inc(s_ln, 1)
            vector.wait_ge(s_lns, nn + 1)
            vector.reciprocal(out=stat[:, 3:4], in_=stat[:, 2:3])
            vector.drain()
            vector.tensor_scalar(out=xnt[:, nn % 2, :], in0=sq[:], scalar1=stat[:, 3:4],
                                 scalar2=None, op0=ALU.mult, op1=ALU.bypass)
            vector.drain()
            vector.memset(bar[:, 0:1], 0.0).then_inc(s_ln, 1)

        def ln_scalar(scalar, nn):
            scalar.wait_ge(s_ln, 2 * nn + 1)
            scalar.activation(out=sq2[:], in_=sq[:], func=AF.Square, accum_out=stat[:, 0:1])
            scalar.drain()
            scalar.activation(out=stat[:, 2:3], in_=stat[:, 0:1], func=AF.Sqrt,
                              scale=1.0 / D, bias=cst[:, 0:1])
            scalar.drain()
            scalar.copy(out=bar[0:1, 0:1], in_=cst[0:1, 0:1]).then_inc(s_lns, 1)

        @block.sync
        def _(sync):
            sync.dma_start(out=cst[:], in_=cst_in.partition_broadcast(128)).then_inc(s_c, 16)
            for n in range(NT):
                if n >= 1:
                    sync.wait_ge(s_ln, 2 * n)
                sync.dma_start(out=xt[:], in_=x1_in[n * 128:(n + 1) * 128, :]).then_inc(s_lx, 16)
                sync.wait_ge(s_ln, 2 * n + 2)
                sync.dma_start(out=xn1_scr[n * 128:(n + 1) * 128, :],
                               in_=xnt[:, n % 2, :]).then_inc(s_xs1, 16)
                sync.wait_ge(s_xs1, 16 * (n + 1))
                sync.sem_inc(s_xq1, 1)
            for k in range(KT):
                sync.dma_start(out=xfm[:, k, :], in_=xn1_scr[:, k * 128:(k + 1) * 128],
                               transpose=True).then_inc(s_tf1, 16)
            for mp in range(MPH):
                pb = mp % 2
                for i in range(4):
                    mm, j = divmod(i, 2)
                    m = mp * 2 + mm
                    sync.wait_ge(s_hv, 4 * mp + i + 1)
                    sync.dma_start(out=h_scr[m * 128:(m + 1) * 128, j * 512:(j + 1) * 512],
                                   in_=hst[:, pb, i, :]).then_inc(s_hs, 16)
                sync.wait_ge(s_hs, 64 * (mp + 1))
                sync.sem_inc(s_hq, 1)
            for mp in range(MPD):
                pb = mp % 2
                for i in range(4):
                    mm, j = divmod(i, 2)
                    m = mp * 2 + mm
                    sync.wait_ge(s_dv, 4 * mp + i + 1)
                    sync.dma_start(out=dn_scr[m * 128:(m + 1) * 128, j * 512:(j + 1) * 512],
                                   in_=evg[:, pb, i, :]).then_inc(s_ds, 16)
                sync.wait_ge(s_ds, 64 * (mp + 1) + mp)
                sync.sem_inc(s_ds, 1)
            sync.wait_ge(s_ds, 64 * MPD + MPD)
            for n in range(NT):
                if n >= 1:
                    sync.wait_ge(s_ln, 2 * (NT + n))
                sync.dma_start(out=xt[:], in_=x1_in[n * 128:(n + 1) * 128, :]).then_inc(s_x2, 16)
                sync.dma_start(out=dnb[:], in_=dn_scr[:, n * 128:(n + 1) * 128],
                               transpose=True).then_inc(s_x2, 16)
                sync.wait_ge(s_xv, n + 1)
                sync.dma_start(out=x2_out[n * 128:(n + 1) * 128, :],
                               in_=x2t[:, n % 2, :]).then_inc(s_x2s, 16)
                sync.wait_ge(s_ln, 2 * (NT + n) + 2)
                sync.dma_start(out=xn2_out[n * 128:(n + 1) * 128, :],
                               in_=xnt[:, n % 2, :]).then_inc(s_x2s, 16)
                sync.wait_ge(s_x2s, 32 * (n + 1))
                sync.sem_inc(s_x2q, 1)
            for k in range(KT):
                sync.dma_start(out=xfm[:, k, :], in_=xn2_out[:, k * 128:(k + 1) * 128],
                               transpose=True).then_inc(s_tf2, 16)
            for mp in range(MPD):
                pb = mp % 2
                for i in range(4):
                    sync.wait_ge(s_kv, 8 * mp + 2 * i + 2)
                    mm, j = divmod(i, 2)
                    m = mp * 2 + mm
                    sync.dma_start(out=K_out[m * 128:(m + 1) * 128, j * 512:(j + 1) * 512],
                                   in_=evg[:, pb, i, :]).then_inc(s_kvs, 16)
                    sync.dma_start(out=V_out[m * 128:(m + 1) * 128, j * 512:(j + 1) * 512],
                                   in_=evu[:, pb, i, :]).then_inc(s_kvs, 16)
                sync.wait_ge(s_kvs, 128 * (mp + 1) + mp)
                sync.sem_inc(s_kvs, 1)
            sync.wait_ge(s_kvs, 128 * MPD + MPD)

        @block.gpsimd
        def _(gpsimd):
            for mp in range(MPH):
                if mp >= 1:
                    gpsimd.wait_ge(s_mm, 8 * mp)
                for k in range(KT):
                    gpsimd.dma_start(out=wA[:, k, :],
                                     in_=qw1T_in[k * 128:(k + 1) * 128, mp * 256:(mp + 1) * 256]
                                     ).then_inc(s_w, 16)
                    gpsimd.dma_start(out=wB[:, k, :],
                                     in_=qw2T_in[k * 128:(k + 1) * 128, mp * 256:(mp + 1) * 256]
                                     ).then_inc(s_w, 16)
            gpsimd.wait_ge(s_hq, MPH)
            for mp in range(MPD):
                if mp >= 1:
                    gpsimd.wait_ge(s_dv, 4 * mp)
                for k in range(KH):
                    gpsimd.dma_start(out=wC[:, k, :],
                                     in_=qw3T_in[k * 128:(k + 1) * 128, mp * 256:(mp + 1) * 256]
                                     ).then_inc(s_w, 16)
                for k in range(KH):
                    if mp * KH + k >= 2:
                        gpsimd.wait_ge(s_mdk, mp * KH + k - 1)
                    gpsimd.dma_start(out=hbuf[:, k % 2, :],
                                     in_=h_scr[k * 128:(k + 1) * 128, :]
                                     ).then_inc([s_hl0, s_hl1][k % 2], 16)
            for mp in range(MPD):
                if mp >= 1:
                    gpsimd.wait_ge(s_mk, 8 * mp)
                for k in range(KT):
                    gpsimd.dma_start(out=wA[:, k, :],
                                     in_=wkT_in[k * 128:(k + 1) * 128, mp * 256:(mp + 1) * 256]
                                     ).then_inc(s_w, 16)
                    gpsimd.dma_start(out=wB[:, k, :],
                                     in_=wvT_in[k * 128:(k + 1) * 128, mp * 256:(mp + 1) * 256]
                                     ).then_inc(s_w, 16)

        @block.tensor
        def _(tensor):
            tensor.wait_ge(s_tf1, 16 * KT)
            for mp in range(MPH):
                tensor.wait_ge(s_w, 16 * 32 * (mp + 1))
                if mp >= 1:
                    tensor.wait_ge(s_hv, 4 * mp)
                for i in range(4):
                    mm, j = divmod(i, 2)
                    for k in range(KT):
                        mg = tensor.matmul(ps[:, i, :], wA[:, k, mm * 128:(mm + 1) * 128],
                                           xfm[:, k, j * 512:(j + 1) * 512],
                                           start=(k == 0), stop=(k == KT - 1))
                    mg.then_inc(s_mm, 1)
                    for k in range(KT):
                        mu = tensor.matmul(ps[:, 4 + i, :], wB[:, k, mm * 128:(mm + 1) * 128],
                                           xfm[:, k, j * 512:(j + 1) * 512],
                                           start=(k == 0), stop=(k == KT - 1))
                    mu.then_inc(s_mm, 1)
            tensor.wait_ge(s_sg, 4 * MPH)
            tensor.wait_ge(s_hv, 4 * MPH)
            W0 = 16 * 32 * MPH
            for mp in range(MPD):
                tensor.wait_ge(s_w, W0 + 16 * KH * (mp + 1))
                if mp >= 1:
                    tensor.wait_ge(s_dv, 4 * mp)
                for k in range(KH):
                    tensor.wait_ge([s_hl0, s_hl1][k % 2], 16 * (mp * 22 + k // 2 + 1))
                    for i in range(4):
                        mm, j = divmod(i, 2)
                        md = tensor.matmul(ps[:, i, :], wC[:, k, mm * 128:(mm + 1) * 128],
                                           hbuf[:, k % 2, j * 512:(j + 1) * 512],
                                           start=(k == 0), stop=(k == KH - 1))
                    md.then_inc(s_mdk, 1)
            tensor.wait_ge(s_dv, 4 * MPD)
            tensor.wait_ge(s_tf2, 16 * KT)
            W1 = W0 + 16 * KH * MPD
            for mp in range(MPD):
                tensor.wait_ge(s_w, W1 + 16 * 32 * (mp + 1))
                if mp >= 1:
                    tensor.wait_ge(s_kv, 8 * mp)
                for i in range(4):
                    mm, j = divmod(i, 2)
                    for k in range(KT):
                        mk1 = tensor.matmul(ps[:, i, :], wA[:, k, mm * 128:(mm + 1) * 128],
                                            xfm[:, k, j * 512:(j + 1) * 512],
                                            start=(k == 0), stop=(k == KT - 1))
                    mk1.then_inc(s_mk, 1)
                    for k in range(KT):
                        mv1 = tensor.matmul(ps[:, 4 + i, :], wB[:, k, mm * 128:(mm + 1) * 128],
                                            xfm[:, k, j * 512:(j + 1) * 512],
                                            start=(k == 0), stop=(k == KT - 1))
                    mv1.then_inc(s_mk, 1)

        @block.vector
        def _(vector):
            vector.wait_ge(s_c, 16)
            for n in range(NT):
                vector.wait_ge(s_lx, 16 * (n + 1))
                if n >= 2:
                    vector.wait_ge(s_xq1, n - 1)
                ln_vector(vector, xt[:], n)
            for mp in range(MPH):
                pb = mp % 2
                for i in range(4):
                    vector.wait_ge(s_sg, 4 * mp + i + 1)
                    if mp >= 2:
                        vector.wait_ge(s_hq, mp - 1)
                    vector.tensor_tensor(out=hst[:, pb, i, :],
                                         in0=sgt[:], in1=ps[:, 4 + i, :], op=ALU.mult)
                    vector.drain()
                    vector.memset(bar[:, 0:1], 0.0).then_inc(s_hv, 1)
            for mp in range(MPD):
                pb = mp % 2
                for i in range(4):
                    vector.wait_ge(s_mdk, (mp + 1) * KH)
                    if mp >= 2:
                        vector.wait_ge(s_ds, 64 * (mp - 1) + (mp - 1))
                    vector.tensor_copy(out=evg[:, pb, i, :], in_=ps[:, i, :])
                    vector.drain()
                    vector.memset(bar[:, 0:1], 0.0).then_inc(s_dv, 1)
            for n in range(NT):
                vector.wait_ge(s_x2, 32 * n + 32)
                if n >= 2:
                    vector.wait_ge(s_x2q, n - 1)
                vector.tensor_copy(out=dn32[:], in_=dnb[:])
                vector.drain()
                vector.tensor_tensor(out=x2t[:, n % 2, :], in0=xt[:], in1=dn32[:],
                                     op=ALU.add)
                vector.drain()
                vector.memset(bar[:, 0:1], 0.0).then_inc(s_xv, 1)
                ln_vector(vector, x2t[:, n % 2, :], NT + n)
            vector.wait_ge(s_ds, 64 * MPD + MPD)
            for mp in range(MPD):
                pb = mp % 2
                for i in range(4):
                    vector.wait_ge(s_mk, 8 * mp + 2 * i + 2)
                    if mp >= 2:
                        vector.wait_ge(s_kvs, 128 * (mp - 1) + (mp - 1))
                    vector.tensor_copy(out=evg[:, pb, i, :], in_=ps[:, i, :])
                    vector.drain()
                    vector.tensor_copy(out=evu[:, pb, i, :], in_=ps[:, 4 + i, :])
                    vector.drain()
                    vector.memset(bar[:, 0:1], 0.0).then_inc(s_kv, 2)

        @block.scalar
        def _(scalar):
            scalar.wait_ge(s_c, 16)
            for n in range(NT):
                ln_scalar(scalar, n)
            for mp in range(MPH):
                for i in range(4):
                    scalar.wait_ge(s_mm, 8 * mp + 2 * i + 2)
                    if 4 * mp + i >= 1:
                        scalar.wait_ge(s_hv, 4 * mp + i)
                    scalar.activation(out=sgt[:], in_=ps[:, i, :], func=AF.Sigmoid)
                    scalar.drain()
                    scalar.copy(out=bar[0:1, 0:1], in_=cst[0:1, 0:1]).then_inc(s_sg, 1)
            for n in range(NT):
                ln_scalar(scalar, NT + n)

    return nc


def run_L5(inputs, x1, qw1, qw2, qw3):
    nc = build_L5()
    qw1T = np.ascontiguousarray(qw1.T)
    qw2T = np.ascontiguousarray(qw2.T)
    qw3T = np.ascontiguousarray(qw3.T)
    wkT = np.ascontiguousarray(inputs["L1_attn_w_k"].T)
    wvT = np.ascontiguousarray(inputs["L1_attn_w_v"].T)
    cstv = np.array([[EPS_LN, 1.0]], np.float32)
    in_maps = []
    for c in range(NCORES):
        tsl = slice(c * TC, (c + 1) * TC)
        in_maps.append(dict(
            x1_in=np.ascontiguousarray(x1[:, tsl, :].reshape(TOK, D)),
            qw1T_in=qw1T, qw2T_in=qw2T, qw3T_in=qw3T,
            wkT_in=wkT, wvT_in=wvT, cst_in=cstv))
    res = run_bass_kernel_spmd(nc, in_maps, core_ids=list(range(NCORES)))
    x2 = np.empty((B, L, D), np.float32)
    xn2 = np.empty((B, L, D), BF)
    Kf = np.empty((D, B, L), BF)
    Vf = np.empty((D, B, L), BF)
    for c, r in enumerate(res.results):
        x2[:, c * TC:(c + 1) * TC, :] = r["x2_out"].reshape(B, TC, D)
        xn2[:, c * TC:(c + 1) * TC, :] = r["xn2_out"].reshape(B, TC, D)
        Kf[:, :, c * TC:(c + 1) * TC] = r["K_out"].reshape(D, B, TC)
        Vf[:, :, c * TC:(c + 1) * TC] = r["V_out"].reshape(D, B, TC)
    return x2, xn2, Kf, Vf


HB2 = 5632


def build_L6():
    nc = bass.Bass("TRN2", target_bir_lowering=False, debug=False)
    gidx_in = nc.dram_tensor("gidx_in", [768, D], I32, kind="ExternalInput").ap()
    uidx_in = nc.dram_tensor("uidx_in", [768, D], I32, kind="ExternalInput").ap()
    didx_in = nc.dram_tensor("didx_in", [256, HGHOST], I32, kind="ExternalInput").ap()
    lut_in = nc.dram_tensor("lut_in", [1, 16], F32, kind="ExternalInput").ap()

    gw_out = nc.dram_tensor("gw_out", [768, D], BF16, kind="ExternalOutput").ap()
    uw_out = nc.dram_tensor("uw_out", [768, D], BF16, kind="ExternalOutput").ap()
    dw_out = nc.dram_tensor("dw_out", [256, HGHOST], BF16, kind="ExternalOutput").ap()

    TILES = ([("g", i, D) for i in range(6)] + [("u", i, D) for i in range(6)]
             + [("d", i, HGHOST) for i in range(2)])

    with contextlib.ExitStack() as ctx:
        sb = lambda *a: ctx.enter_context(nc.sbuf_tensor(*a))
        sem = lambda n: ctx.enter_context(nc.semaphore(n))
        lutb = sb("lutb", [128, 16], F32)
        idxt = sb("idxt", [128, 2, HGHOST], I32)
        acc = sb("acc", [128, 2, HGHOST], BF16)
        tmp = sb("tmp", [128, HGHOST], BF16)
        bar = sb("bar", [128, 1], F32)
        s_c = sem("s_c")
        s_l0 = sem("s_l0")
        s_l1 = sem("s_l1")
        s_v = sem("s_v")
        s_st0 = sem("s_st0")
        s_st1 = sem("s_st1")
        block = ctx.enter_context(nc.Block())

        @block.sync
        def _(sync):
            sync.dma_start(out=lutb[:], in_=lut_in.partition_broadcast(128)).then_inc(s_c, 16)
            for i, (wn, ti, wid) in enumerate(TILES):
                if i >= 2:
                    sync.wait_ge(s_v, i - 1)
                srcm = {"g": gidx_in, "u": uidx_in, "d": didx_in}[wn]
                sync.dma_start(out=idxt[:, i % 2, 0:wid],
                               in_=srcm[ti * 128:(ti + 1) * 128, :]).then_inc([s_l0, s_l1][i % 2], 16)

        @block.gpsimd
        def _(gpsimd):
            for i, (wn, ti, wid) in enumerate(TILES):
                dstm = {"g": gw_out, "u": uw_out, "d": dw_out}[wn]
                gpsimd.wait_ge(s_v, i + 1)
                gpsimd.dma_start(out=dstm[ti * 128:(ti + 1) * 128, :],
                                 in_=acc[:, i % 2, 0:wid]).then_inc([s_st0, s_st1][i % 2], 16)

        @block.vector
        def _(vector):
            vector.wait_ge(s_c, 16)
            for i, (wn, ti, wid) in enumerate(TILES):
                vector.wait_ge([s_l0, s_l1][i % 2], 16 * (i // 2 + 1))
                if i >= 2:
                    vector.wait_ge([s_st0, s_st1][i % 2], 16 * (i // 2))
                for kk in range(16):
                    vector.tensor_scalar(out=tmp[:, 0:wid], in0=idxt[:, i % 2, 0:wid],
                                         scalar1=float(kk), scalar2=lutb[:, kk:kk + 1],
                                         op0=ALU.is_equal, op1=ALU.mult)
                    vector.drain()
                    if kk == 0:
                        vector.tensor_copy(out=acc[:, i % 2, 0:wid], in_=tmp[:, 0:wid])
                    else:
                        vector.tensor_tensor(out=acc[:, i % 2, 0:wid], in0=acc[:, i % 2, 0:wid],
                                             in1=tmp[:, 0:wid], op=ALU.add)
                    vector.drain()
                vector.memset(bar[:, 0:1], 0.0).then_inc(s_v, 1)

        @block.tensor
        def _(tensor):
            tensor.nop()

        @block.scalar
        def _(scalar):
            scalar.nop()

        @block.sync
        def _(sync):
            sync.wait_ge(s_st0, 16 * 7)
            sync.wait_ge(s_st1, 16 * 7)

    return nc


def run_L6(inputs):
    nc = build_L6()
    gidx = inputs["L1_ffn_gate_idx"]
    uidx = inputs["L1_ffn_up_idx"]
    didx = inputs["L1_ffn_down_idx"]
    lut = np.ascontiguousarray(inputs["L1_ffn_lut"].reshape(1, 16))
    in_maps = []
    for c in range(NCORES):
        in_maps.append(dict(
            gidx_in=_padrows(gidx[c * 688:(c + 1) * 688], 768),
            uidx_in=_padrows(uidx[c * 688:(c + 1) * 688], 768),
            didx_in=np.ascontiguousarray(didx[c * 256:(c + 1) * 256]),
            lut_in=lut))
    res = run_bass_kernel_spmd(nc, in_maps, core_ids=list(range(NCORES)))
    gw = np.empty((HGHOST, D), BF)
    uw = np.empty((HGHOST, D), BF)
    dw = np.empty((D, HGHOST), BF)
    for c, r in enumerate(res.results):
        gw[c * 688:(c + 1) * 688] = r["gw_out"][:688]
        uw[c * 688:(c + 1) * 688] = r["uw_out"][:688]
        dw[c * 256:(c + 1) * 256] = r["dw_out"]
    return gw, uw, dw


NQI = 8


def build_L7a():
    nc = bass.Bass("TRN2", target_bir_lowering=False, debug=False)
    xn2s_in = nc.dram_tensor("xn2s_in", [TOK, D], BF16, kind="ExternalInput").ap()
    x2s_in = nc.dram_tensor("x2s_in", [TOK, D], F32, kind="ExternalInput").ap()
    K_in = nc.dram_tensor("K_in", [D, B * L], BF16, kind="ExternalInput").ap()
    V_in = nc.dram_tensor("V_in", [B * L, D], BF16, kind="ExternalInput").ap()
    wqT_in = nc.dram_tensor("wqT_in", [D, D], F32, kind="ExternalInput").ap()
    woT_in = nc.dram_tensor("woT_in", [D, D], F32, kind="ExternalInput").ap()
    mask_in = nc.dram_tensor("mask_in", [TOK, L], BF16, kind="ExternalInput").ap()
    ident_in = nc.dram_tensor("ident_in", [128, 128], BF16, kind="ExternalInput").ap()

    x3s_out = nc.dram_tensor("x3s_out", [TOK, D], F32, kind="ExternalOutput").ap()

    o_scr = nc.dram_tensor("o_scr", [TOK, D], BF16).ap()
    wo_scr = nc.dram_tensor("wo_scr", [D, TOK], BF16).ap()

    KT = D // 128
    MPD = D // 256
    NT = TOK // 128
    NKC = L // 512
    NKK = L // 128
    SCL = 1.0 / float(np.sqrt(HD))

    with contextlib.ExitStack() as ctx:
        sb = lambda *a: ctx.enter_context(nc.sbuf_tensor(*a))
        pt = lambda *a: ctx.enter_context(nc.psum_tensor(*a))
        sem = lambda n: ctx.enter_context(nc.semaphore(n))
        xfm = sb("xfm", [128, KT, TOK], BF16)
        wA = sb("wA", [128, KT, 256], BF16)
        wB = sb("wB", [128, KT, 256], BF16)
        Qfm = sb("Qfm", [128, KT, TOK], BF16)
        Kslab = sb("Kslab", [128, L], BF16)
        Vslab = sb("Vslab", [128, NKK, 128], BF16)
        maskt = sb("maskt", [128, 4, L], BF16)
        Ssb = sb("Ssb", [128, L], F32)
        Psb = sb("Psb", [128, L], BF16)
        PTt = sb("PTt", [128, 2, 128], BF16)
        ident = sb("ident", [128, 128], BF16)
        negmax = sb("negmax", [128, 1], F32)
        sbar = sb("sbar", [1, 2], F32)
        rowsum = sb("rowsum", [128, 1], F32)
        rl = sb("rl", [128, 1], F32)
        otile = sb("otile", [128, 4, 2, 128], BF16)
        evg = sb("evg", [128, 2, 4, 512], BF16)
        xt = sb("xt", [128, D], F32)
        wob = sb("wob", [128, D], BF16)
        wo32 = sb("wo32", [128, D], F32)
        x3t = sb("x3t", [128, 2, D], F32)
        bar = sb("bar", [128, 1], F32)
        psS = pt("psS", [128, 2, 512], F32)
        psT = pt("psT", [128, 2, 128], BF16)
        psO = pt("psO", [128, 128], F32)
        psW = pt("psW", [128, 4, 512], F32)
        s_c = sem("s_c")
        s_tf = sem("s_tf")
        s_wq = sem("s_wq")
        s_mq = sem("s_mq")
        s_qe = sem("s_qe")
        s_kv = sem("s_kv")
        s_mk = sem("s_mk")
        s_ms = sem("s_ms")
        s_se = sem("s_se")
        s_ex = sem("s_ex")
        s_rv = sem("s_rv")
        s_pt = sem("s_pt")
        s_pw = sem("s_pw")
        s_pe = sem("s_pe")
        s_pv = sem("s_pv")
        s_ao = sem("s_ao")
        s_oe = sem("s_oe")
        s_os = sem("s_os")
        s_oq = sem("s_oq")
        s_tf2 = sem("s_tf2")
        s_ww = sem("s_ww")
        s_mw = sem("s_mw")
        s_we = sem("s_we")
        s_ws = sem("s_ws")
        s_x3l = sem("s_x3l")
        s_x3v = sem("s_x3v")
        s_x3s = sem("s_x3s")
        s_x3q = sem("s_x3q")
        block = ctx.enter_context(nc.Block())

        def qi_of(i):
            return i

        @block.sync
        def _(sync):
            sync.dma_start(out=ident[:], in_=ident_in[:]).then_inc(s_c, 16)
            for k in range(KT):
                sync.dma_start(out=xfm[:, k, :], in_=xn2s_in[:, k * 128:(k + 1) * 128],
                               transpose=True).then_inc(s_tf, 16)
            for b in range(B):
                if b > 0:
                    sync.wait_ge(s_oq, b * (HEADS // 2))
                sync.dma_start(out=maskt[:],
                               in_=mask_in.rearrange("(b q p) l -> p (b q) l", b=B, p=128)[
                                   :, b * 4:(b + 1) * 4, :]).then_inc(s_mk, 16)
                for h in range(HEADS):
                    ih = b * HEADS + h
                    if ih >= 1:
                        sync.wait_ge(s_ao, ih * NQI // 2)
                    sync.dma_start(out=Kslab[:],
                                   in_=K_in[h * 128:(h + 1) * 128, b * L:(b + 1) * L]
                                   ).then_inc(s_kv, 16)
                    sync.dma_start(out=Vslab[:],
                                   in_=V_in[b * L:(b + 1) * L, h * 128:(h + 1) * 128]
                                   .rearrange("(kk p) d -> p kk d", p=128)).then_inc(s_kv, 16)
            sync.wait_ge(s_oq, B * (HEADS // 2))
            for k in range(KT):
                sync.dma_start(out=xfm[:, k, :], in_=o_scr[:, k * 128:(k + 1) * 128],
                               transpose=True).then_inc(s_tf2, 16)
            for mp in range(MPD):
                pb = mp % 2
                for i in range(4):
                    mm, j = divmod(i, 2)
                    m = mp * 2 + mm
                    sync.wait_ge(s_we, 4 * mp + i + 1)
                    sync.dma_start(out=wo_scr[m * 128:(m + 1) * 128, j * 512:(j + 1) * 512],
                                   in_=evg[:, pb, i, :]).then_inc(s_ws, 16)
                sync.wait_ge(s_ws, 64 * (mp + 1) + mp)
                sync.sem_inc(s_ws, 1)
            for n in range(NT):
                if n >= 1:
                    sync.wait_ge(s_x3v, 2 * n)
                sync.dma_start(out=xt[:], in_=x2s_in[n * 128:(n + 1) * 128, :]).then_inc(s_x3l, 16)
                sync.dma_start(out=wob[:], in_=wo_scr[:, n * 128:(n + 1) * 128],
                               transpose=True).then_inc(s_x3l, 16)
                sync.wait_ge(s_x3v, 2 * n + 2)
                sync.dma_start(out=x3s_out[n * 128:(n + 1) * 128, :],
                               in_=x3t[:, n % 2, :]).then_inc(s_x3s, 16)
                sync.wait_ge(s_x3s, 16 * (n + 1))
                sync.sem_inc(s_x3q, 1)

        @block.gpsimd
        def _(gpsimd):
            for mp in range(MPD):
                if mp >= 1:
                    gpsimd.wait_ge(s_mq, 4 * mp)
                for k in range(KT):
                    gpsimd.dma_start(out=wA[:, k, :],
                                     in_=wqT_in[k * 128:(k + 1) * 128, mp * 256:(mp + 1) * 256]
                                     ).then_inc(s_wq, 16)
            for b in range(B):
                for h in range(HEADS):
                    ih = b * HEADS + h
                    if h % 2 == 1:
                        for qi in range(4):
                            iq = (ih // 2) * 4 + qi
                            gpsimd.wait_ge(s_oe, ih * 4 + qi + 1)
                            gq = b * 4 + qi
                            gpsimd.dma_start(
                                out=o_scr[gq * 128:(gq + 1) * 128, (h - 1) * 128:(h + 1) * 128],
                                in_=otile[:, qi, :, :]).then_inc(s_os, 16)
                        gpsimd.wait_ge(s_os, 16 * ((ih // 2) * 4 + 4))
                        gpsimd.sem_inc(s_oq, 1)
            for mp in range(MPD):
                if mp >= 1:
                    gpsimd.wait_ge(s_mw, 4 * mp)
                for k in range(KT):
                    gpsimd.dma_start(out=wB[:, k, :],
                                     in_=woT_in[k * 128:(k + 1) * 128, mp * 256:(mp + 1) * 256]
                                     ).then_inc(s_ww, 16)

        @block.tensor
        def _(tensor):
            tensor.wait_ge(s_c, 16)
            tensor.wait_ge(s_tf, 16 * KT)
            for mp in range(MPD):
                tensor.wait_ge(s_wq, 16 * KT * (mp + 1))
                if mp >= 1:
                    tensor.wait_ge(s_qe, 4 * mp)
                for i in range(4):
                    mm, j = divmod(i, 2)
                    for k in range(KT):
                        mq = tensor.matmul(psW[:, i, :], wA[:, k, mm * 128:(mm + 1) * 128],
                                           xfm[:, k, j * 512:(j + 1) * 512],
                                           start=(k == 0), stop=(k == KT - 1))
                    mq.then_inc(s_mq, 1)
            tensor.wait_ge(s_qe, 4 * MPD)
            for b in range(B):
                for h in range(HEADS):
                    ih = b * HEADS + h
                    tensor.wait_ge(s_kv, 32 * (ih + 1))
                    for qi in range(4):
                        gq = b * 4 + qi
                        iq = ih * 4 + qi
                        for kb in range(NKC):
                            if iq * NKC + kb >= 2:
                                tensor.wait_ge(s_se, iq * NKC + kb - 1)
                            ms = tensor.matmul(psS[:, kb % 2, :],
                                               Qfm[:, h, gq * 128:(gq + 1) * 128],
                                               Kslab[:, kb * 512:(kb + 1) * 512],
                                               start=True, stop=True)
                            ms.then_inc(s_ms, 1)
                        tensor.wait_ge(s_ex, iq + 1)
                        for kk in range(NKK):
                            sem_e = [s_pe, s_pv][kk % 2]
                            cnt = iq * (NKK // 2) + kk // 2
                            if cnt >= 1:
                                tensor.wait_ge(sem_e, cnt)
                            tensor.transpose(psT[:, kk % 2, :], Psb[:, kk * 128:(kk + 1) * 128],
                                             ident[:]).then_inc([s_pt, s_pw][kk % 2], 1)
                            tensor.wait_ge(sem_e, cnt + 1)
                            mav = tensor.matmul(psO[:], PTt[:, kk % 2, :], Vslab[:, kk, :],
                                                start=(kk == 0), stop=(kk == NKK - 1))
                        mav.then_inc(s_ao, 1)
                        if True:
                            tensor.wait_ge(s_oe, iq + 1)
            tensor.wait_ge(s_tf2, 16 * KT)
            for mp in range(MPD):
                tensor.wait_ge(s_ww, 16 * KT * (mp + 1))
                if mp >= 1:
                    tensor.wait_ge(s_we, 4 * mp)
                for i in range(4):
                    mm, j = divmod(i, 2)
                    for k in range(KT):
                        mw = tensor.matmul(psW[:, i, :], wB[:, k, mm * 128:(mm + 1) * 128],
                                           xfm[:, k, j * 512:(j + 1) * 512],
                                           start=(k == 0), stop=(k == KT - 1))
                    mw.then_inc(s_mw, 1)

        @block.vector
        def _(vector):
            for mp in range(MPD):
                for i in range(4):
                    mm, j = divmod(i, 2)
                    m = mp * 2 + mm
                    vector.wait_ge(s_mq, 4 * mp + i + 1)
                    vector.tensor_scalar(out=Qfm[:, m, j * 512:(j + 1) * 512],
                                         in0=psW[:, i, :], scalar1=SCL, scalar2=None,
                                         op0=ALU.mult, op1=ALU.bypass)
                    vector.drain()
                    vector.memset(bar[:, 0:1], 0.0).then_inc(s_qe, 1)
            for b in range(B):
                vector.wait_ge(s_mk, 16 * (b + 1))
                for h in range(HEADS):
                    ih = b * HEADS + h
                    for qi in range(4):
                        iq = ih * 4 + qi
                        for kb in range(NKC):
                            vector.wait_ge(s_ms, iq * NKC + kb + 1)
                            vector.tensor_tensor(out=Ssb[:, kb * 512:(kb + 1) * 512],
                                                 in0=psS[:, kb % 2, :],
                                                 in1=maskt[:, qi, kb * 512:(kb + 1) * 512],
                                                 op=ALU.add)
                            vector.drain()
                            vector.memset(bar[:, 0:1], 0.0).then_inc(s_se, 1)
                        vector.reduce_max(out=negmax[:], in_=Ssb[:], axis=AX.X, negate=True)
                        vector.drain()
                        vector.memset(bar[:, 0:1], 0.0).then_inc(s_rv, 2)
                        vector.wait_ge(s_ex, iq + 1)
                        vector.reciprocal(out=rl[:], in_=rowsum[:])
                        vector.drain()
                        vector.memset(bar[:, 0:1], 0.0).then_inc(s_rv, 1)
                        for kk in range(1, NKK, 2):
                            vector.wait_ge(s_pw, iq * (NKK // 2) + kk // 2 + 1)
                            vector.tensor_copy(out=PTt[:, 1, :], in_=psT[:, 1, :])
                            vector.drain()
                            vector.memset(bar[:, 0:1], 0.0).then_inc(s_pv, 1)
                        vector.wait_ge(s_ao, iq + 1)
                        if ih >= 2:
                            vector.wait_ge(s_oq, ih // 2)
                        vector.tensor_scalar(out=otile[:, qi, h % 2, :], in0=psO[:],
                                             scalar1=rl[:], scalar2=None,
                                             op0=ALU.mult, op1=ALU.bypass)
                        vector.drain()
                        vector.memset(bar[:, 0:1], 0.0).then_inc(s_oe, 1)
            for mp in range(MPD):
                pb = mp % 2
                for i in range(4):
                    vector.wait_ge(s_mw, 4 * mp + i + 1)
                    if mp >= 2:
                        vector.wait_ge(s_ws, 64 * (mp - 1) + (mp - 1))
                    vector.tensor_copy(out=evg[:, pb, i, :], in_=psW[:, i, :])
                    vector.drain()
                    vector.memset(bar[:, 0:1], 0.0).then_inc(s_we, 1)
            for n in range(NT):
                vector.wait_ge(s_x3l, 32 * (n + 1))
                if n >= 2:
                    vector.wait_ge(s_x3q, n - 1)
                vector.tensor_copy(out=wo32[:], in_=wob[:])
                vector.drain()
                vector.memset(bar[:, 0:1], 0.0).then_inc(s_x3v, 1)
                vector.tensor_tensor(out=x3t[:, n % 2, :], in0=xt[:], in1=wo32[:],
                                     op=ALU.add)
                vector.drain()
                vector.memset(bar[:, 0:1], 0.0).then_inc(s_x3v, 1)

        @block.scalar
        def _(scalar):
            scalar.wait_ge(s_c, 16)
            for b in range(B):
                for h in range(HEADS):
                    ih = b * HEADS + h
                    for qi in range(4):
                        iq = ih * 4 + qi
                        scalar.wait_ge(s_rv, 3 * iq + 2)
                        scalar.activation(out=Psb[:], in_=Ssb[:], func=AF.Exp,
                                          bias=negmax[:], accum_out=rowsum[:])
                        scalar.drain()
                        scalar.copy(out=sbar[0:1, 0:1], in_=ident[0:1, 0:1]).then_inc(s_ex, 1)
                        for kk in range(0, NKK, 2):
                            scalar.wait_ge(s_pt, iq * (NKK // 2) + kk // 2 + 1)
                            scalar.copy(out=PTt[:, 0, :], in_=psT[:, 0, :])
                            scalar.drain()
                            scalar.copy(out=sbar[0:1, 0:1], in_=ident[0:1, 0:1]).then_inc(s_pe, 1)

    return nc


def build_L7b():
    nc = bass.Bass("TRN2", target_bir_lowering=False, debug=False)
    x3_in = nc.dram_tensor("x3_in", [TOK, D], F32, kind="ExternalInput").ap()
    gwT_in = nc.dram_tensor("gwT_in", [D, HB2], BF16, kind="ExternalInput").ap()
    uwT_in = nc.dram_tensor("uwT_in", [D, HB2], BF16, kind="ExternalInput").ap()
    dwT_in = nc.dram_tensor("dwT_in", [HB2, D], BF16, kind="ExternalInput").ap()
    cst_in = nc.dram_tensor("cst_in", [1, 2], F32, kind="ExternalInput").ap()

    x4_out = nc.dram_tensor("x4_out", [TOK, D], F32, kind="ExternalOutput").ap()

    xn3_scr = nc.dram_tensor("xn3_scr", [TOK, D], BF16).ap()
    h_scr = nc.dram_tensor("h_scr", [HB2, TOK], BF16).ap()
    dn_scr = nc.dram_tensor("dn_scr", [D, TOK], BF16).ap()

    KT = D // 128
    MPH = HB2 // 256
    MPD = D // 256
    KH = HB2 // 128
    NT = TOK // 128

    with contextlib.ExitStack() as ctx:
        sb = lambda *a: ctx.enter_context(nc.sbuf_tensor(*a))
        pt = lambda *a: ctx.enter_context(nc.psum_tensor(*a))
        sem = lambda n: ctx.enter_context(nc.semaphore(n))
        xt = sb("xt", [128, D], F32)
        dnb = sb("dnb", [128, D], BF16)
        dn32 = sb("dn32", [128, D], F32)
        x4t = sb("x4t", [128, 2, D], F32)
        sq = sb("sq", [128, D], F32)
        sq2 = sb("sq2", [128, D], F32)
        stat = sb("stat", [128, 4], F32)
        xnt = sb("xnt", [128, 2, D], BF16)
        cst = sb("cst", [128, 2], F32)
        bar = sb("bar", [128, 1], F32)
        sbar = sb("sbar", [1, 2], F32)
        xfm = sb("xfm", [128, KT, TOK], BF16)
        wA = sb("wA", [128, KT, 256], BF16)
        wB = sb("wB", [128, KT, 256], BF16)
        wC = sb("wC", [128, KH, 256], BF16)
        hst = sb("hst", [128, 2, 4, 512], BF16)
        hbuf = sb("hbuf", [128, 2, TOK], BF16)
        evg = sb("evg", [128, 2, 4, 512], BF16)
        sgt = sb("sgt", [128, 512], F32)
        ps = pt("ps", [128, 8, 512], F32)
        s_c = sem("s_c")
        s_lx = sem("s_lx")
        s_xs1 = sem("s_xs1")
        s_xq1 = sem("s_xq1")
        s_ln = sem("s_ln")
        s_lns = sem("s_lns")
        s_tf1 = sem("s_tf1")
        s_w = sem("s_w")
        s_mm = sem("s_mm")
        s_sg = sem("s_sg")
        s_hv = sem("s_hv")
        s_hs = sem("s_hs")
        s_hq = sem("s_hq")
        s_hl0 = sem("s_hl0")
        s_hl1 = sem("s_hl1")
        s_mdk = sem("s_mdk")
        s_dv = sem("s_dv")
        s_ds = sem("s_ds")
        s_x2 = sem("s_x2")
        s_xv = sem("s_xv")
        s_x2q = sem("s_x2q")
        s_x2s = sem("s_x2s")
        block = ctx.enter_context(nc.Block())

        def ln_vector(vector, in_ap, nn):
            vector.reduce_sum(out=stat[:, 0:1], in_=in_ap, axis=AX.X)
            vector.drain()
            vector.tensor_scalar(out=stat[:, 1:2], in0=stat[:, 0:1], scalar1=-1.0 / D,
                                 scalar2=None, op0=ALU.mult, op1=ALU.bypass)
            vector.drain()
            vector.tensor_scalar(out=sq[:], in0=in_ap, scalar1=stat[:, 1:2],
                                 scalar2=None, op0=ALU.add, op1=ALU.bypass)
            vector.drain()
            vector.memset(bar[:, 0:1], 0.0).then_inc(s_ln, 1)
            vector.wait_ge(s_lns, nn + 1)
            vector.reciprocal(out=stat[:, 3:4], in_=stat[:, 2:3])
            vector.drain()
            vector.tensor_scalar(out=xnt[:, nn % 2, :], in0=sq[:], scalar1=stat[:, 3:4],
                                 scalar2=None, op0=ALU.mult, op1=ALU.bypass)
            vector.drain()
            vector.memset(bar[:, 0:1], 0.0).then_inc(s_ln, 1)

        def ln_scalar(scalar, nn):
            scalar.wait_ge(s_ln, 2 * nn + 1)
            scalar.activation(out=sq2[:], in_=sq[:], func=AF.Square, accum_out=stat[:, 0:1])
            scalar.drain()
            scalar.activation(out=stat[:, 2:3], in_=stat[:, 0:1], func=AF.Sqrt,
                              scale=1.0 / D, bias=cst[:, 0:1])
            scalar.drain()
            scalar.copy(out=bar[0:1, 0:1], in_=cst[0:1, 0:1]).then_inc(s_lns, 1)

        @block.sync
        def _(sync):
            sync.dma_start(out=cst[:], in_=cst_in.partition_broadcast(128)).then_inc(s_c, 16)
            for n in range(NT):
                if n >= 1:
                    sync.wait_ge(s_ln, 2 * n)
                sync.dma_start(out=xt[:], in_=x3_in[n * 128:(n + 1) * 128, :]).then_inc(s_lx, 16)
                sync.wait_ge(s_ln, 2 * n + 2)
                sync.dma_start(out=xn3_scr[n * 128:(n + 1) * 128, :],
                               in_=xnt[:, n % 2, :]).then_inc(s_xs1, 16)
                sync.wait_ge(s_xs1, 16 * (n + 1))
                sync.sem_inc(s_xq1, 1)
            for k in range(KT):
                sync.dma_start(out=xfm[:, k, :], in_=xn3_scr[:, k * 128:(k + 1) * 128],
                               transpose=True).then_inc(s_tf1, 16)
            for mp in range(MPH):
                pb = mp % 2
                for i in range(4):
                    mm, j = divmod(i, 2)
                    m = mp * 2 + mm
                    sync.wait_ge(s_hv, 4 * mp + i + 1)
                    sync.dma_start(out=h_scr[m * 128:(m + 1) * 128, j * 512:(j + 1) * 512],
                                   in_=hst[:, pb, i, :]).then_inc(s_hs, 16)
                sync.wait_ge(s_hs, 64 * (mp + 1))
                sync.sem_inc(s_hq, 1)
            for mp in range(MPD):
                pb = mp % 2
                for i in range(4):
                    mm, j = divmod(i, 2)
                    m = mp * 2 + mm
                    sync.wait_ge(s_dv, 4 * mp + i + 1)
                    sync.dma_start(out=dn_scr[m * 128:(m + 1) * 128, j * 512:(j + 1) * 512],
                                   in_=evg[:, pb, i, :]).then_inc(s_ds, 16)
                sync.wait_ge(s_ds, 64 * (mp + 1) + mp)
                sync.sem_inc(s_ds, 1)
            sync.wait_ge(s_ds, 64 * MPD + MPD)
            for n in range(NT):
                if n >= 1:
                    sync.wait_ge(s_xv, 2 * n)
                sync.dma_start(out=xt[:], in_=x3_in[n * 128:(n + 1) * 128, :]).then_inc(s_x2, 16)
                sync.dma_start(out=dnb[:], in_=dn_scr[:, n * 128:(n + 1) * 128],
                               transpose=True).then_inc(s_x2, 16)
                sync.wait_ge(s_xv, 2 * n + 2)
                sync.dma_start(out=x4_out[n * 128:(n + 1) * 128, :],
                               in_=x4t[:, n % 2, :]).then_inc(s_x2s, 16)
                sync.wait_ge(s_x2s, 16 * (n + 1))
                sync.sem_inc(s_x2q, 1)

        @block.gpsimd
        def _(gpsimd):
            for mp in range(MPH):
                if mp >= 1:
                    gpsimd.wait_ge(s_mm, 8 * mp)
                for k in range(KT):
                    gpsimd.dma_start(out=wA[:, k, :],
                                     in_=gwT_in[k * 128:(k + 1) * 128, mp * 256:(mp + 1) * 256]
                                     ).then_inc(s_w, 16)
                    gpsimd.dma_start(out=wB[:, k, :],
                                     in_=uwT_in[k * 128:(k + 1) * 128, mp * 256:(mp + 1) * 256]
                                     ).then_inc(s_w, 16)
            gpsimd.wait_ge(s_hq, MPH)
            for mp in range(MPD):
                if mp >= 1:
                    gpsimd.wait_ge(s_dv, 4 * mp)
                for k in range(KH):
                    gpsimd.dma_start(out=wC[:, k, :],
                                     in_=dwT_in[k * 128:(k + 1) * 128, mp * 256:(mp + 1) * 256]
                                     ).then_inc(s_w, 16)
                for k in range(KH):
                    if mp * KH + k >= 2:
                        gpsimd.wait_ge(s_mdk, mp * KH + k - 1)
                    gpsimd.dma_start(out=hbuf[:, k % 2, :],
                                     in_=h_scr[k * 128:(k + 1) * 128, :]
                                     ).then_inc([s_hl0, s_hl1][k % 2], 16)

        @block.tensor
        def _(tensor):
            tensor.wait_ge(s_tf1, 16 * KT)
            for mp in range(MPH):
                tensor.wait_ge(s_w, 16 * 32 * (mp + 1))
                if mp >= 1:
                    tensor.wait_ge(s_hv, 4 * mp)
                for i in range(4):
                    mm, j = divmod(i, 2)
                    for k in range(KT):
                        mg = tensor.matmul(ps[:, i, :], wA[:, k, mm * 128:(mm + 1) * 128],
                                           xfm[:, k, j * 512:(j + 1) * 512],
                                           start=(k == 0), stop=(k == KT - 1))
                    mg.then_inc(s_mm, 1)
                    for k in range(KT):
                        mu = tensor.matmul(ps[:, 4 + i, :], wB[:, k, mm * 128:(mm + 1) * 128],
                                           xfm[:, k, j * 512:(j + 1) * 512],
                                           start=(k == 0), stop=(k == KT - 1))
                    mu.then_inc(s_mm, 1)
            tensor.wait_ge(s_sg, 4 * MPH)
            tensor.wait_ge(s_hv, 4 * MPH)
            W0 = 16 * 32 * MPH
            for mp in range(MPD):
                tensor.wait_ge(s_w, W0 + 16 * KH * (mp + 1))
                if mp >= 1:
                    tensor.wait_ge(s_dv, 4 * mp)
                for k in range(KH):
                    tensor.wait_ge([s_hl0, s_hl1][k % 2], 16 * (mp * 22 + k // 2 + 1))
                    for i in range(4):
                        mm, j = divmod(i, 2)
                        md = tensor.matmul(ps[:, i, :], wC[:, k, mm * 128:(mm + 1) * 128],
                                           hbuf[:, k % 2, j * 512:(j + 1) * 512],
                                           start=(k == 0), stop=(k == KH - 1))
                    md.then_inc(s_mdk, 1)

        @block.vector
        def _(vector):
            vector.wait_ge(s_c, 16)
            for n in range(NT):
                vector.wait_ge(s_lx, 16 * (n + 1))
                if n >= 2:
                    vector.wait_ge(s_xq1, n - 1)
                ln_vector(vector, xt[:], n)
            for mp in range(MPH):
                pb = mp % 2
                for i in range(4):
                    vector.wait_ge(s_sg, 4 * mp + i + 1)
                    if mp >= 2:
                        vector.wait_ge(s_hq, mp - 1)
                    vector.tensor_tensor(out=hst[:, pb, i, :],
                                         in0=sgt[:], in1=ps[:, 4 + i, :], op=ALU.mult)
                    vector.drain()
                    vector.memset(bar[:, 0:1], 0.0).then_inc(s_hv, 1)
            for mp in range(MPD):
                pb = mp % 2
                for i in range(4):
                    vector.wait_ge(s_mdk, (mp + 1) * KH)
                    if mp >= 2:
                        vector.wait_ge(s_ds, 64 * (mp - 1) + (mp - 1))
                    vector.tensor_copy(out=evg[:, pb, i, :], in_=ps[:, i, :])
                    vector.drain()
                    vector.memset(bar[:, 0:1], 0.0).then_inc(s_dv, 1)
            for n in range(NT):
                vector.wait_ge(s_x2, 32 * n + 32)
                if n >= 2:
                    vector.wait_ge(s_x2q, n - 1)
                vector.tensor_copy(out=dn32[:], in_=dnb[:])
                vector.drain()
                vector.memset(bar[:, 0:1], 0.0).then_inc(s_xv, 1)
                vector.tensor_tensor(out=x4t[:, n % 2, :], in0=xt[:], in1=dn32[:],
                                     op=ALU.add)
                vector.drain()
                vector.memset(bar[:, 0:1], 0.0).then_inc(s_xv, 1)

        @block.scalar
        def _(scalar):
            scalar.wait_ge(s_c, 16)
            for n in range(NT):
                ln_scalar(scalar, n)
            for mp in range(MPH):
                for i in range(4):
                    scalar.wait_ge(s_mm, 8 * mp + 2 * i + 2)
                    if 4 * mp + i >= 1:
                        scalar.wait_ge(s_hv, 4 * mp + i)
                    scalar.activation(out=sgt[:], in_=ps[:, i, :], func=AF.Silu)
                    scalar.drain()
                    scalar.copy(out=sbar[0:1, 0:1], in_=cst[0:1, 0:1]).then_inc(s_sg, 1)

    return nc


def _padrows(a, rows):
    out = np.zeros((rows,) + a.shape[1:], a.dtype)
    out[:a.shape[0]] = a
    return out


STRIPE_BLOCKS = {c: [c, 15 - c, 16 + c, 31 - c] for c in range(NCORES)}


def _stripe_rows(c):
    rows = []
    for b in range(B):
        for blk in STRIPE_BLOCKS[c]:
            for p in range(128):
                rows.append(b * L + blk * 128 + p)
    return np.array(rows)


def run_L7a(inputs, x2, xn2, Kf, Vf):
    nc = build_L7a()
    K_full = np.ascontiguousarray(Kf.reshape(D, B * L))
    V_full = np.ascontiguousarray(Kf.reshape(D, B * L).T) if False else \
        np.ascontiguousarray(Vf.reshape(D, B * L).T)
    wqT = np.ascontiguousarray(inputs["L1_attn_w_q"].T)
    woT = np.ascontiguousarray(inputs["L1_attn_w_o"].T)
    ident = np.eye(128, dtype=BF)
    cols = np.arange(L)
    x2f = x2.reshape(B * L, D)
    xn2f = xn2.reshape(B * L, D)
    in_maps = []
    for c in range(NCORES):
        rows = _stripe_rows(c)
        qpos = np.concatenate([np.arange(blk * 128, (blk + 1) * 128)
                               for _ in range(B) for blk in [0]] * 0) if False else None
        qp = []
        for b in range(B):
            for blk in STRIPE_BLOCKS[c]:
                qp.append(np.arange(blk * 128, (blk + 1) * 128))
        qp = np.concatenate(qp)
        mask = np.where(cols[None, :] > qp[:, None], np.float32(-30000.0),
                        np.float32(0.0)).astype(BF)
        in_maps.append(dict(
            xn2s_in=np.ascontiguousarray(xn2f[rows]),
            x2s_in=np.ascontiguousarray(x2f[rows]),
            K_in=K_full, V_in=V_full, wqT_in=wqT, woT_in=woT,
            mask_in=np.ascontiguousarray(mask), ident_in=ident))
    res = run_bass_kernel_spmd(nc, in_maps, core_ids=list(range(NCORES)))
    return [r["x3s_out"] for r in res.results]


def run_L7b(inputs, x3s_list, gw, uw, dw):
    nc = build_L7b()
    gwT = np.zeros((D, HB2), BF)
    gwT[:, :HGHOST] = gw.T
    uwT = np.zeros((D, HB2), BF)
    uwT[:, :HGHOST] = uw.T
    dwT = np.zeros((HB2, D), BF)
    dwT[:HGHOST, :] = dw.T
    cstv = np.array([[EPS_LN, 1.0]], np.float32)
    in_maps = []
    for c in range(NCORES):
        in_maps.append(dict(x3_in=np.ascontiguousarray(x3s_list[c]),
                            gwT_in=gwT, uwT_in=uwT, dwT_in=dwT, cst_in=cstv))
    res = run_bass_kernel_spmd(nc, in_maps, core_ids=list(range(NCORES)))
    x4 = np.empty((B * L, D), np.float32)
    for c, r in enumerate(res.results):
        x4[_stripe_rows(c)] = r["x4_out"]
    return x4.reshape(B, L, D)


def kernel(**inputs):
    inputs = {k: np.asarray(v) for k, v in inputs.items()}
    ln0, xz = run_L1(inputs)
    xc, xdbl_parts, gparts = run_L2(inputs, xz)
    y = run_L3(inputs, xz, xc, xdbl_parts)
    x1, qw1, qw2, qw3 = run_L4(inputs, y, ln0, gparts)
    x2, xn2, Kf, Vf = run_L5(inputs, x1, qw1, qw2, qw3)
    gw, uw, dw = run_L6(inputs)
    x3s_list = run_L7a(inputs, x2, xn2, Kf, Vf)
    x4 = run_L7b(inputs, x3s_list, gw, uw, dw)
    return x4
